# version 28
# baseline (speedup 1.0000x reference)
import numpy as np
import ml_dtypes
import concourse.bass as bass
import concourse.mybir as mybir
from concourse.bass_utils import run_bass_kernel_spmd
from contextlib import ExitStack

F32 = mybir.dt.float32
BF16 = mybir.dt.bfloat16
AF = mybir.ActivationFunctionType
ALU = mybir.AluOpType
AX = mybir.AxisListType

T = 4096
E = 1024
NT = T // 128
NE = E // 128
HID = 4096
NH = HID // 128
QL, KVL = 64, 32
NHEAD, KVH = 16, 4
NEXP = 8
LN_EPS = 1e-5
NCORE = 8

ENGINES = ("sync", "gpsimd", "tensor", "vector", "scalar")


class SemCtr:
    __slots__ = ("h", "n")

    def __init__(self, h):
        self.h = h
        self.n = 0

    def tok(self):
        return (self.h, self.n)


class Prog:
    def __init__(self, nc, ctx):
        self.nc = nc
        self.ctx = ctx
        self.streams = {e: [] for e in ENGINES}
        self.sems = {}
        self.lastwait = {e: {} for e in ENGINES}

    def sem(self, name):
        if name not in self.sems:
            h = self.ctx.enter_context(self.nc.semaphore(name))
            self.sems[name] = SemCtr(h)
        return self.sems[name]

    def op(self, eng, build, waits=(), inc=None, inc_k=1):
        real = []
        lw = self.lastwait[eng]
        for wt in waits:
            if wt is None:
                continue
            h, v = wt
            if v <= 0 or lw.get(h, -1) >= v:
                continue
            lw[h] = v
            real.append((h, v))
        if inc is not None:
            inc.n += inc_k

        def fn(q, build=build, rw=tuple(real), inc=inc, k=inc_k):
            for (h, v) in rw:
                q.wait_ge(h, v)
            ins = build(q)
            if inc is not None:
                ins.then_inc(inc.h, k)
        self.streams[eng].append(fn)
        return inc.tok() if inc is not None else None

    def fence_sem(self, eng):
        return self.sem("fence_" + eng)

    def sf(self, q, eng, ins):
        s = self.fence_sem(eng)
        s.n += 1
        ins.then_inc(s.h, 1)
        q.wait_ge(s.h, s.n)
        return ins

    def barrier_wait(self, eng, sem):
        def fn(q, sem=sem):
            q.wait_ge(sem.h, sem.n)
        self.streams[eng].append(fn)

    def flush(self, name=None):
        for snm, sem in self.sems.items():
            if sem.n > 0 and (snm.startswith("dm") or snm.startswith("gm")):
                eng = "gpsimd" if snm.startswith("gm") else "sync"
                self.barrier_wait(eng, sem)
        with self.nc.Block(name) as blk:
            for e in ENGINES:
                fns = self.streams[e]
                if fns:
                    getattr(blk, e)(lambda q, fns=tuple(fns): [f(q) for f in fns])
        self.streams = {e: [] for e in ENGINES}
        self.lastwait = {e: {} for e in ENGINES}


def ap3(ap, inner):
    return ap.rearrange("p (h d) -> p h d", d=inner)



def _host_prep(inputs):
    g = {}
    f32 = np.float32
    bf16 = ml_dtypes.bfloat16
    A = lambda a: np.ascontiguousarray(np.asarray(a), f32)

    pair_order = [0, 4, 1, 5, 2, 6, 3, 7, 8, 12, 9, 13, 10, 14, 11, 15]
    perm_q = []
    for s in range(16):
        base = pair_order[s] * 64
        perm_q += [base + 2 * j for j in range(32)]
        perm_q += [base + 2 * j + 1 for j in range(32)]
    perm_q = np.asarray(perm_q)
    wout_rows = np.concatenate([np.arange(64) + 64 * pair_order[s] for s in range(16)])

    g["wqa"] = A(inputs["L0_attn_Wqa"])
    g["wqb"] = A(np.asarray(inputs["L0_attn_Wqb"])[:, perm_q])
    g["wkva"] = A(inputs["L0_attn_Wkva"])
    wkvb = np.asarray(inputs["L0_attn_Wkvb"])
    cols_k, cols_v = [], []
    for kv in range(KVH):
        base = kv * 128
        cols_k += [base + 4 * j for j in range(32)]
        cols_k += [base + 4 * j + 2 for j in range(32)]
        cols_v += [base + 2 * d + 1 for d in range(64)]
    g["wkvb"] = A(wkvb[:, np.asarray(cols_k + cols_v)])
    g["wout"] = A(np.asarray(inputs["L0_attn_Wout"])[wout_rows, :])
    g["bout"] = A(np.asarray(inputs["L0_attn_bout"]).reshape(1, E))

    g["w1t"] = A(np.asarray(inputs["L0_ffn_w1_w"]).T)
    g["w2t"] = A(np.asarray(inputs["L0_ffn_w2_w"]).T)
    g["h1t"] = A(np.asarray(inputs["L0_ffn_h1_w"]).T)
    g["h2t"] = A(np.asarray(inputs["L0_ffn_h2_w"]).T)
    g["w3t"] = A(np.asarray(inputs["L0_ffn_w3_w"]).T)
    for nm, key in [("s1m", "L0_ffn_w1_s"), ("s2m", "L0_ffn_w2_s"),
                    ("sh1m", "L0_ffn_h1_s"), ("sh2m", "L0_ffn_h2_s")]:
        g[nm] = A(np.asarray(inputs[key]).reshape(-1, 128).T)
    g["s3r"] = A(np.asarray(inputs["L0_ffn_w3_s"]).reshape(1, E))

    g["wrt"] = A(np.asarray(inputs["L1_attn_W_r"]).T)
    g["wkt"] = A(np.asarray(inputs["L1_attn_W_k"]).T)
    g["wvt"] = A(np.asarray(inputs["L1_attn_W_v"]).T)
    g["wot"] = A(np.asarray(inputs["L1_attn_W_o"]).T)
    for nm, key in [("tmr", "L1_attn_time_mix_r"), ("tmk", "L1_attn_time_mix_k"),
                    ("tmv", "L1_attn_time_mix_v"), ("tdm", "L1_attn_time_decay"),
                    ("tfm", "L1_attn_time_first"), ("gmm", "L1_attn_ln_gamma"),
                    ("gbm", "L1_attn_ln_beta")]:
        g[nm] = A(np.asarray(inputs[key]).reshape(-1, 128).T)

    g["gw"] = A(inputs["L1_ffn_gate_w"])
    g["gb"] = A(np.asarray(inputs["L1_ffn_gate_b"]).reshape(1, NEXP))
    g["ewt"] = A(np.asarray(inputs["L1_ffn_expert_w"]).reshape(NEXP * E, E))
    g["eb"] = A(inputs["L1_ffn_expert_b"])

    g["eye128"] = np.eye(128, dtype=bf16)
    g["eye128f"] = np.eye(128, dtype=f32)
    inv_freq = (1.0 / (f32(10000.0) ** (np.arange(0, 64, 2, dtype=f32) / f32(64.0)))).astype(f32)
    freqs = np.outer(np.arange(T, dtype=f32), inv_freq).astype(f32)
    g["cosq"] = np.ascontiguousarray(np.tile(np.cos(freqs).astype(f32), (1, NHEAD)))
    g["sinq"] = np.ascontiguousarray(np.tile(np.sin(freqs).astype(f32), (1, NHEAD)))
    m = np.zeros((128, 4 * 1024), dtype=bf16)
    r = np.arange(128)[:, None]
    qf = np.arange(512)[None, :]
    for j in range(4):
        pat = (qf >= j * 128 + r).astype(bf16)
        m[:, 1024 * j:1024 * j + 512] = pat
        m[:, 1024 * j + 512:1024 * (j + 1)] = pat
    g["mask4"] = m
    osel = np.zeros((128, 256), dtype=bf16)
    for h in range(16):
        osel[:, 16 * h + h] = 1
    g["onesel"] = osel
    sbc = np.zeros((16, 8 * 128), dtype=f32)
    for p in range(8):
        sbc[2 * p, 128 * p:128 * p + 64] = 1
        sbc[2 * p + 1, 128 * p + 64:128 * p + 128] = 1
    g["selbc"] = sbc
    ob = np.zeros((128, 2), dtype=f32)
    ob[0:64, 0] = 1
    ob[64:128, 1] = 1
    g["onesbd2"] = ob
    s2 = np.zeros((2, 128), dtype=f32)
    s2[0, 0:64] = 1
    s2[1, 64:128] = 1
    g["sel2"] = s2
    se = np.zeros((8, 8 * 128), dtype=f32)
    for e in range(8):
        se[e, 128 * e:128 * (e + 1)] = 1
    g["selexp"] = se
    g["miscc"] = np.asarray([[1e-5, 0.0, 0.0, 0.0]], dtype=f32)
    return g



def build(dbg=(), blocks=(0, 1)):
    nc = bass.Bass("TRN2", target_bir_lowering=False, detect_race_conditions=False)
    I = lambda name, shape, dt=F32: nc.dram_tensor(name, shape, dt, kind="ExternalInput").ap()
    D = lambda name, shape, dt=F32: nc.dram_tensor(name, shape, dt).ap()

    t = {}
    t["x_in"] = I("x", [T, E])
    for name, shape in [("wqa", [E, QL]), ("wqb", [QL, E]), ("wkva", [E, KVL]),
                        ("wkvb", [KVL, 512]), ("wout", [E, E]), ("bout", [1, E]),
                        ("w1t", [E, HID]), ("w2t", [E, HID]), ("h1t", [HID, HID]),
                        ("h2t", [HID, HID]), ("w3t", [HID, E]),
                        ("s1m", [128, NH]), ("s2m", [128, NH]), ("sh1m", [128, NH]),
                        ("sh2m", [128, NH]), ("s3r", [1, E]),
                        ("wrt", [E, E]), ("wkt", [E, E]), ("wvt", [E, E]), ("wot", [E, E]),
                        ("tmr", [128, NE]), ("tmk", [128, NE]), ("tmv", [128, NE]),
                        ("tdm", [128, NE]), ("tfm", [128, NE]),
                        ("gmm", [128, NE]), ("gbm", [128, NE]),
                        ("gw", [E, NEXP]), ("gb", [1, NEXP]),
                        ("ewt", [NEXP * E, E]), ("eb", [NEXP, E]),
                        ("cosq", [T, 512]), ("sinq", [T, 512]),
                        ("selbc", [16, 8 * 128]), ("onesbd2", [128, 2]),
                        ("sel2", [2, 128]), ("selexp", [8, 8 * 128]),
                        ("miscc", [1, 4])]:
        t[name] = I(name, shape)
    for name, shape in [("eye128", [128, 128]), ("mask4", [128, 4 * 1024]),
                        ("onesel", [128, 256])]:
        t[name] = I(name, shape, BF16)
    t["eye128f"] = I("eye128f", [128, 128])

    t["out"] = nc.dram_tensor("out", [T, E], F32, kind="ExternalOutput").ap()
    t["w1b"] = D("w1b", [NH, 128 * E], BF16)
    t["w2b"] = D("w2b", [NH, 128 * E], BF16)
    t["h1b"] = D("h1b", [NH, 128 * HID], BF16)
    t["h2b"] = D("h2b", [NH, 128 * HID], BF16)
    t["w3b"] = D("w3b", [HID, E], BF16)
    t["xnt_hbm"] = D("xnt_hbm", [E, T], BF16)
    t["acc_hbm"] = D("acc_hbm", [T, E])
    t["x1_hbm"] = D("x1_hbm", [T, E])
    t["xnt1_hbm"] = D("xnt1_hbm", [E, T], BF16)
    t["rwkv_hbm"] = D("rwkv_hbm", [E, T], BF16)
    t["moe_hbm"] = D("moe_hbm", [E, T])
    t["ewq"] = D("ewq", [2 * NEXP, 128 * NE * 512], BF16)
    t["wn_hbm"] = D("wn_hbm", [NEXP, T], BF16)
    t["sc_hbm"] = D("sc_hbm", [8, 128])

    t["dbg"] = {}
    for name in dbg:
        shp, dt = {"dbg_xnt": ([E, T], BF16), "dbg_acc": ([T, E], F32),
                   "dbg_x1": ([T, E], F32), "dbg_xnt1": ([E, T], BF16),
                   "dbg_kk": ([E, T], F32), "dbg_rr": ([E, T], F32),
                   "dbg_vv": ([E, T], F32),
                   "dbg_wkv": ([E, T], F32), "dbg_rwkv": ([E, T], BF16),
                   "dbg_acc1": ([T, E], F32)}[name]
        t["dbg"][name] = nc.dram_tensor(name, shp, dt, kind="ExternalOutput").ap()

    with ExitStack() as ctx:
        P = Prog(nc, ctx)
        sb = lambda name, shape, dt=F32: ctx.enter_context(nc.sbuf_tensor(name, shape, dt))
        t["ident"] = sb("identsb", [128, 128], BF16)
        t["identf"] = sb("identfsb", [128, 128])
        t["negc"] = sb("negcsb", [128, 1])
        t["epsc"] = sb("epscsb", [128, 4])

        d0 = P.sem("dmP0")
        P.op("sync", lambda q: q.dma_start(t["ident"][:, :], t["eye128"]), inc=d0, inc_k=16)
        P.op("sync", lambda q: q.dma_start(t["identf"][:, :], t["eye128f"]), inc=d0, inc_k=16)
        P.op("sync", lambda q: q.dma_start(
            t["epsc"][:, :], t["miscc"][0:1, :].partition_broadcast(128)), inc=d0, inc_k=16)
        P.flush("p0")

        if 0 in blocks:
            _b0(nc, ctx, P, t)
        if 1 in blocks:
            _b1(nc, ctx, P, t)

        P.flush("fin")
    return nc



def _emit_wprecast(P, t):
    first = [t.get("qkv_gate")]
    for (dst, src, nf, j) in [("w1b", "w1t", NH, NE), ("w2b", "w2t", NH, NE),
                              ("h1b", "h1t", NH, NH), ("h2b", "h2t", NH, NH)]:
        pc = P.sem("pc_" + dst)
        srcv = t[src].rearrange("(j p) n -> p j n", p=128)
        for f in range(nf):
            P.op("gpsimd", lambda q, dst=dst, srcv=srcv, f=f, j=j: q.dma_start(
                t[dst][f:f + 1, :].rearrange("f (p j c) -> (f p) j c", p=128, j=j),
                srcv[:, :, 128 * f:128 * (f + 1)]), waits=first, inc=pc, inc_k=16)
            first = []
        t["pc_" + dst] = pc.tok()
    pc = P.sem("pc_w3b")
    w3v = t["w3t"].rearrange("(a p) n -> p a n", p=128)
    w3bv = t["w3b"].rearrange("(a p) n -> p a n", p=128)
    for a in range(8):
        P.op("gpsimd", lambda q, a=a: q.dma_start(
            w3bv[:, 4 * a:4 * (a + 1), :], w3v[:, 4 * a:4 * (a + 1), :]),
            inc=pc, inc_k=16)
    t["pc_w3b"] = pc.tok()
    pc = P.sem("pc_ewt")
    ewv4 = t["ewt"].rearrange("(e j p) n -> e p j n", p=128, j=NE)
    for e in range(NEXP):
        for rF in range(2):
            P.op("gpsimd", lambda q, e=e, rF=rF: q.dma_start(
                t["ewq"][2 * e + rF:2 * e + rF + 1, :].rearrange(
                    "o (p j c) -> (o p) j c", p=128, j=NE),
                ewv4[e:e + 1, :, :, 512 * rF:512 * (rF + 1)].rearrange(
                    "e p j n -> (e p) j n")), inc=pc, inc_k=16)
    t["pc_ewt"] = pc.tok()



def _emit_ln(nc, pctx_unused, P, t, src_hbm, xnt, dst_hbm, tag, dbgname):
    lctx = ExitStack()
    pctx = lctx
    sb = lambda name, shape, dt=F32: pctx.enter_context(nc.sbuf_tensor(name, shape, dt))
    ps = lambda name, shape, dt=F32: pctx.enter_context(nc.psum_tensor(name, shape, dt))
    xbuf = sb("xbuf" + tag, [128, 3 * E])
    xn_bf = sb("xn_bf" + tag, [128, 2 * E], BF16)
    sq_scr = sb("sq_scr" + tag, [128, E])
    stat = sb("stat" + tag, [128, 24])
    ptr = [ps(f"ptrLN{k}" + tag, [128, 128], BF16) for k in range(4)]

    dv, ac, pe, dv2, ac2 = (P.sem(n) for n in ["dv", "ac", "pe", "dv2", "ac2"])
    lds = [P.sem("dmLNx0" + tag), P.sem("dmLNx1" + tag), P.sem("dmLNx2" + tag)]
    stx = P.sem("dmLNs" + tag)
    apply_tok = [None] * NT
    sq_tok = [None] * NT
    tr_last = [None] * NT
    cp_tok = []
    for i in range(NT):
        b = i % 3
        b2 = i % 2
        buf = xbuf[:, b * E:(b + 1) * E]
        obf = xn_bf[:, b2 * E:(b2 + 1) * E]
        sc = 8 * b
        w = [apply_tok[i - 3], sq_tok[i - 3]] if i >= 3 else []
        tok_x = P.op("sync", lambda q, i=i, buf=buf: q.dma_start(
            buf, src_hbm[128 * i:128 * (i + 1), :]), waits=w, inc=lds[b], inc_k=16)
        if i == 24 and tag == "A":
            t["ln_gate"] = tok_x
        P.op("vector", lambda q, buf=buf, sc=sc: P.sf(q, "vector", q.reduce_sum(
            stat[:, sc:sc + 1], buf, AX.X)) and None, waits=[tok_x])
        sq_tok[i] = P.op("scalar", lambda q, buf=buf, sc=sc: q.activation(
            sq_scr[:, :], buf, AF.Square, accum_out=stat[:, sc + 1:sc + 2]),
            waits=[tok_x], inc=ac)

        def small(q, sc=sc):
            P.sf(q, "vector", q.tensor_scalar_mul(stat[:, sc + 2:sc + 3],
                                                  stat[:, sc:sc + 1], 1.0 / E))
            q.tensor_scalar_mul(stat[:, sc + 3:sc + 4], stat[:, sc + 2:sc + 3], -1.0)
            P.sf(q, "vector", q.tensor_mul(stat[:, sc + 4:sc + 5], stat[:, sc + 2:sc + 3],
                                           stat[:, sc + 2:sc + 3]))
            P.sf(q, "vector", q.tensor_scalar_mul(stat[:, sc + 5:sc + 6],
                                                  stat[:, sc + 1:sc + 2], 1.0 / E))
            return q.tensor_sub(stat[:, sc + 6:sc + 7], stat[:, sc + 5:sc + 6],
                                stat[:, sc + 4:sc + 5])
        tok_var = P.op("vector", small, waits=[sq_tok[i]], inc=dv)

        def rstd(q, sc=sc):
            P.sf(q, "scalar", q.activation(stat[:, sc + 7:sc + 8], stat[:, sc + 6:sc + 7],
                                           AF.Ln, bias=t["epsc"][:, 0:1]))
            return q.activation(stat[:, sc + 7:sc + 8], stat[:, sc + 7:sc + 8], AF.Exp, scale=-0.5)
        tok_rstd = P.op("scalar", rstd, waits=[tok_var], inc=ac)
        wapply = [tok_rstd] + ([tr_last[i - 2]] if i >= 2 else [])
        apply_tok[i] = P.op("vector", lambda q, buf=buf, obf=obf, sc=sc: q.tensor_scalar(
            obf, buf, stat[:, sc + 3:sc + 4], stat[:, sc + 7:sc + 8], ALU.add, ALU.mult),
            waits=wapply, inc=dv)
        for j in range(NE):
            k = (NE * i + j) % 4
            w = [apply_tok[i]] + ([cp_tok[-4]] if len(cp_tok) >= 4 else [])
            tok_tr = P.op("tensor", lambda q, obf=obf, j=j, k=k: q.transpose(
                ptr[k][:, :], obf[:, 128 * j:128 * (j + 1)], t["ident"][:, :]),
                waits=w, inc=pe)
            isv = (j % 2 == 0)

            def cpf(q, i=i, j=j, k=k, isv=isv):
                dst = xnt[j][:, 128 * i:128 * (i + 1)]
                return q.tensor_copy(dst, ptr[k][:, :]) if isv else q.copy(
                    dst, ptr[k][:, :])
            cp_tok.append(P.op("vector" if isv else "scalar", cpf,
                               waits=[tok_tr], inc=(dv2 if isv else ac2)))
        tr_last[i] = pe.tok()
    for j in range(NE):
        if dst_hbm is not None:
            P.op("sync", lambda q, j=j: q.dma_start(
                dst_hbm[128 * j:128 * (j + 1), :], xnt[j][:, :]),
                waits=[cp_tok[-1], cp_tok[-2]], inc=stx, inc_k=16)
    t["ln_cp_last" + tag] = [cp_tok[-1], cp_tok[-2]]
    P.flush("ln" + tag)
    lctx.close()



def _b0(nc, ctx, P, t):
    with ExitStack() as pctx:
        sb = lambda name, shape, dt=F32: pctx.enter_context(nc.sbuf_tensor(name, shape, dt))
        qlat = sb("qlat", [QL, T], BF16)
        kvlat = sb("kvlat", [KVL, T], BF16)
        t["wqb_sb_g"] = sb("wqb_sb_g", [QL, E], BF16)
        t["wkvb_sb_g"] = sb("wkvb_sb_g", [KVL, 512], BF16)
        with ExitStack() as lctx:
            xnt = [lctx.enter_context(nc.sbuf_tensor(f"xnt{j}", [128, T], BF16))
                   for j in range(NE)]
            _emit_ln(nc, None, P, t, t["x_in"], xnt, t["xnt_hbm"], "A", "dbg_xnt")
            _b0_latents(nc, lctx, P, t, xnt, qlat, kvlat)
        qts = [sb(f"qt{j}", [128, T], BF16) for j in range(NE)]
        ktd = sb("ktd", [128, 2 * T], BF16)
        vsb = sb("vsb", [128, NT * 256], BF16)
        wout_sb = sb("wout_sb", [128, NE * E], BF16)
        P.op("gpsimd", lambda q: q.dma_start(
            ap3(wout_sb[:, :], E), t["wout"].rearrange("(j p) c -> p j c", p=128)),
            inc=P.sem("gmWo"), inc_k=16)
        _b0_qkv_tiles(nc, pctx, P, t, qlat, kvlat, qts, ktd, vsb)
        _emit_wprecast(P, t)
        _b0_attn(nc, pctx, P, t, qts, ktd, vsb, wout_sb)
    _b0_ffn(nc, ctx, P, t)


def _b0_latents(nc, lctx, P, t, xnt, qlat, kvlat):
    sb = lambda name, shape, dt=F32: lctx.enter_context(nc.sbuf_tensor(name, shape, dt))
    ps = lambda name, shape, dt=F32: lctx.enter_context(nc.psum_tensor(name, shape, dt))
    wqa_sb = sb("wqa_sb", [128, NE * QL], BF16)
    wkva_sb = sb("wkva_sb", [128, NE * KVL], BF16)
    pql = [ps(f"pql{k}", [128, 512]) for k in range(2)]

    gw = P.sem("gmW")
    pe, ac = P.sem("pe"), P.sem("ac")

    P.op("gpsimd", lambda q: q.dma_start(
        ap3(wqa_sb[:, :], QL), t["wqa"].rearrange("(j p) c -> p j c", p=128)), inc=gw, inc_k=16)
    P.op("gpsimd", lambda q: q.dma_start(
        ap3(wkva_sb[:, :], KVL), t["wkva"].rearrange("(j p) c -> p j c", p=128)), inc=gw, inc_k=16)
    P.op("gpsimd", lambda q: q.dma_start(t["wqb_sb_g"][:, :], t["wqb"]), inc=gw, inc_k=16)
    P.op("gpsimd", lambda q: q.dma_start(t["wkvb_sb_g"][:, :], t["wkvb"]), inc=gw, inc_k=16)
    tok_wqa = tok_wkva = gw.tok()
    t["wqbkv_tok"] = gw.tok()

    ev_tok = [None, None]
    for which, (wtok, wsb, cw, dst, rows) in enumerate(
            [(tok_wqa, wqa_sb, QL, qlat, 64), (tok_wkva, wkva_sb, KVL, kvlat, 32)]):
        for n in range(8):
            k = n % 2
            w = [wtok, ev_tok[k]] + (t["ln_cp_lastA"] if n >= 7 else [])
            for j in range(NE):
                P.op("tensor", lambda q, n=n, j=j, k=k, wsb=wsb, cw=cw, rows=rows: q.matmul(
                    pql[k][0:rows, :], wsb[:, cw * j:cw * (j + 1)],
                    xnt[j][:, 512 * n:512 * (n + 1)], start=(j == 0), stop=(j == NE - 1)),
                    waits=(w if j == 0 else []), inc=(pe if j == NE - 1 else None))
            tok_mm = pe.tok()
            ev_tok[k] = P.op("scalar", lambda q, n=n, k=k, dst=dst, rows=rows: q.copy(
                dst[:, 512 * n:512 * (n + 1)], pql[k][0:rows, :]), waits=[tok_mm], inc=ac)
    P.flush("p2a1")


def _b0_qkv_tiles(nc, pctx, P, t, qlat, kvlat, qts, ktd, vsb):
    sb = lambda name, shape, dt=F32: pctx.enter_context(nc.sbuf_tensor(name, shape, dt))
    lctx = ExitStack()
    sbl = lambda name, shape, dt=F32: lctx.enter_context(nc.sbuf_tensor(name, shape, dt))
    psl = lambda name, shape, dt=F32: lctx.enter_context(nc.psum_tensor(name, shape, dt))
    wqb_sb = t["wqb_sb_g"]
    wkvb_sb = t["wkvb_sb_g"]
    qf = sbl("qf", [128, 2 * E])
    kvf = sbl("kvf", [128, 2 * 512])
    q_rope = sbl("q_rope", [128, 2 * E], BF16)
    k_rope = sbl("k_rope", [128, 2 * 256], BF16)
    cs_t = sbl("cs_t", [128, 2 * 512])
    sn_t = sbl("sn_t", [128, 2 * 512])
    rtmp = sbl("rtmp", [128, 2 * 512])
    sq_scr2 = sbl("sq_scr2", [128, E])
    nrm = sbl("nrm", [128, 8])
    smal = sbl("smal", [1, 160])
    pQ = psl("pQ", [128, 1024])
    pkv = psl("pkv", [128, 512])
    ptr = [psl(f"ptrQK{k}", [128, 128], BF16) for k in range(2)]

    pe, ac, dv, dv2 = (P.sem(n) for n in ["pe", "ac", "dv", "dv2"])
    dcs = [P.sem("dmCS0"), P.sem("dmCS1")]
    dsc, dscl, dnc = P.sem("dmSc"), P.sem("dmScL"), P.sem("dmNc")
    tok_wqb = t["wqbkv_tok"]
    tok_wkvb = t["wqbkv_tok"]

    P.op("vector", lambda q: q.memset(nrm[:, :], 0.0), inc=dv)

    cp_tok = []
    tile = [dict() for _ in range(NT)]
    for i in range(NT):
        b = i % 2
        ti = tile[i]
        prev = tile[i - 2] if i >= 2 else None
        w = [prev["rope_q"], prev["rope_k"]] if prev else []
        P.op("sync", lambda q, i=i, b=b: q.dma_start(
            cs_t[:, 512 * b:512 * (b + 1)], t["cosq"][128 * i:128 * (i + 1), :]),
            waits=w, inc=dcs[b], inc_k=16)
        ti["cs"] = P.op("sync", lambda q, i=i, b=b: q.dma_start(
            sn_t[:, 512 * b:512 * (b + 1)], t["sinq"][128 * i:128 * (i + 1), :]),
            inc=dcs[b], inc_k=16)
        if i == 8:
            t["qkv_gate"] = ti["cs"]
        w = [tok_wqb] + ([tile[i - 1]["qev"]] if i >= 1 else [])
        for h in range(2):
            P.op("tensor", lambda q, i=i, h=h: q.matmul(
                pQ[:, 512 * h:512 * (h + 1)], qlat[:, 128 * i:128 * (i + 1)],
                wqb_sb[:, 512 * h:512 * (h + 1)], start=True, stop=True),
                waits=(w if h == 0 else []), inc=(pe if h == 1 else None))
        tok_qmm = pe.tok()
        w = [tok_qmm] + ([prev["rope_q"], prev["qn"]] if prev else [])
        ti["qev"] = P.op("vector", lambda q, b=b: q.tensor_copy(
            qf[:, E * b:E * (b + 1)], pQ[:, :]), waits=w, inc=dv)
        ti["qn"] = P.op("scalar", lambda q, b=b: q.activation(
            sq_scr2[:, :], qf[:, E * b:E * (b + 1)], AF.Square, accum_out=nrm[:, 2:3]),
            waits=[ti["qev"]], inc=ac)
        P.op("vector", lambda q: q.tensor_max(nrm[:, 0:1], nrm[:, 0:1], nrm[:, 2:3]),
             waits=[ti["qn"]], inc=dv)

        def ropeq(q, b=b):
            qv = qf[:, E * b:E * (b + 1)]
            e = ap3(qv, 64)[:, :, 0:32]
            o = ap3(qv, 64)[:, :, 32:64]
            c = ap3(cs_t[:, 512 * b:512 * (b + 1)], 32)
            s = ap3(sn_t[:, 512 * b:512 * (b + 1)], 32)
            t1 = ap3(rtmp[:, 0:512], 32)
            t2 = ap3(rtmp[:, 512:1024], 32)
            ob = ap3(q_rope[:, E * b:E * (b + 1)], 64)
            q.tensor_mul(t1, e, c)
            q.tensor_mul(t2, o, s)
            q.tensor_sub(ob[:, :, 0:32], t1, t2)
            q.tensor_mul(t1, e, s)
            q.tensor_mul(t2, o, c)
            return q.tensor_add(ob[:, :, 32:64], t1, t2)
        w = [ti["cs"], ti["qev"]] + ([prev["trq_last"]] if prev else [])
        ti["rope_q"] = P.op("vector", ropeq, waits=w, inc=dv)

        w = [tok_wkvb] + ([tile[i - 1]["kvev"]] if i >= 1 else [])
        P.op("tensor", lambda q, i=i: q.matmul(
            pkv[:, :], kvlat[:, 128 * i:128 * (i + 1)], wkvb_sb[:, :], start=True, stop=True),
            waits=w, inc=pe)
        tok_kvmm = pe.tok()
        w = [tok_kvmm] + ([prev["rope_k"], prev["kn"], prev["vev"]] if prev else [])
        ti["kvev"] = P.op("vector", lambda q, b=b: q.tensor_copy(
            kvf[:, 512 * b:512 * (b + 1)], pkv[:, :]), waits=w, inc=dv)
        ti["kn"] = P.op("scalar", lambda q, b=b: q.activation(
            sq_scr2[:, 0:256], kvf[:, 512 * b:512 * b + 256], AF.Square,
            accum_out=nrm[:, 3:4]), waits=[ti["kvev"]], inc=ac)
        P.op("vector", lambda q: q.tensor_max(nrm[:, 1:2], nrm[:, 1:2], nrm[:, 3:4]),
             waits=[ti["kn"]], inc=dv)

        def ropek(q, b=b):
            kv = kvf[:, 512 * b:512 * b + 256]
            e = ap3(kv, 64)[:, :, 0:32]
            o = ap3(kv, 64)[:, :, 32:64]
            c = ap3(cs_t[:, 512 * b:512 * b + 128], 32)
            s = ap3(sn_t[:, 512 * b:512 * b + 128], 32)
            t1 = ap3(rtmp[:, 0:128], 32)
            t2 = ap3(rtmp[:, 128:256], 32)
            ob = ap3(k_rope[:, 256 * b:256 * (b + 1)], 64)
            q.tensor_mul(t1, e, c)
            q.tensor_mul(t2, o, s)
            q.tensor_sub(ob[:, :, 0:32], t1, t2)
            q.tensor_mul(t1, e, s)
            q.tensor_mul(t2, o, c)
            return q.tensor_add(ob[:, :, 32:64], t1, t2)
        w = [ti["cs"], ti["kvev"]] + ([prev["trk_last"]] if prev else [])
        ti["rope_k"] = P.op("vector", ropek, waits=w, inc=dv)
        ti["vev"] = P.op("vector", lambda q, i=i, b=b: q.tensor_copy(
            vsb[:, 256 * i:256 * (i + 1)], kvf[:, 512 * b + 256:512 * b + 512]),
            waits=[ti["kvev"]], inc=dv)

        for j in range(NE):
            k = len(cp_tok) % 2
            w = [ti["rope_q"]] + ([cp_tok[-2]] if len(cp_tok) >= 2 else [])
            tok_tr = P.op("tensor", lambda q, b=b, j=j, k=k: q.transpose(
                ptr[k][:, :], q_rope[:, E * b + 128 * j:E * b + 128 * (j + 1)],
                t["ident"][:, :]), waits=w, inc=pe)
            cp_tok.append(P.op("vector", lambda q, i=i, j=j, k=k: q.tensor_copy(
                qts[j][:, 128 * i:128 * (i + 1)], ptr[k][:, :]), waits=[tok_tr], inc=dv2))
        ti["trq_last"] = pe.tok()
        for jj in range(2):
            k = len(cp_tok) % 2
            w = [ti["rope_k"]] + ([cp_tok[-2]] if len(cp_tok) >= 2 else [])
            tok_tr = P.op("tensor", lambda q, b=b, jj=jj, k=k: q.transpose(
                ptr[k][:, :], k_rope[:, 256 * b + 128 * jj:256 * b + 128 * (jj + 1)],
                t["ident"][:, :]), waits=w, inc=pe)
            cp_tok.append(P.op("vector", lambda q, i=i, jj=jj, k=k: q.tensor_copy(
                ktd[:, T * jj + 128 * i:T * jj + 128 * (i + 1)], ptr[k][:, :]),
                waits=[tok_tr], inc=dv2))
        ti["trk_last"] = pe.tok()

    P.op("sync", lambda q: q.dma_start(
        t["sc_hbm"][0:1, :].rearrange("o n -> n o"), nrm[:, 0:1]),
        waits=[dv.tok()], inc=dsc, inc_k=16)
    tok_k2h = P.op("sync", lambda q: q.dma_start(
        t["sc_hbm"][1:2, :].rearrange("o n -> n o"), nrm[:, 1:2]),
        waits=[dv.tok()], inc=dsc, inc_k=16)
    P.op("sync", lambda q: q.dma_start(
        smal[0:1, 0:128], t["sc_hbm"][0:1, :]), waits=[tok_k2h], inc=dscl, inc_k=16)
    tok_b2 = P.op("sync", lambda q: q.dma_start(
        rtmp[0:1, 0:128], t["sc_hbm"][1:2, :]), waits=[tok_k2h], inc=dscl, inc_k=16)
    tok_b1 = tok_b2

    def cfin(q):
        P.sf(q, "vector", q.reduce_max(smal[0:1, 144:145], smal[0:1, 0:128], AX.X))
        P.sf(q, "vector", q.reduce_max(smal[0:1, 145:146], rtmp[0:1, 0:128], AX.X))
        return q.tensor_mul(smal[0:1, 146:147], smal[0:1, 144:145], smal[0:1, 145:146])
    tok_c1 = P.op("vector", cfin, waits=[tok_b1, tok_b2], inc=dv)

    def cfin2(q):
        P.sf(q, "scalar", q.activation(smal[0:1, 146:147], smal[0:1, 146:147], AF.Ln))
        return q.activation(smal[0:1, 146:147], smal[0:1, 146:147], AF.Exp, scale=0.5)
    tok_c2 = P.op("scalar", cfin2, waits=[tok_c1], inc=ac)

    def cfin3(q):
        P.sf(q, "vector", q.tensor_scalar_mul(smal[0:1, 147:148], smal[0:1, 146:147], 0.125))
        P.sf(q, "vector", q.tensor_scalar_min(smal[0:1, 147:148], smal[0:1, 147:148], 80.0))
        return q.tensor_scalar_mul(smal[0:1, 147:148], smal[0:1, 147:148], -1.0)
    tok_c3 = P.op("vector", cfin3, waits=[tok_c2], inc=dv)
    tok_c4 = P.op("sync", lambda q: q.dma_start(
        t["sc_hbm"][2:3, 0:1], smal[0:1, 147:148]), waits=[tok_c3], inc=dsc, inc_k=16)
    P.op("sync", lambda q: q.dma_start(
        t["negc"][:, :], t["sc_hbm"][2:3, 0:1].partition_broadcast(128)),
        waits=[tok_c4], inc=dnc, inc_k=16)
    P.flush("p2a2")
    lctx.close()


def _b0_attn(nc, pctx, P, t, qts, ktd, vsb, wout_sb):
    sb = lambda name, shape, dt=F32: pctx.enter_context(nc.sbuf_tensor(name, shape, dt))
    ps = lambda name, shape, dt=F32: pctx.enter_context(nc.psum_tensor(name, shape, dt))
    mask_sb = sb("mask_sb", [128, 4 * 1024], BF16)
    osel_sb = sb("osel_sb", [128, 256], BF16)
    sbc_sb = sb("sbc_sb", [16, 8 * 128])
    expb = sb("expb", [128, 4 * 512], BF16)
    outu = sb("outu", [128, 8 * 512], BF16)
    outn = sb("outn", [128, 8 * 512], BF16)
    rec16 = sb("rec16", [16, 512])
    recb = sb("recb", [128, 512])
    xres = sb("xres", [128, 2 * E])
    acct = sb("acct", [128, 2 * E])
    bout_rep = sb("bout_repL", [128, E])

    pst2 = [ps(f"pst2_{k}", [128, 1024]) for k in range(2)]
    pav0 = ps("pav0", [128, 512])
    pav1 = ps("pav1", [128, 512])
    pden = ps("pden", [128, 512])
    pbc = ps("pbc", [128, 512])

    pe, ac, dv, dv2, ac2 = (P.sem(n) for n in ["pe", "ac", "dv", "dv2", "ac2"])
    dm = P.sem("dmM")
    dxr = [P.sem("dmXr0"), P.sem("dmXr1")]
    dac = [P.sem("dmAc0"), P.sem("dmAc1")]

    P.op("sync", lambda q: q.dma_start(mask_sb[:, :], t["mask4"]), inc=dm, inc_k=16)
    P.op("sync", lambda q: q.dma_start(osel_sb[:, :], t["onesel"]), inc=dm, inc_k=16)
    P.op("sync", lambda q: q.dma_start(sbc_sb[:, :], t["selbc"]), inc=dm, inc_k=16)
    P.op("sync", lambda q: q.dma_start(
        bout_rep[:, :], t["bout"][0:1, :].partition_broadcast(128)), inc=dm, inc_k=16)
    dm_tok = dm.tok()

    xload_tok = [None, None]
    acc_store_tok = [None, None]
    norm_tail = None
    wout_tail = None
    rec_tail = None
    stbank_free = [None, None]
    expset_free = [None, None]

    for qt in range(8):
        nkc = 4 * (qt + 1)
        evac_tok = [None] * 8
        first_den = True
        last_exp = None
        for p in range(8):
            kv = p // 4

            def emit_st(kc, p=None, kvblk=None):
                ks = kc % 2
                wst = [stbank_free[ks], dm_tok]
                for h in range(2):
                    P.op("tensor", lambda q, p=p, kvblk=kvblk, kc=kc, h=h, ks=ks, qt=qt: q.matmul(
                        pst2[ks][:, 512 * h:512 * (h + 1)],
                        ktd[64 * h:64 * (h + 1),
                            T * kvblk + 128 * kc:T * kvblk + 128 * (kc + 1)],
                        qts[p][64 * h:64 * (h + 1), 512 * qt:512 * (qt + 1)],
                        start=True, stop=True, tile_position=(64 * h, 0)),
                        waits=(wst if h == 0 else []), inc=(pe if h == 1 else None))
                return pe.tok()

            st_tok = {0: emit_st(0, p, kv)}
            if nkc > 1:
                st_tok[1] = emit_st(1, p, kv)
            av_last = None
            for kc in range(nkc):
                diag_j = kc - 4 * qt
                es = kc % 2
                wexp = [st_tok[kc], expset_free[es]]

                def expf(q, kc=kc, es=es):
                    return q.activation(expb[:, 1024 * es:1024 * (es + 1)],
                                        pst2[kc % 2][:, :],
                                        AF.Exp, scale=0.125, bias=t["negc"][:, 0:1])
                tok_exp = P.op("scalar", expf, waits=wexp, inc=ac)
                stbank_free[kc % 2] = tok_exp
                if diag_j >= 0:
                    def maskf(q, diag_j=diag_j, es=es):
                        return q.tensor_mul(
                            expb[:, 1024 * es:1024 * (es + 1)],
                            expb[:, 1024 * es:1024 * (es + 1)],
                            mask_sb[:, 1024 * diag_j:1024 * (diag_j + 1)])
                    tok_exp = P.op("vector", maskf, waits=[tok_exp, dm_tok], inc=dv)
                last_exp = tok_exp
                if kc + 2 < nkc:
                    st_tok[kc + 2] = emit_st(kc + 2, p, kv)
                wav = [tok_exp]
                if kc == 0:
                    if p > 0:
                        wav.append(evac_tok[p - 1])
                    elif rec_tail is not None:
                        wav.append(rec_tail)

                def avf(q, p=p, kv=kv, kc=kc, nkc=nkc, fd=first_den, es=es):
                    e0 = expb[:, 1024 * es:1024 * es + 512]
                    e1 = expb[:, 1024 * es + 512:1024 * es + 1024]
                    q.matmul(pav0[0:64, :],
                             vsb[:, 256 * kc + 64 * (2 * kv):256 * kc + 64 * (2 * kv) + 64],
                             e0, start=(kc == 0), stop=(kc == nkc - 1), tile_position=(0, 0))
                    q.matmul(pav1[64:128, :],
                             vsb[:, 256 * kc + 64 * (2 * kv + 1):256 * kc + 64 * (2 * kv + 1) + 64],
                             e1, start=(kc == 0), stop=(kc == nkc - 1), tile_position=(0, 64))
                    q.matmul(pden[0:16, :], osel_sb[:, 16 * (2 * p):16 * (2 * p) + 16],
                             e0, start=fd, stop=False)
                    return q.matmul(pden[0:16, :],
                                    osel_sb[:, 16 * (2 * p + 1):16 * (2 * p + 1) + 16],
                                    e1, start=False, stop=(p == 7 and kc == nkc - 1))
                av_last = P.op("tensor", avf, waits=wav + [dm_tok], inc=pe)
                expset_free[es] = av_last
                first_den = False
            w = [av_last] + ([norm_tail] if norm_tail else [])
            def evacf(q, p=p):
                q.copy(outu[0:64, 512 * p:512 * (p + 1)], pav0[0:64, :])
                return q.copy(outu[64:128, 512 * p:512 * (p + 1)], pav1[64:128, :])
            evac_tok[p] = P.op("scalar", evacf, waits=w, inc=ac2)

        def recf(q):
            P.sf(q, "scalar", q.activation(rec16[:, :], pden[0:16, :], AF.Ln))
            return q.activation(rec16[:, :], rec16[:, :], AF.Exp, scale=-1.0)
        rec_tail = P.op("scalar", recf, waits=[av_last], inc=ac)
        norm_toks = []
        for p in range(8):
            w = [rec_tail] + ([norm_toks[-1]] if norm_toks else [])
            tok_bc = P.op("tensor", lambda q, p=p: q.matmul(
                pbc[:, :], sbc_sb[:, 128 * p:128 * (p + 1)], rec16[:, :],
                start=True, stop=True), waits=w, inc=pe)

            def normf(q, p=p):
                q.tensor_copy(recb[:, :], pbc[:, :])
                return q.tensor_mul(outn[:, 512 * p:512 * (p + 1)],
                                    outu[:, 512 * p:512 * (p + 1)], recb[:, :])
            norm_toks.append(P.op("vector", normf, waits=[tok_bc, evac_tok[p]], inc=dv))
        norm_tail = norm_toks[-1]

        wacc_tok = [None] * 4
        for s4 in range(4):
            i = 4 * qt + s4
            b = i % 2
            w = [acc_store_tok[b]] if acc_store_tok[b] else []
            xload_tok[b] = P.op("sync", lambda q, i=i, b=b: q.dma_start(
                xres[:, E * b:E * (b + 1)], t["x_in"][128 * i:128 * (i + 1), :]),
                waits=w, inc=dxr[b], inc_k=16)
            wmm = [norm_tail, last_exp, P.sems["gmWo"].tok()]
            if s4 >= 2:
                wmm.append(wacc_tok[s4 - 2])
            bs = s4 % 2
            for p in range(8):
                for h in range(2):
                    P.op("tensor", lambda q, p=p, h=h, s4=s4, bs=bs: q.matmul(
                        pst2[bs][:, 512 * h:512 * (h + 1)],
                        outn[:, 512 * p + 128 * s4:512 * p + 128 * (s4 + 1)],
                        wout_sb[:, E * p + 512 * h:E * p + 512 * (h + 1)],
                        start=(p == 0), stop=(p == 7)),
                        waits=(wmm if (p == 0 and h == 0) else []),
                        inc=(pe if (p == 7 and h == 1) else None))
            tok_wmm = pe.tok()
            wout_tail = tok_wmm

            def accf(q, b=b, bs=bs):
                q.tensor_add(acct[:, E * b:E * (b + 1)], pst2[bs][:, :], bout_rep[:, :])
                if False:
                    pass
                return q.tensor_add(acct[:, E * b:E * (b + 1)], acct[:, E * b:E * (b + 1)],
                                    xres[:, E * b:E * (b + 1)])
            wacc_tok[s4] = P.op("vector", accf, waits=[tok_wmm, xload_tok[b], dm_tok], inc=dv)
            acc_store_tok[b] = P.op("sync", lambda q, i=i, b=b: q.dma_start(
                t["acc_hbm"][128 * i:128 * (i + 1), :], acct[:, E * b:E * (b + 1)]),
                waits=[wacc_tok[s4]], inc=dac[b], inc_k=16)
            if "dbg_acc" in t["dbg"]:
                acc_store_tok[b] = P.op("sync", lambda q, i=i, b=b: q.dma_start(
                    t["dbg"]["dbg_acc"][128 * i:128 * (i + 1), :], acct[:, E * b:E * (b + 1)]),
                    waits=[wacc_tok[s4]], inc=dac[b], inc_k=16)
        stbank_free[0] = wacc_tok[2]
        stbank_free[1] = wacc_tok[3]
    P.flush("p2b")


def _b0_ffn(nc, ctx, P, t):
    with ExitStack() as pctx:
        sb = lambda name, shape, dt=F32: pctx.enter_context(nc.sbuf_tensor(name, shape, dt))
        ps = lambda name, shape, dt=F32: pctx.enter_context(nc.psum_tensor(name, shape, dt))
        TC = 1024
        NC = T // TC
        g1 = sb("g1", [128, NH * TC], BF16)
        g2 = sb("g2", [128, NH * TC], BF16)
        xct = sb("xct", [128, NE * TC], BF16)
        wt1 = [sb(f"wt1_{k}", [128, NE * 128], BF16) for k in range(2)]
        wt2 = [sb(f"wt2_{k}", [128, NE * 128], BF16) for k in range(2)]
        wth = [sb(f"wth_{k}", [128, NH * 128], BF16) for k in range(2)]
        wt3 = wth
        s1_sb = sb("s1_sb", [128, NH])
        s2_sb = sb("s2_sb", [128, NH])
        sh1_sb = sb("sh1_sb", [128, NH])
        sh2_sb = sb("sh2_sb", [128, NH])
        evtmp = sb("evtmp", [128, 2 * 512])
        t2tmp = sb("t2tmp", [128, 512])
        acc_t = sb("acc_t", [128, 2 * E])
        s3_rep = sb("s3_repL", [128, E])
        stat2 = sb("statB2", [128, 16])
        xnf = sb("xnfB", [128, 2 * E])
        stg = [sb("stgB0", [128, 8 * 512], BF16)]

        pp = [ps(f"pp{k}", [128, 1024]) for k in range(4)]

        pe, ac, dv, dv2, ac2 = (P.sem(n) for n in ["pe", "ac", "dv", "dv2", "ac2"])
        dm = P.sem("dmM")
        dxc = P.sem("dmXc")
        dw1 = [P.sem("gmw1_0"), P.sem("gmw1_1")]
        dw2 = [P.sem("gmw2_0"), P.sem("gmw2_1")]
        dwh = [P.sem("gmwh_0"), P.sem("gmwh_1")]
        wq = ["gpsimd", "gpsimd"]
        daq = [P.sem("dmAq0"), P.sem("dmAq1")]
        dx1 = [P.sem("dmX1_0"), P.sem("dmX1_1")]

        for nm, dst in [("s1m", s1_sb), ("s2m", s2_sb), ("sh1m", sh1_sb), ("sh2m", sh2_sb)]:
            P.op("sync", lambda q, nm=nm, dst=dst: q.dma_start(dst[:, :], t[nm]),
                 inc=dm, inc_k=16)
        P.op("sync", lambda q: q.dma_start(
            s3_rep[:, :], t["s3r"][0:1, :].partition_broadcast(128)), inc=dm, inc_k=16)
        dm_tok = dm.tok()

        ring_free = [None] * 8
        ring = [0]
        lnb_apply = [None, None]
        lnb_tr = [None, None]
        lnb_store = [None]
        dxn = P.sem("dmXn1")

        def slot_ap(s):
            return pp[s // 2][:, 512 * (s % 2):512 * (s % 2 + 1)]

        stageA_mm_tail = None
        acc_guard = [None, None]
        wcons1 = [None, None]
        wcons2 = [None, None]
        for c in range(NC):
            w = [stageA_mm_tail] if stageA_mm_tail else []
            xc_tok = P.op("sync", lambda q, c=c: q.dma_start(
                ap3(xct[:, :], TC),
                t["xnt_hbm"].rearrange("(j p) n -> p j n", p=128)[:, :, TC * c:TC * (c + 1)]),
                waits=w, inc=dxc, inc_k=16)

            evtmp_free = [None] * 2
            t2_free = [None] * 2
            evA_tail = None
            for f in range(NH):
                kb = f % 2
                wl1 = P.op(wq[kb], lambda q, f=f, kb=kb: q.dma_start(
                    ap3(wt1[kb][:, :], 128),
                    t["w1b"][f:f + 1, :].rearrange("f (p j c) -> (f p) j c", p=128, j=NE)),
                    waits=[wcons1[kb]] + ([t["pc_w1b"]] if (c == 0 and f < 2) else []),
                    inc=dw1[kb], inc_k=16)
                wl2 = P.op(wq[kb], lambda q, f=f, kb=kb: q.dma_start(
                    ap3(wt2[kb][:, :], 128),
                    t["w2b"][f:f + 1, :].rearrange("f (p j c) -> (f p) j c", p=128, j=NE)),
                    waits=[wcons2[kb]] + ([t["pc_w2b"]] if (c == 0 and f < 2) else []),
                    inc=dw2[kb], inc_k=16)
                slots = [(ring[0] + d) % 8 for d in range(4)]
                ring[0] += 4
                for half in range(2):
                    s = slots[half]
                    wmm = [xc_tok, wl1, ring_free[s]]
                    for j in range(NE):
                        P.op("tensor", lambda q, f=f, kb=kb, j=j, half=half, s=s: q.matmul(
                            slot_ap(s), wt1[kb][:, 128 * j:128 * (j + 1)],
                            xct[:, TC * j + 512 * half:TC * j + 512 * (half + 1)],
                            start=(j == 0), stop=(j == NE - 1)),
                            waits=(wmm if j == 0 else []), inc=(pe if j == NE - 1 else None))
                wcons1[kb] = pe.tok()
                for half in range(2):
                    s = slots[2 + half]
                    wmm = [wl2, ring_free[s]]
                    for j in range(NE):
                        P.op("tensor", lambda q, f=f, kb=kb, j=j, half=half, s=s: q.matmul(
                            slot_ap(s), wt2[kb][:, 128 * j:128 * (j + 1)],
                            xct[:, TC * j + 512 * half:TC * j + 512 * (half + 1)],
                            start=(j == 0), stop=(j == NE - 1)),
                            waits=(wmm if j == 0 else []), inc=(pe if j == NE - 1 else None))
                wcons2[kb] = pe.tok()
                tok_mm = pe.tok()
                for half in range(2):
                    s_w1 = slots[half]
                    s_w2 = slots[2 + half]
                    et = (2 * f + half) % 2
                    tt = 0
                    tok_silu = P.op("scalar", lambda q, f=f, s_w1=s_w1, et=et: q.activation(
                        evtmp[:, 512 * et:512 * (et + 1)], slot_ap(s_w1), AF.Silu,
                        scale=s1_sb[:, f:f + 1]),
                        waits=[tok_mm, evtmp_free[et], dm_tok], inc=ac)
                    ring_free[s_w1] = tok_silu

                    def evg(q, f=f, half=half, s_w2=s_w2, et=et):
                        q.tensor_scalar(t2tmp[:, :],
                                        slot_ap(s_w2), s2_sb[:, f:f + 1], None, ALU.mult)
                        return q.tensor_mul(
                            g1[:, TC * f + 512 * half:TC * f + 512 * (half + 1)],
                            evtmp[:, 512 * et:512 * (et + 1)],
                            t2tmp[:, :])
                    tok_g = P.op("vector", evg, waits=[tok_mm, tok_silu, dm_tok], inc=dv)
                    ring_free[s_w2] = tok_g
                    evtmp_free[et] = tok_g
                    evA_tail = tok_g
            stageA_mm_tail = pe.tok()

            def hidden_stage(src_g, dst_g, wname, ssb, prev_ev_tail):
                wconsh = [pe.tok(), pe.tok()]
                ev_tail = None
                for f in range(NH):
                    kb = f % 2
                    wlh = P.op(wq[kb], lambda q, f=f, kb=kb, wname=wname: q.dma_start(
                        ap3(wth[kb][:, :], 128),
                        t[wname][f:f + 1, :].rearrange("f (p j c) -> (f p) j c",
                                                       p=128, j=NH)),
                        waits=[wconsh[kb]] + ([t["pc_" + wname]] if f < 2 else []),
                        inc=dwh[kb], inc_k=16)
                    for half in range(2):
                        s = ring[0] % 8
                        ring[0] += 1
                        wmm = [wlh, ring_free[s], prev_ev_tail]
                        for j in range(NH):
                            P.op("tensor",
                                 lambda q, f=f, kb=kb, j=j, half=half, s=s, src_g=src_g: q.matmul(
                                     slot_ap(s), wth[kb][:, 128 * j:128 * (j + 1)],
                                     src_g[:, TC * j + 512 * half:TC * j + 512 * (half + 1)],
                                     start=(j == 0), stop=(j == NH - 1)),
                                 waits=(wmm if j == 0 else []),
                                 inc=(pe if j == NH - 1 else None))
                        tok_mm = pe.tok()
                        tok_ev = P.op("scalar",
                                      lambda q, f=f, half=half, s=s, ssb=ssb, dst_g=dst_g:
                                      q.activation(
                                          dst_g[:, TC * f + 512 * half:TC * f + 512 * (half + 1)],
                                          slot_ap(s), AF.Gelu, scale=ssb[:, f:f + 1]),
                                      waits=[tok_mm, dm_tok], inc=ac)
                        ring_free[s] = tok_ev
                        ev_tail = tok_ev
                    wconsh[kb] = pe.tok()
                return ev_tail

            evB_tail = hidden_stage(g1, g2, "h1b", sh1_sb, evA_tail)
            evC_tail = hidden_stage(g2, g1, "h2b", sh2_sb, evB_tail)

            for hp in range(2):
                wcons3 = [pe.tok(), pe.tok()]
                for jg in range(8):
                    kb = jg % 2
                    wl3 = P.op(wq[kb], lambda q, jg=jg, kb=kb: q.dma_start(
                        ap3(wt3[kb][:, :], E),
                        t["w3b"].rearrange("(a p) n -> p a n", p=128)[:, 4 * jg:4 * (jg + 1), :]),
                        waits=[wcons3[kb], t["pc_w3b"]], inc=dwh[kb], inc_k=16)
                    for j4 in range(4):
                        j = 4 * jg + j4
                        for ts_ in range(4):
                            wmm = []
                            if jg == 0 and j4 == 0:
                                wmm = [wl3, evC_tail, ring_free[2 * ts_],
                                       ring_free[2 * ts_ + 1]]
                            elif j4 == 0 and ts_ == 0:
                                wmm = [wl3]
                            for h in range(2):
                                P.op("tensor",
                                     lambda q, j=j, j4=j4, kb=kb, ts_=ts_, h=h, hp=hp: q.matmul(
                                         pp[ts_][:, 512 * h:512 * (h + 1)],
                                         g1[:, TC * j + 128 * (4 * hp + ts_):
                                            TC * j + 128 * (4 * hp + ts_ + 1)],
                                         wt3[kb][:, E * j4 + 512 * h:E * j4 + 512 * (h + 1)],
                                         start=(j == 0), stop=(j == NH - 1)),
                                     waits=wmm, inc=(pe if (j == NH - 1 and h == 1) else None))
                    wcons3[kb] = pe.tok()
                stageD_mm_tail = pe.tok()
                for ts_ in range(4):
                    i = (TC // 128) * c + 4 * hp + ts_
                    b = ts_ % 2
                    g = 2 * c + hp
                    kst = 0
                    sc = 8 * b
                    w = [acc_guard[b], lnb_apply[b]] if acc_guard[b] else []
                    tok_acc = P.op("sync", lambda q, i=i, b=b: q.dma_start(
                        acc_t[:, E * b:E * (b + 1)], t["acc_hbm"][128 * i:128 * (i + 1), :]),
                        waits=w, inc=daq[b], inc_k=16)

                    def evd(q, ts_=ts_, b=b):
                        q.tensor_mul(evtmp[:, :], pp[ts_][:, :], s3_rep[:, :])
                        return q.tensor_add(acc_t[:, E * b:E * (b + 1)],
                                            acc_t[:, E * b:E * (b + 1)],
                                            evtmp[:, :])
                    tok_ev = P.op("vector", evd, waits=[stageD_mm_tail, tok_acc, dm_tok], inc=dv)
                    tok_st = P.op("sync", lambda q, i=i, b=b: q.dma_start(
                        t["x1_hbm"][128 * i:128 * (i + 1), :], acc_t[:, E * b:E * (b + 1)]),
                        waits=[tok_ev], inc=dx1[b], inc_k=16)
                    acc_guard[b] = tok_st

                    P.op("vector", lambda q, b=b, sc=sc: P.sf(q, "vector", q.reduce_sum(
                        stat2[:, sc:sc + 1], acc_t[:, E * b:E * (b + 1)], AX.X)) and None,
                        waits=[tok_ev, lnb_apply[b]])
                    sq_tok = P.op("scalar", lambda q, b=b, sc=sc: q.activation(
                        xnf[:, E * b:E * (b + 1)], acc_t[:, E * b:E * (b + 1)], AF.Square,
                        accum_out=stat2[:, sc + 1:sc + 2]),
                        waits=[tok_ev, lnb_tr[b]], inc=ac)

                    def smallb(q, sc=sc):
                        P.sf(q, "vector", q.tensor_scalar_mul(stat2[:, sc + 2:sc + 3],
                                                              stat2[:, sc:sc + 1], 1.0 / E))
                        q.tensor_scalar_mul(stat2[:, sc + 3:sc + 4], stat2[:, sc + 2:sc + 3],
                                            -1.0)
                        P.sf(q, "vector", q.tensor_mul(stat2[:, sc + 4:sc + 5],
                                                       stat2[:, sc + 2:sc + 3],
                                                       stat2[:, sc + 2:sc + 3]))
                        P.sf(q, "vector", q.tensor_scalar_mul(stat2[:, sc + 5:sc + 6],
                                                              stat2[:, sc + 1:sc + 2], 1.0 / E))
                        return q.tensor_sub(stat2[:, sc + 6:sc + 7], stat2[:, sc + 5:sc + 6],
                                            stat2[:, sc + 4:sc + 5])
                    tok_var = P.op("vector", smallb, waits=[sq_tok], inc=dv)

                    def rstdb(q, sc=sc):
                        P.sf(q, "scalar", q.activation(stat2[:, sc + 7:sc + 8],
                                                       stat2[:, sc + 6:sc + 7],
                                                       AF.Ln, bias=t["epsc"][:, 0:1]))
                        return q.activation(stat2[:, sc + 7:sc + 8], stat2[:, sc + 7:sc + 8],
                                            AF.Exp, scale=-0.5)
                    tok_rstd = P.op("scalar", rstdb, waits=[tok_var], inc=ac)
                    lnb_apply[b] = P.op("vector", lambda q, b=b, sc=sc: q.tensor_scalar(
                        xnf[:, E * b:E * (b + 1)], acc_t[:, E * b:E * (b + 1)],
                        stat2[:, sc + 3:sc + 4], stat2[:, sc + 7:sc + 8], ALU.add, ALU.mult),
                        waits=[tok_rstd], inc=dv)
                    for k in range(NE):
                        P.op("tensor", lambda q, ts_=ts_, b=b, k=k: q.transpose(
                            pp[ts_][:, 128 * k:128 * (k + 1)],
                            xnf[:, E * b + 128 * k:E * b + 128 * (k + 1)],
                            t["identf"][:, :]),
                            waits=([lnb_apply[b]] if k == 0 else []),
                            inc=(pe if k == NE - 1 else None))
                    lnb_tr[b] = pe.tok()
                    ev_half = [None, None]
                    prev_v = None
                    for k in range(NE):
                        isv = (k % 2 == 0)

                        def evk(q, ts_=ts_, k=k, kst=kst, isv=isv):
                            dst = stg[kst][:, 512 * k + 128 * ts_:512 * k + 128 * (ts_ + 1)]
                            src = pp[ts_][:, 128 * k:128 * (k + 1)]
                            return q.tensor_copy(dst, src) if isv else q.copy(dst, src)
                        wv = [lnb_tr[b]]
                        if ts_ == 0 and k < 2:
                            wv.append(lnb_store[kst])
                        if not isv:
                            wv.append(prev_v)
                        tok = P.op("vector" if isv else "scalar", evk, waits=wv,
                                   inc=(dv if isv else ac))
                        if isv:
                            prev_v = tok
                        else:
                            ev_half[k // 4] = tok
                    ring_free[2 * ts_] = ev_half[0]
                    ring_free[2 * ts_ + 1] = ev_half[1]
                    if ts_ == 3:
                        lnb_store[kst] = P.op("sync", lambda q, g=g, kst=kst: q.dma_start(
                            t["xnt1_hbm"].rearrange("(j p) n -> p j n", p=128)
                            [:, :, 512 * g:512 * (g + 1)],
                            stg[kst][:, :].rearrange("p (j n) -> p j n", n=512)),
                            waits=[ev_half[0], ev_half[1]], inc=dxn, inc_k=16)
        P.flush("p3")



def _b1(nc, ctx, P, t):
    with ExitStack() as pctx:
        sb = lambda name, shape, dt=F32: pctx.enter_context(nc.sbuf_tensor(name, shape, dt))
        scale_gn = sb("scale_gn", [128, NE])
        bias_gn = sb("bias_gn", [128, NE])
        _b1_merged2(nc, P, t, scale_gn, bias_gn)
        _b1_wo(nc, P, t, scale_gn, bias_gn)


def _b1_merged2(nc, P, t, scale_gn, bias_gn):
    lctx = ExitStack()
    sb = lambda name, shape, dt=F32: lctx.enter_context(nc.sbuf_tensor(name, shape, dt))
    ps = lambda name, shape, dt=F32: lctx.enter_context(nc.psum_tensor(name, shape, dt))
    CH, NCH = 256, 16
    TCM, NCM = 512, 8
    CH1 = CH + 1

    pxp = [ps(f"pxpM{k}", [128, 512]) for k in range(4)]
    pr = [ps(f"prB{k}", [128, 256]) for k in range(3)]
    pmisc = ps("pmiscB", [128, 512])

    wrt_sb = sb("wrt_sb", [128, NE * E], BF16)
    wkt_sb = sb("wkt_sb", [128, NE * E], BF16)
    wvt_sb = sb("wvt_sb", [128, NE * E], BF16)
    small8 = sb("small8", [128, 96])
    ob2_sb = sb("ob2_sb", [128, 2])
    sel2_sb = sb("sel2_sb", [2, 128])
    ew_rep = sb("ew_rep", [128, NE * CH])
    xw = [sb(f"xwM{k}", [128, NE * CH1], BF16) for k in range(2)]
    Kbuf = sb("Kbuf", [128, NE * CH])
    vbuf = sb("vbufB", [128, NE * CH], BF16)
    sigr = sb("sigrB", [128, NE * CH], BF16)
    Bbuf = sb("Bbuf", [128, NE * CH1])
    Gbuf = sb("Gbuf", [128, NE * CH1])
    tmp1 = sb("tmp1B", [128, NE * CH])
    tmp2 = sb("tmp2B", [128, NE * CH])
    tmp3 = sb("tmp3B", [128, NE * CH])
    tmp4 = sb("tmp4B", [128, CH])
    tokd = sb("tokdB", [128, 1])
    xmixp = {p: sb(f"xmix_{p}", [128, NE * CH], BF16) for p in "rkv"}
    sums = sb("sumsB", [128, 32])
    redb = sb("redbB", [128, 16])
    gstat = sb("gstatB", [2, 32])

    mw = [sb(f"mwM{k}", [128, NE * TCM], BF16) for k in range(2)]
    ew_t = [sb(f"ewtM{k}", [128, NE * 512], BF16) for k in range(2)]
    scl = [sb(f"sclM{k}", [128, NE * 512], BF16) for k in range(3)]
    wball = [sb(f"wbM{k}", [128, TCM], BF16) for k in range(3)]
    mev = sb("mevM", [128, 4 * 512])
    gw_sb = sb("gwM", [128, NE * NEXP], BF16)
    gbr = sb("gbrM", [128, NEXP])
    eb_sb = sb("ebM", [NEXP, E], BF16)
    gg = sb("ggM", [128, 64])
    wn = sb("wnM", [128, NEXP], BF16)
    wnT = [sb(f"wnTM{k}", [NEXP, TCM], BF16) for k in range(2)]

    pe, ac, dv = (P.sem(n) for n in ["pe", "ac", "dv"])
    gp = P.sem("gpP")
    gw = P.sem("gmW")
    gmG = P.sem("gmG")
    dmb = P.sem("dmM")
    drw = P.sem("gmRw")
    dmw = [P.sem("dmMw0"), P.sem("dmMw1")]
    dwn = P.sem("dmWn")
    dwb = [P.sem("smWb0"), P.sem("smWb1"), P.sem("smWb2")]
    dmx = [P.sem("dmXw0"), P.sem("dmXw1")]
    dme = [P.sem(f"dmEw{k}") for k in range(2)]
    dmoe = P.sem("dmMoe")

    xnv = t["xnt1_hbm"].rearrange("(j p) n -> p j n", p=128)
    TD, TF, TMR, TMK, TMV, GM, GB = 0, 8, 16, 24, 32, 40, 48
    EW, EU, OMR, OMK, OMV = 56, 64, 72, 80, 88

    shared = {"moe_tail": []}

    def gen_scan():
        for nm, dst in [("wrt", wrt_sb), ("wkt", wkt_sb), ("wvt", wvt_sb)]:
            P.op("gpsimd", lambda q, nm=nm, dst=dst: q.dma_start(
                ap3(dst[:, :], E), t[nm].rearrange("(j p) c -> p j c", p=128)),
                inc=gw, inc_k=16)
        gw_tok = gw.tok()
        for i, nm in enumerate(["tdm", "tfm", "tmr", "tmk", "tmv", "gmm", "gbm"]):
            P.op("sync", lambda q, nm=nm, i=i: q.dma_start(
                small8[:, 8 * i:8 * (i + 1)], t[nm]), inc=dmb, inc_k=16)
        P.op("sync", lambda q: q.dma_start(ob2_sb[:, :], t["onesbd2"]), inc=dmb, inc_k=16)
        P.op("sync", lambda q: q.dma_start(sel2_sb[:, :], t["sel2"]), inc=dmb, inc_k=16)
        dmb_tok = dmb.tok()
        OM = {TMR: OMR, TMK: OMK, TMV: OMV}

        def prep(q):
            P.sf(q, "scalar", q.activation(small8[:, EW:EW + 8], small8[:, TD:TD + 8],
                                           AF.Exp))
            P.sf(q, "scalar", q.activation(small8[:, EW:EW + 8], small8[:, EW:EW + 8],
                                           AF.Exp, scale=-1.0))
            return q.activation(small8[:, EU:EU + 8], small8[:, TF:TF + 8], AF.Exp)
        tok_prep = P.op("scalar", prep, waits=[dmb_tok], inc=ac)

        def prep2(q):
            for src_c, dst_c in [(TMR, OMR), (TMK, OMK), (TMV, OMV)]:
                P.sf(q, "vector", q.tensor_scalar(small8[:, dst_c:dst_c + 8],
                                                  small8[:, src_c:src_c + 8],
                                                  -1.0, 1.0, ALU.mult, ALU.add))
            q.memset(Bbuf[:, :], 0.0)
            q.memset(Gbuf[:, :], 0.0)
            q.memset(sums[:, :], 0.0)
            q.memset(ap3(xw[0][:, :], CH1)[:, :, 0:1], 0.0)
            P.sf(q, "vector", q.memset(tmp1[:, 0:CH], 1.0))
            last = None
            for j in range(NE):
                last = q.tensor_scalar(ew_rep[:, CH * j:CH * (j + 1)], tmp1[:, 0:CH],
                                       small8[:, EW + j:EW + j + 1], None, ALU.mult)
            return last
        tok_prep2 = P.op("vector", prep2, waits=[dmb_tok, tok_prep], inc=dv)
        yield

        B3 = lambda off: Bbuf[:, :].rearrange("p (j x) -> p j x", x=CH1)[:, :, off:off + CH]
        G3 = lambda off: Gbuf[:, :].rearrange("p (j x) -> p j x", x=CH1)[:, :, off:off + CH]
        f3 = lambda buf: ap3(buf[:, :], CH)

        ring_free = [None] * 3
        ring = [0]
        mmtail_p = {}
        prev_store = None
        tok_out = None
        mix_last = [None, None]
        def emit_xw(c):
            c0 = CH * c
            w = c % 2
            if c == 0:
                xw_tok = P.op("sync", lambda q: q.dma_start(
                    ap3(xw[0][:, :], CH1)[:, :, 1:CH1], xnv[:, :, 0:CH]),
                    waits=[], inc=dmx[0], inc_k=16)
            else:
                xw_tok = P.op("sync", lambda q, c0=c0, w=w: q.dma_start(
                    ap3(xw[w][:, :], CH1), xnv[:, :, c0 - 1:c0 + CH]),
                    waits=[mix_last[w]], inc=dmx[w], inc_k=16)
            return xw_tok, w

        def emit_mix(proj, xw_tok, w):
            tm_c = {"r": TMR, "k": TMK, "v": TMV}[proj]
            om_c = OM[tm_c]
            xmix = xmixp[proj]

            def mixf(q, w=w, tm_c=tm_c, om_c=om_c, xmix=xmix):
                last = None
                for j in range(NE):
                    cur = xw[w][:, CH1 * j + 1:CH1 * (j + 1)]
                    prv = xw[w][:, CH1 * j:CH1 * j + CH]
                    q.tensor_scalar(tmp4[:, :], cur,
                                    small8[:, tm_c + j:tm_c + j + 1], None, ALU.mult)
                    last = q.scalar_tensor_tensor(
                        xmix[:, CH * j:CH * (j + 1)], prv,
                        small8[:, om_c + j:om_c + j + 1],
                        tmp4[:, :], ALU.mult, ALU.add)
                return last
            tok = P.op("vector", mixf,
                       waits=[tok_prep2, xw_tok, mmtail_p.get(proj)], inc=dv)
            if proj == "v":
                mix_last[w] = tok
            return tok

        nxt = emit_xw(0)
        mixes = {p: emit_mix(p, *nxt) for p in "rkv"}
        for c in range(NCH):
            c0 = CH * c
            proj_ev = {}
            for proj, wsb, tm_c in [("r", wrt_sb, TMR), ("k", wkt_sb, TMK),
                                    ("v", wvt_sb, TMV)]:
                xmix = xmixp[proj]
                tok_mix = mixes[proj]
                for fj in range(NE):
                    s = ring[0] % 3
                    ring[0] += 1
                    wmm = [tok_mix, gw_tok, ring_free[s]]
                    for j in range(NE):
                        P.op("tensor", lambda q, fj=fj, j=j, wsb=wsb, s=s, xmix=xmix: q.matmul(
                            pr[s][:, :], wsb[:, E * j + 128 * fj:E * j + 128 * (fj + 1)],
                            xmix[:, CH * j:CH * (j + 1)], start=(j == 0),
                            stop=(j == NE - 1)),
                            waits=(wmm if j == 0 else []), inc=(pe if j == NE - 1 else None))
                    tok_mm = pe.tok()
                    if proj == "r":
                        ev = P.op("scalar", lambda q, fj=fj, s=s: q.activation(
                            sigr[:, CH * fj:CH * (fj + 1)], pr[s][:, :], AF.Sigmoid),
                            waits=[tok_mm] + ([prev_store] if prev_store else []), inc=ac)
                    elif proj == "k":
                        ev = P.op("scalar", lambda q, fj=fj, s=s: q.activation(
                            Kbuf[:, CH * fj:CH * (fj + 1)], pr[s][:, :], AF.Exp),
                            waits=[tok_mm] + ([prev_store] if prev_store else []), inc=ac)
                    else:
                        ev = P.op("scalar", lambda q, fj=fj, s=s: q.copy(
                            vbuf[:, CH * fj:CH * (fj + 1)], pr[s][:, :]),
                            waits=[tok_mm], inc=ac)
                    ring_free[s] = ev
                mmtail_p[proj] = pe.tok()
                proj_ev[proj] = ev
                yield

            nxt = emit_xw(c + 1) if c + 1 < NCH else None
            nmix = {}

            def bscan(q):
                last = None
                for j in range(NE):
                    last = q.tensor_tensor_scan(
                        Bbuf[:, CH1 * j + 1:CH1 * (j + 1)],
                        ew_rep[:, CH * j:CH * (j + 1)], Kbuf[:, CH * j:CH * (j + 1)],
                        Bbuf[:, CH1 * j:CH1 * j + 1], ALU.mult, ALU.add)
                return last
            tok_b = P.op("vector", bscan, waits=[proj_ev["k"], tok_prep2], inc=dv)

            def mxf(q):
                last = None
                for j in range(NE):
                    last = q.scalar_tensor_tensor(
                        tmp1[:, CH * j:CH * (j + 1)],
                        Bbuf[:, CH1 * j:CH1 * j + CH],
                        small8[:, EW + j:EW + j + 1],
                        Kbuf[:, CH * j:CH * (j + 1)], ALU.mult, ALU.max)
                return last
            tok_mx = P.op("vector", mxf, waits=[tok_b], inc=dv)
            if nxt:
                nmix["r"] = emit_mix("r", *nxt)

            def rmxf(q):
                P.sf(q, "scalar", q.activation(tmp2[:, :], tmp1[:, :], AF.Ln))
                return q.activation(tmp1[:, :], tmp2[:, :], AF.Exp, scale=-1.0)
            tok_rmx = P.op("scalar", rmxf, waits=[tok_mx], inc=ac)
            tok_rho = P.op("vector", lambda q: q.tensor_mul(f3(tmp2), B3(1), f3(tmp1)),
                           waits=[tok_rmx] + ([prev_store] if prev_store else []), inc=dv)

            def cdf(q):
                for j in range(NE):
                    q.tensor_scalar(tmp1[:, CH * j:CH * (j + 1)],
                                    tmp2[:, CH * j:CH * (j + 1)],
                                    small8[:, EW + j:EW + j + 1], None, ALU.mult)
                q.tensor_mul(tmp3[:, :], Kbuf[:, :], vbuf[:, :])
                return q.tensor_mul(tmp3[:, :], tmp2[:, :], tmp3[:, :])
            tok_cd = P.op("vector", cdf, waits=[tok_rho, proj_ev["v"]], inc=dv)

            def gscan(q):
                last = None
                for j in range(NE):
                    last = q.tensor_tensor_scan(
                        Gbuf[:, CH1 * j + 1:CH1 * (j + 1)],
                        tmp1[:, CH * j:CH * (j + 1)], tmp3[:, CH * j:CH * (j + 1)],
                        Gbuf[:, CH1 * j:CH1 * j + 1], ALU.mult, ALU.add)
                return last
            tok_g = P.op("vector", gscan, waits=[tok_cd], inc=dv)

            def kuf(q):
                for j in range(NE):
                    q.tensor_scalar(tmp2[:, CH * j:CH * (j + 1)],
                                    Kbuf[:, CH * j:CH * (j + 1)],
                                    small8[:, EU + j:EU + j + 1], None, ALU.mult)
                return q.tensor_add(f3(tmp3), B3(0), f3(tmp2))
            tok_kua = P.op("vector", kuf, waits=[tok_g], inc=dv)
            if nxt:
                nmix["k"] = emit_mix("k", *nxt)

            def rdenf(q):
                P.sf(q, "scalar", q.activation(tmp1[:, :], tmp3[:, :], AF.Ln))
                return q.activation(tmp3[:, :], tmp1[:, :], AF.Exp, scale=-1.0)
            tok_rden = P.op("scalar", rdenf, waits=[tok_kua], inc=ac)

            def outf(q):
                q.tensor_mul(tmp1[:, :], tmp2[:, :], vbuf[:, :])
                q.tensor_add(f3(tmp2), G3(0), f3(tmp1))
                q.tensor_mul(tmp1[:, :], tmp2[:, :], tmp3[:, :])
                q.tensor_mul(tmp2[:, :], tmp1[:, :], sigr[:, :])
                P.sf(q, "vector", q.tensor_copy(
                    Bbuf[:, :].rearrange("p (j x) -> p j x", x=CH1)[:, :, 0:1],
                    Bbuf[:, :].rearrange("p (j x) -> p j x", x=CH1)[:, :, CH:CH1]))
                P.sf(q, "vector", q.tensor_copy(
                    Gbuf[:, :].rearrange("p (j x) -> p j x", x=CH1)[:, :, 0:1],
                    Gbuf[:, :].rearrange("p (j x) -> p j x", x=CH1)[:, :, CH:CH1]))
                return q.memset(tokd[:, :], 0.0)
            wo = [tok_rden, proj_ev["r"]] + ([prev_store] if prev_store else [])
            tok_out = P.op("vector", outf, waits=wo, inc=dv)
            if nxt:
                nmix["v"] = emit_mix("v", *nxt)
                mixes = nmix

            tok_sq = P.op("scalar", lambda q: q.activation(tmp1[:, :], tmp2[:, :],
                                                           AF.Square),
                          waits=[tok_out], inc=ac)

            def statf(q, c=c):
                po, pn = 16 * (c % 2), 16 * ((c + 1) % 2)
                for j in range(NE):
                    P.sf(q, "vector", q.reduce_sum(redb[:, j:j + 1],
                                                   tmp2[:, CH * j:CH * (j + 1)], AX.X))
                    P.sf(q, "vector", q.reduce_sum(redb[:, NE + j:NE + j + 1],
                                                   tmp1[:, CH * j:CH * (j + 1)], AX.X))
                return q.tensor_add(sums[:, pn:pn + 16], sums[:, po:po + 16], redb[:, :])
            tok_stat = P.op("vector", statf, waits=[tok_sq], inc=dv)
            prev_store = P.op("gpsimd", lambda q, c0=c0: q.dma_start(
                t["rwkv_hbm"].rearrange("(j p) n -> p j n", p=128)[:, :, c0:c0 + CH],
                ap3(tmp2[:, :], CH)), waits=[tok_out], inc=drw, inc_k=16)
            t["rwkv_store_tok"] = prev_store
            if "dbg_wkv" in t["dbg"]:
                prev_store = P.op("sync", lambda q, c0=c0: q.dma_start(
                    t["dbg"]["dbg_wkv"].rearrange("(j p) n -> p j n", p=128)[:, :,
                                                                            c0:c0 + CH],
                    ap3(tmp2[:, :], CH)), waits=[tok_stat], inc=drw, inc_k=16)
            yield

        shared["tok_stat"] = tok_stat
        shared["dmb_tok"] = dmb_tok

    def gen_moe():
        P.op("gpsimd", lambda q: q.dma_start(
            ap3(gw_sb[:, :], NEXP), t["gw"].rearrange("(j p) c -> p j c", p=128)),
            inc=gmG, inc_k=16)
        P.op("gpsimd", lambda q: q.dma_start(eb_sb[:, :], t["eb"]), inc=gmG, inc_k=16)
        tok_tbl = gmG.tok()
        P.op("sync", lambda q: q.dma_start(
            gbr[:, :], t["gb"][0:1, :].partition_broadcast(128)), inc=dmb, inc_k=16)
        tok_gb = dmb.tok()

        widx = 0
        scl_toks = {}
        wl_toks = {}
        wb_toks = {}
        emm_tails = {}
        mw_guard = [None, None]
        wnT_guard = [None, None]
        wnT_done = [None, None]
        gst = [None, None]
        mevs = [mev]
        st_mev = [None]
        prev_ev = {}
        tok_bias = None
        gstate = {"g1": None, "g2": None, "g4": None, "wn_ev": None}

        def emit_gates_load(c):
            par = c % 2
            c0 = TCM * c
            mw_tok = P.op("sync", lambda q, c0=c0, par=par: q.dma_start(
                ap3(mw[par][:, :], TCM), xnv[:, :, c0:c0 + TCM]),
                waits=(mw_guard[par] or []), inc=dmw[par], inc_k=16)
            return mw_tok

        def emit_gates_i4(c, mw_tok, i4):
            par = c % 2
            c0 = TCM * c
            if True:
                wmm = [mw_tok, tok_tbl]
                if gstate_first[0]:
                    gstate_first[0] = False
                else:
                    wmm += [gstate["g1"], gstate["wn_ev"]]
                for j in range(NE):
                    P.op("tensor", lambda q, i4=i4, j=j, par=par: q.matmul(
                        pmisc[:, 0:NEXP],
                        mw[par][:, TCM * j + 128 * i4:TCM * j + 128 * (i4 + 1)],
                        gw_sb[:, NEXP * j:NEXP * (j + 1)],
                        start=(j == 0), stop=(j == NE - 1)),
                        waits=(wmm if j == 0 else []), inc=(pe if j == NE - 1 else None))
                tok_gmm = pe.tok()

                def gatef(q):
                    P.sf(q, "vector", q.tensor_add(gg[:, 0:8], pmisc[:, 0:NEXP],
                                                   gbr[:, :]))
                    P.sf(q, "vector", q.reduce_max(gg[:, 8:9], gg[:, 0:8], AX.X))
                    return q.tensor_scalar_mul(gg[:, 9:10], gg[:, 8:9], -1.0)
                gstate["g1"] = P.op("vector", gatef,
                                    waits=[tok_gmm, tok_gb] +
                                    ([gstate["g2"]] if gstate["g2"] else []), inc=dv)
                gstate["g2"] = P.op("scalar", lambda q: q.activation(
                    gg[:, 16:24], gg[:, 0:8], AF.Exp, bias=gg[:, 9:10],
                    accum_out=gg[:, 10:11]), waits=[gstate["g1"]], inc=ac)

                def gatef2(q):
                    P.sf(q, "vector", q.reciprocal(gg[:, 11:12], gg[:, 10:11]))
                    P.sf(q, "vector", q.tensor_scalar(gg[:, 24:32], gg[:, 16:24],
                                                      gg[:, 11:12], None, ALU.mult))
                    P.sf(q, "vector", q.tensor_scalar(gg[:, 32:40], gg[:, 24:32],
                                                      0.1, None, ALU.is_ge))
                    P.sf(q, "vector", q.tensor_mul(gg[:, 40:48], gg[:, 24:32],
                                                   gg[:, 32:40]))
                    P.sf(q, "vector", q.reduce_sum(gg[:, 12:13], gg[:, 40:48], AX.X))
                    P.sf(q, "vector", q.tensor_scalar(gg[:, 13:14], gg[:, 12:13],
                                                      0.0, None, ALU.is_equal))
                    P.sf(q, "vector", q.tensor_add(gg[:, 14:15], gg[:, 12:13],
                                                   gg[:, 13:14]))
                    P.sf(q, "vector", q.reciprocal(gg[:, 15:16], gg[:, 14:15]))
                    return q.tensor_scalar(wn[:, :], gg[:, 40:48], gg[:, 15:16],
                                           None, ALU.mult)
                tok_g3 = P.op("vector", gatef2,
                              waits=[gstate["g2"]] +
                              ([gstate["g4"]] if gstate["g4"] else []), inc=dv)
                gstate["g4"] = P.op("tensor", lambda q: q.matmul(
                    pmisc[0:8, 128:256], wn[:, :], t["ident"][:, :],
                    start=True, stop=True),
                    waits=[tok_g3] + ([gstate["wn_ev"]] if gstate["wn_ev"] else []),
                    inc=pe)
                wev_w = [gstate["g4"]]
                if i4 == 0 and wnT_guard[par] is not None:
                    wev_w.append(wnT_guard[par])
                gstate["wn_ev"] = P.op("scalar", lambda q, i4=i4, par=par: q.copy(
                    wnT[par][:, 128 * i4:128 * (i4 + 1)], pmisc[0:8, 128:256]),
                    waits=wev_w, inc=ac)
            if i4 == 3:
                wnT_done[par] = gstate["wn_ev"]
                gst[par] = P.op("sync", lambda q, c0=c0, par=par: q.dma_start(
                    t["wn_hbm"][:, c0:c0 + TCM], wnT[par][:, :]),
                    waits=[gstate["wn_ev"]], inc=dwn, inc_k=16)

        gstate_first = [True]
        mwg = emit_gates_load(0)
        for _i4 in range(4):
            emit_gates_i4(0, mwg, _i4)
        yield

        for c in range(NCM):
            c0 = TCM * c
            par = c % 2
            quanta = [(rF, e) for rF in range(2) for e in range(NEXP)]

            def prepq(qi, c0=c0, par=par, quanta=quanta, widx=None):
                rF, e = quanta[qi]
                g = wbase + qi
                s = g % 2
                r = g % 3
                wl_toks[g] = P.op("sync", lambda q, e=e, rF=rF, s=s: q.dma_start(
                    ap3(ew_t[s][:, :], 512),
                    t["ewq"][2 * e + rF:2 * e + rF + 1, :].rearrange(
                        "o (p j c) -> (o p) j c", p=128, j=NE)),
                    waits=[emm_tails.get(g - 2)] + ([t["pc_ewt"]] if g < 2 else []),
                    inc=dme[s], inc_k=16)
                pw = scl_toks.get(g - 3)
                wb_toks[g] = P.op("scalar", lambda q, e=e, r=r, c0=c0: q.dma_start(
                    wball[r][:, :],
                    t["wn_hbm"][e:e + 1, c0:c0 + TCM].partition_broadcast(128)),
                    waits=[gst[par]] + (pw if pw else []), inc=dwb[r], inc_k=16)

                def sclh(q, r=r, par=par, js=None):
                    last = None
                    for j in js:
                        last = q.tensor_mul(scl[r][:, 512 * j:512 * (j + 1)],
                                            mw[par][:, TCM * j:TCM * (j + 1)],
                                            wball[r][:, :])
                    return last
                wsc = [wb_toks[g], emm_tails.get(g - 3)]
                ta = P.op("gpsimd", lambda q, r=r, par=par: sclh(q, r, par, (0, 1, 2, 3)),
                          waits=wsc, inc=gp)
                tb = P.op("gpsimd", lambda q, r=r, par=par: sclh(q, r, par, (4, 5, 6, 7)),
                          waits=[], inc=gp)
                scl_toks[g] = [ta, tb]

            wbase = widx
            prepq(0)
            for qi, (rF, e) in enumerate(quanta):
                if qi + 1 < len(quanta):
                    prepq(qi + 1)
                g = wbase + qi
                s = g % 2
                r = g % 3
                for Fp in range(4):
                    wmm = [wl_toks[g]] + scl_toks[g]
                    if e == 0:
                        wmm.append(prev_ev.get(Fp))
                    for j in range(NE):
                        P.op("tensor", lambda q, Fp=Fp, j=j, e=e, s=s, r=r: q.matmul(
                            pxp[Fp][:, :],
                            ew_t[s][:, 512 * j + 128 * Fp:512 * j + 128 * (Fp + 1)],
                            scl[r][:, 512 * j:512 * (j + 1)],
                            start=(e == 0 and j == 0), stop=False),
                            waits=(wmm if j == 0 else []),
                            inc=(pe if j == NE - 1 else None))
                emm_tails[g] = pe.tok()
                yield
                if c + 1 < NCM and qi in (1, 5, 9, 13):
                    if qi == 1:
                        mwg = emit_gates_load(c + 1)
                    emit_gates_i4(c + 1, mwg, (qi - 1) // 4)
                    yield
                if e < NEXP - 1:
                    continue
                for Fp in range(4):
                    P.op("tensor", lambda q, Fp=Fp, rF=rF, par=par: q.matmul(
                        pxp[Fp][:, :], eb_sb[:, 128 * (4 * rF + Fp):128 * (4 * rF + Fp + 1)],
                        wnT[par][:, :], start=False, stop=True),
                        waits=([tok_tbl, wnT_done[par]] if Fp == 0 else []),
                        inc=(pe if Fp == 3 else None))
                tok_bias = pe.tok()
                gpar = 0
                mv = mevs[gpar]
                for Fp in range(4):
                    def evm(q, Fp=Fp, mv=mv):
                        return q.copy(mv[:, 512 * Fp:512 * (Fp + 1)], pxp[Fp][:, :])
                    prev_ev[Fp] = P.op("scalar", evm,
                                       waits=[tok_bias, st_mev[gpar]], inc=ac)
                st_mev[gpar] = P.op("sync", lambda q, c0=c0, rF=rF, mv=mv: q.dma_start(
                    t["moe_hbm"].rearrange("(f p) n -> p f n", p=128)
                    [:, 4 * rF:4 * (rF + 1), c0:c0 + TCM],
                    mv[:, :].rearrange("p (f n) -> p f n", n=512)),
                    waits=[prev_ev[2], prev_ev[3]], inc=dmoe, inc_k=16)
                yield
            widx += len(quanta)
            wnT_guard[par] = tok_bias
            mw_guard[par] = scl_toks[widx - 1]
        shared["moe_tail"] = [st_mev[0],
                              gstate["g1"], gstate["wn_ev"]]

    gs = gen_scan()
    gm = gen_moe()
    next(gs)
    next(gm)
    gs_alive = gm_alive = True
    phase = 0
    while gs_alive or gm_alive:
        if gs_alive:
            try:
                next(gs)
            except StopIteration:
                gs_alive = False
        nmoe = 3 if (phase % 2 == 0) else 2
        for _ in range(nmoe):
            if gm_alive:
                try:
                    next(gm)
                except StopIteration:
                    gm_alive = False
        phase += 1

    fin = 16 * (NCH % 2)

    def statmm(q, fin=fin):
        return q.matmul(pmisc[0:2, 0:16], ob2_sb[:, :], sums[:, fin:fin + 16],
                        start=True, stop=True)
    tok_smm = P.op("tensor", statmm,
                   waits=[shared["tok_stat"], shared["dmb_tok"]] + shared["moe_tail"], inc=pe)
    NTOT = float(64 * T)

    def gnf(q):
        P.sf(q, "vector", q.tensor_copy(gstat[:, 0:16], pmisc[0:2, 0:16]))
        P.sf(q, "vector", q.tensor_scalar_mul(gstat[:, 16:24], gstat[:, 0:8],
                                              1.0 / NTOT))
        P.sf(q, "vector", q.tensor_scalar_mul(gstat[:, 24:32], gstat[:, 8:16],
                                              1.0 / NTOT))
        P.sf(q, "vector", q.tensor_mul(gstat[:, 0:8], gstat[:, 16:24],
                                       gstat[:, 16:24]))
        return q.tensor_sub(gstat[:, 8:16], gstat[:, 24:32], gstat[:, 0:8])
    tok_gn = P.op("vector", gnf, waits=[tok_smm], inc=dv)

    def gnf2(q):
        P.sf(q, "scalar", q.activation(gstat[:, 24:32], gstat[:, 8:16], AF.Ln,
                                       bias=t["epsc"][0:2, 0:1]))
        return q.activation(gstat[:, 24:32], gstat[:, 24:32], AF.Exp, scale=-0.5)
    tok_gn2 = P.op("scalar", gnf2, waits=[tok_gn], inc=ac)

    def bcmm(q):
        q.matmul(pmisc[:, 16:24], sel2_sb[:, :], gstat[:, 16:24],
                 start=True, stop=True)
        return q.matmul(pmisc[:, 24:32], sel2_sb[:, :], gstat[:, 24:32],
                        start=True, stop=True)
    tok_bcm = P.op("tensor", bcmm, waits=[tok_gn2], inc=pe)

    def sbf(q):
        P.sf(q, "vector", q.tensor_copy(redb[:, 0:8], pmisc[:, 16:24]))
        P.sf(q, "vector", q.tensor_copy(redb[:, 8:16], pmisc[:, 24:32]))
        P.sf(q, "vector", q.tensor_mul(scale_gn[:, :], redb[:, 8:16],
                                       small8[:, GM:GM + 8]))
        P.sf(q, "vector", q.tensor_mul(redb[:, 0:8], redb[:, 0:8], scale_gn[:, :]))
        return q.tensor_sub(bias_gn[:, :], small8[:, GB:GB + 8], redb[:, 0:8])
    P.op("vector", sbf, waits=[tok_bcm], inc=dv)


    P.flush("p45")
    lctx.close()


def _b1_wo(nc, P, t, scale_gn, bias_gn):
    lctx = ExitStack()
    sb = lambda name, shape, dt=F32: lctx.enter_context(nc.sbuf_tensor(name, shape, dt))
    ps = lambda name, shape, dt=F32: lctx.enter_context(nc.psum_tensor(name, shape, dt))
    wot_sb = sb("wotW", [128, NE * E], BF16)
    rwt = sb("rwtW", [128, NE * 512], BF16)
    nrm_b = sb("nrmW", [128, NE * 512], BF16)
    wo32 = sb("wo32W", [128, 512])
    moe_t = [sb(f"moeW{k}", [128, 512]) for k in range(2)]
    x1t = sb("x1tW", [128, 4 * E])
    aA = sb("aAW", [128, 4 * E])
    pwo = [ps(f"pwoW{k}", [128, 512]) for k in range(4)]

    pe, ac, dv, dv2 = (P.sem(n) for n in ["pe", "ac", "dv", "dv2"])
    gmw = P.sem("gmW")
    drl = P.sem("dmRl")
    dx1 = P.sem("dmXr0")
    dot = P.sem("dmAc0")

    P.op("gpsimd", lambda q: q.dma_start(
        ap3(wot_sb[:, :], E), t["wot"].rearrange("(j p) c -> p j c", p=128)),
        inc=gmw, inc_k=16)
    tok_w = gmw.tok()

    x1_guard = None
    o_guard = None
    rl_guard = None
    nrm_guard = None
    for n in range(8):
        c0 = 512 * n
        w = ([rl_guard] if rl_guard else []) + \
            ([t["rwkv_store_tok"]] if n == 0 else [])
        tok_rl = P.op("sync", lambda q, c0=c0: q.dma_start(
            ap3(rwt[:, :], 512),
            t["rwkv_hbm"].rearrange("(j p) n -> p j n", p=128)[:, :, c0:c0 + 512]),
            waits=w, inc=drl, inc_k=16)

        def nrmf(q):
            last = None
            for j in range(NE):
                last = q.tensor_scalar(nrm_b[:, 512 * j:512 * (j + 1)],
                                       rwt[:, 512 * j:512 * (j + 1)],
                                       scale_gn[:, j:j + 1], bias_gn[:, j:j + 1],
                                       ALU.mult, ALU.add)
            return last
        tok_nrm = P.op("vector", nrmf, waits=[tok_rl] + ([nrm_guard] if nrm_guard else []),
                       inc=dv)
        rl_guard = tok_nrm

        w = [x1_guard] if x1_guard else []
        tok_a1 = P.op("sync", lambda q, n=n: q.dma_start(
            x1t[:, :].rearrange("p (s e) -> p s e", e=E),
            t["x1_hbm"][512 * n:512 * (n + 1), :].rearrange("(s p) e -> p s e", p=128)),
            waits=w, inc=dx1, inc_k=16)

        comb_tok = {}
        ev_tok = {}
        tr_tok = {}
        tok_mo = {}
        for F in range(NE):
            kbank = F % 2
            tok_mo[F] = P.op("sync", lambda q, n=n, F=F: q.dma_start(
                moe_t[F % 2][:, :],
                t["moe_hbm"][128 * F:128 * (F + 1), 512 * n:512 * (n + 1)]),
                waits=[ev_tok.get(F - 2)], inc=dx1, inc_k=16)
            wmm = [tok_nrm, tok_w]
            if F >= 1:
                wmm.append(ev_tok[F - 1])
            if F >= 2:
                wmm.append(ev_tok[F - 2])
            for j in range(NE):
                P.op("tensor", lambda q, F=F, j=j, kbank=kbank: q.matmul(
                    pwo[kbank][:, :], wot_sb[:, E * j + 128 * F:E * j + 128 * (F + 1)],
                    nrm_b[:, 512 * j:512 * (j + 1)], start=(j == 0), stop=(j == NE - 1)),
                    waits=(wmm if j == 0 else []), inc=(pe if j == NE - 1 else None))
            tok_mm = pe.tok()
            w = [tok_mm, tok_mo[F]] + ([tr_tok[F - 1]] if F >= 1 else [])
            ev_tok[F] = P.op("vector", lambda q, F=F, kbank=kbank: q.tensor_add(
                wo32[:, :], pwo[kbank][:, :], moe_t[F % 2][:, :]), waits=w, inc=dv)

            def trf(q, F=F):
                last = None
                for s in range(4):
                    last = q.transpose(pwo[2 + (F % 2)][:, 128 * s:128 * (s + 1)],
                                      wo32[:, 128 * s:128 * (s + 1)], t["identf"][:, :])
                return last
            w = [ev_tok[F]] + ([comb_tok[F - 2]] if F >= 2 else [])
            tr_tok[F] = P.op("tensor", trf, waits=w, inc=pe)

            def combF(q, F=F):
                last = None
                for s in range(4):
                    last = q.tensor_add(aA[:, E * s + 128 * F:E * s + 128 * (F + 1)],
                                        x1t[:, E * s + 128 * F:E * s + 128 * (F + 1)],
                                        pwo[2 + (F % 2)][:, 128 * s:128 * (s + 1)])
                return last
            w = [tr_tok[F], tok_a1] + ([o_guard] if (F == 0 and o_guard) else [])
            comb_tok[F] = P.op("vector", combF, waits=w, inc=dv)
        x1_guard = comb_tok[NE - 1]
        o_guard = P.op("sync", lambda q, n=n: q.dma_start(
            t["out"][512 * n:512 * (n + 1), :].rearrange("(s p) e -> p s e", p=128),
            aA[:, :].rearrange("p (s e) -> p s e", e=E)),
            waits=[comb_tok[NE - 1]], inc=dot, inc_k=16)
        nrm_guard = tok_mm
    P.flush("p6")
    lctx.close()



def kernel(**inputs):
    prep = _host_prep(inputs)
    x = np.ascontiguousarray(np.asarray(inputs["L0_attn_x"], np.float32))
    nc = build()
    in_maps = []
    for c in range(NCORE):
        m = dict(prep)
        m["x"] = x[c]
        in_maps.append(m)
    res = run_bass_kernel_spmd(nc, in_maps, core_ids=list(range(NCORE)))
    out = np.stack([np.asarray(res.results[c]["out"], np.float32) for c in range(NCORE)], axis=0)
    return out


# version 29 (final)
# speedup vs baseline: 1.0183x; 1.0183x over previous
import numpy as np
import ml_dtypes
import concourse.bass as bass
import concourse.mybir as mybir
from concourse.bass_utils import run_bass_kernel_spmd
from contextlib import ExitStack

F32 = mybir.dt.float32
BF16 = mybir.dt.bfloat16
AF = mybir.ActivationFunctionType
ALU = mybir.AluOpType
AX = mybir.AxisListType

T = 4096
E = 1024
NT = T // 128
NE = E // 128
HID = 4096
NH = HID // 128
QL, KVL = 64, 32
NHEAD, KVH = 16, 4
NEXP = 8
LN_EPS = 1e-5
NCORE = 8

ENGINES = ("sync", "gpsimd", "tensor", "vector", "scalar")


class SemCtr:
    __slots__ = ("h", "n")

    def __init__(self, h):
        self.h = h
        self.n = 0

    def tok(self):
        return (self.h, self.n)


class Prog:
    def __init__(self, nc, ctx):
        self.nc = nc
        self.ctx = ctx
        self.streams = {e: [] for e in ENGINES}
        self.sems = {}
        self.lastwait = {e: {} for e in ENGINES}

    def sem(self, name):
        if name not in self.sems:
            h = self.ctx.enter_context(self.nc.semaphore(name))
            self.sems[name] = SemCtr(h)
        return self.sems[name]

    def op(self, eng, build, waits=(), inc=None, inc_k=1):
        real = []
        lw = self.lastwait[eng]
        for wt in waits:
            if wt is None:
                continue
            h, v = wt
            if v <= 0 or lw.get(h, -1) >= v:
                continue
            lw[h] = v
            real.append((h, v))
        if inc is not None:
            inc.n += inc_k

        def fn(q, build=build, rw=tuple(real), inc=inc, k=inc_k):
            for (h, v) in rw:
                q.wait_ge(h, v)
            ins = build(q)
            if inc is not None:
                ins.then_inc(inc.h, k)
        self.streams[eng].append(fn)
        return inc.tok() if inc is not None else None

    def fence_sem(self, eng):
        return self.sem("fence_" + eng)

    def sf(self, q, eng, ins):
        s = self.fence_sem(eng)
        s.n += 1
        ins.then_inc(s.h, 1)
        q.wait_ge(s.h, s.n)
        return ins

    def barrier_wait(self, eng, sem):
        def fn(q, sem=sem):
            q.wait_ge(sem.h, sem.n)
        self.streams[eng].append(fn)

    def flush(self, name=None):
        for snm, sem in self.sems.items():
            if sem.n > 0 and (snm.startswith("dm") or snm.startswith("gm")):
                eng = "gpsimd" if snm.startswith("gm") else "sync"
                self.barrier_wait(eng, sem)
        with self.nc.Block(name) as blk:
            for e in ENGINES:
                fns = self.streams[e]
                if fns:
                    getattr(blk, e)(lambda q, fns=tuple(fns): [f(q) for f in fns])
        self.streams = {e: [] for e in ENGINES}
        self.lastwait = {e: {} for e in ENGINES}


def ap3(ap, inner):
    return ap.rearrange("p (h d) -> p h d", d=inner)



def _host_prep(inputs):
    g = {}
    f32 = np.float32
    bf16 = ml_dtypes.bfloat16
    A = lambda a: np.ascontiguousarray(np.asarray(a), f32)

    pair_order = [0, 4, 1, 5, 2, 6, 3, 7, 8, 12, 9, 13, 10, 14, 11, 15]
    perm_q = []
    for s in range(16):
        base = pair_order[s] * 64
        perm_q += [base + 2 * j for j in range(32)]
        perm_q += [base + 2 * j + 1 for j in range(32)]
    perm_q = np.asarray(perm_q)
    wout_rows = np.concatenate([np.arange(64) + 64 * pair_order[s] for s in range(16)])

    g["wqa"] = A(inputs["L0_attn_Wqa"])
    g["wqb"] = A(np.asarray(inputs["L0_attn_Wqb"])[:, perm_q])
    g["wkva"] = A(inputs["L0_attn_Wkva"])
    wkvb = np.asarray(inputs["L0_attn_Wkvb"])
    cols_k, cols_v = [], []
    for kv in range(KVH):
        base = kv * 128
        cols_k += [base + 4 * j for j in range(32)]
        cols_k += [base + 4 * j + 2 for j in range(32)]
        cols_v += [base + 2 * d + 1 for d in range(64)]
    g["wkvb"] = A(wkvb[:, np.asarray(cols_k + cols_v)])
    g["wout"] = A(np.asarray(inputs["L0_attn_Wout"])[wout_rows, :])
    g["bout"] = A(np.asarray(inputs["L0_attn_bout"]).reshape(1, E))

    g["w1t"] = A(np.asarray(inputs["L0_ffn_w1_w"]).T)
    g["w2t"] = A(np.asarray(inputs["L0_ffn_w2_w"]).T)
    g["h1t"] = A(np.asarray(inputs["L0_ffn_h1_w"]).T)
    g["h2t"] = A(np.asarray(inputs["L0_ffn_h2_w"]).T)
    g["w3t"] = A(np.asarray(inputs["L0_ffn_w3_w"]).T)
    for nm, key in [("s1m", "L0_ffn_w1_s"), ("s2m", "L0_ffn_w2_s"),
                    ("sh1m", "L0_ffn_h1_s"), ("sh2m", "L0_ffn_h2_s")]:
        g[nm] = A(np.asarray(inputs[key]).reshape(-1, 128).T)
    g["s3r"] = A(np.asarray(inputs["L0_ffn_w3_s"]).reshape(1, E))

    g["wrt"] = A(np.asarray(inputs["L1_attn_W_r"]).T)
    g["wkt"] = A(np.asarray(inputs["L1_attn_W_k"]).T)
    g["wvt"] = A(np.asarray(inputs["L1_attn_W_v"]).T)
    g["wot"] = A(np.asarray(inputs["L1_attn_W_o"]).T)
    for nm, key in [("tmr", "L1_attn_time_mix_r"), ("tmk", "L1_attn_time_mix_k"),
                    ("tmv", "L1_attn_time_mix_v"), ("tdm", "L1_attn_time_decay"),
                    ("tfm", "L1_attn_time_first"), ("gmm", "L1_attn_ln_gamma"),
                    ("gbm", "L1_attn_ln_beta")]:
        g[nm] = A(np.asarray(inputs[key]).reshape(-1, 128).T)

    g["gw"] = A(inputs["L1_ffn_gate_w"])
    g["gb"] = A(np.asarray(inputs["L1_ffn_gate_b"]).reshape(1, NEXP))
    g["ewt"] = A(np.asarray(inputs["L1_ffn_expert_w"]).reshape(NEXP * E, E))
    g["eb"] = A(inputs["L1_ffn_expert_b"])

    g["eye128"] = np.eye(128, dtype=bf16)
    g["eye128f"] = np.eye(128, dtype=f32)
    inv_freq = (1.0 / (f32(10000.0) ** (np.arange(0, 64, 2, dtype=f32) / f32(64.0)))).astype(f32)
    freqs = np.outer(np.arange(T, dtype=f32), inv_freq).astype(f32)
    g["cosq"] = np.ascontiguousarray(np.tile(np.cos(freqs).astype(f32), (1, NHEAD)))
    g["sinq"] = np.ascontiguousarray(np.tile(np.sin(freqs).astype(f32), (1, NHEAD)))
    m = np.zeros((128, 4 * 1024), dtype=bf16)
    r = np.arange(128)[:, None]
    qf = np.arange(512)[None, :]
    for j in range(4):
        pat = (qf >= j * 128 + r).astype(bf16)
        m[:, 1024 * j:1024 * j + 512] = pat
        m[:, 1024 * j + 512:1024 * (j + 1)] = pat
    g["mask4"] = m
    osel = np.zeros((128, 256), dtype=bf16)
    for h in range(16):
        osel[:, 16 * h + h] = 1
    g["onesel"] = osel
    sbc = np.zeros((16, 8 * 128), dtype=f32)
    for p in range(8):
        sbc[2 * p, 128 * p:128 * p + 64] = 1
        sbc[2 * p + 1, 128 * p + 64:128 * p + 128] = 1
    g["selbc"] = sbc
    ob = np.zeros((128, 2), dtype=f32)
    ob[0:64, 0] = 1
    ob[64:128, 1] = 1
    g["onesbd2"] = ob
    s2 = np.zeros((2, 128), dtype=f32)
    s2[0, 0:64] = 1
    s2[1, 64:128] = 1
    g["sel2"] = s2
    se = np.zeros((8, 8 * 128), dtype=f32)
    for e in range(8):
        se[e, 128 * e:128 * (e + 1)] = 1
    g["selexp"] = se
    g["miscc"] = np.asarray([[1e-5, 0.0, 0.0, 0.0]], dtype=f32)
    return g



def build(dbg=(), blocks=(0, 1)):
    nc = bass.Bass("TRN2", target_bir_lowering=False, detect_race_conditions=False)
    I = lambda name, shape, dt=F32: nc.dram_tensor(name, shape, dt, kind="ExternalInput").ap()
    D = lambda name, shape, dt=F32: nc.dram_tensor(name, shape, dt).ap()

    t = {}
    t["x_in"] = I("x", [T, E])
    for name, shape in [("wqa", [E, QL]), ("wqb", [QL, E]), ("wkva", [E, KVL]),
                        ("wkvb", [KVL, 512]), ("wout", [E, E]), ("bout", [1, E]),
                        ("w1t", [E, HID]), ("w2t", [E, HID]), ("h1t", [HID, HID]),
                        ("h2t", [HID, HID]), ("w3t", [HID, E]),
                        ("s1m", [128, NH]), ("s2m", [128, NH]), ("sh1m", [128, NH]),
                        ("sh2m", [128, NH]), ("s3r", [1, E]),
                        ("wrt", [E, E]), ("wkt", [E, E]), ("wvt", [E, E]), ("wot", [E, E]),
                        ("tmr", [128, NE]), ("tmk", [128, NE]), ("tmv", [128, NE]),
                        ("tdm", [128, NE]), ("tfm", [128, NE]),
                        ("gmm", [128, NE]), ("gbm", [128, NE]),
                        ("gw", [E, NEXP]), ("gb", [1, NEXP]),
                        ("ewt", [NEXP * E, E]), ("eb", [NEXP, E]),
                        ("cosq", [T, 512]), ("sinq", [T, 512]),
                        ("selbc", [16, 8 * 128]), ("onesbd2", [128, 2]),
                        ("sel2", [2, 128]), ("selexp", [8, 8 * 128]),
                        ("miscc", [1, 4])]:
        t[name] = I(name, shape)
    for name, shape in [("eye128", [128, 128]), ("mask4", [128, 4 * 1024]),
                        ("onesel", [128, 256])]:
        t[name] = I(name, shape, BF16)
    t["eye128f"] = I("eye128f", [128, 128])

    t["out"] = nc.dram_tensor("out", [T, E], F32, kind="ExternalOutput").ap()
    t["w1b"] = D("w1b", [NH, 128 * E], BF16)
    t["w2b"] = D("w2b", [NH, 128 * E], BF16)
    t["h1b"] = D("h1b", [NH, 128 * HID], BF16)
    t["h2b"] = D("h2b", [NH, 128 * HID], BF16)
    t["w3b"] = D("w3b", [HID, E], BF16)
    t["xnt_hbm"] = D("xnt_hbm", [E, T], BF16)
    t["acc_hbm"] = D("acc_hbm", [T, E])
    t["x1_hbm"] = D("x1_hbm", [T, E])
    t["xnt1_hbm"] = D("xnt1_hbm", [E, T], BF16)
    t["rwkv_hbm"] = D("rwkv_hbm", [E, T], BF16)
    t["moe_hbm"] = D("moe_hbm", [E, T])
    t["ewq"] = D("ewq", [2 * NEXP, 128 * NE * 512], BF16)
    t["wn_hbm"] = D("wn_hbm", [NEXP, T], BF16)
    t["sc_hbm"] = D("sc_hbm", [8, 128])

    t["dbg"] = {}
    for name in dbg:
        shp, dt = {"dbg_xnt": ([E, T], BF16), "dbg_acc": ([T, E], F32),
                   "dbg_x1": ([T, E], F32), "dbg_xnt1": ([E, T], BF16),
                   "dbg_kk": ([E, T], F32), "dbg_rr": ([E, T], F32),
                   "dbg_vv": ([E, T], F32),
                   "dbg_wkv": ([E, T], F32), "dbg_rwkv": ([E, T], BF16),
                   "dbg_acc1": ([T, E], F32)}[name]
        t["dbg"][name] = nc.dram_tensor(name, shp, dt, kind="ExternalOutput").ap()

    with ExitStack() as ctx:
        P = Prog(nc, ctx)
        sb = lambda name, shape, dt=F32: ctx.enter_context(nc.sbuf_tensor(name, shape, dt))
        t["ident"] = sb("identsb", [128, 128], BF16)
        t["identf"] = sb("identfsb", [128, 128])
        t["negc"] = sb("negcsb", [128, 1])
        t["epsc"] = sb("epscsb", [128, 4])

        d0 = P.sem("dmP0")
        P.op("sync", lambda q: q.dma_start(t["ident"][:, :], t["eye128"]), inc=d0, inc_k=16)
        P.op("sync", lambda q: q.dma_start(t["identf"][:, :], t["eye128f"]), inc=d0, inc_k=16)
        P.op("sync", lambda q: q.dma_start(
            t["epsc"][:, :], t["miscc"][0:1, :].partition_broadcast(128)), inc=d0, inc_k=16)
        P.flush("p0")

        if 0 in blocks:
            _b0(nc, ctx, P, t)
        if 1 in blocks:
            _b1(nc, ctx, P, t)

        P.flush("fin")
    return nc



def _emit_wprecast(P, t):
    first = [t.get("qkv_gate")]
    for (dst, src, nf, j) in [("w1b", "w1t", NH, NE), ("w2b", "w2t", NH, NE),
                              ("h1b", "h1t", NH, NH), ("h2b", "h2t", NH, NH)]:
        pc = P.sem("pc_" + dst)
        srcv = t[src].rearrange("(j p) n -> p j n", p=128)
        for f in range(nf):
            P.op("gpsimd", lambda q, dst=dst, srcv=srcv, f=f, j=j: q.dma_start(
                t[dst][f:f + 1, :].rearrange("f (p j c) -> (f p) j c", p=128, j=j),
                srcv[:, :, 128 * f:128 * (f + 1)]), waits=first, inc=pc, inc_k=16)
            first = []
        t["pc_" + dst] = pc.tok()
    pc = P.sem("pc_w3b")
    w3v = t["w3t"].rearrange("(a p) n -> p a n", p=128)
    w3bv = t["w3b"].rearrange("(a p) n -> p a n", p=128)
    for a in range(8):
        P.op("gpsimd", lambda q, a=a: q.dma_start(
            w3bv[:, 4 * a:4 * (a + 1), :], w3v[:, 4 * a:4 * (a + 1), :]),
            inc=pc, inc_k=16)
    t["pc_w3b"] = pc.tok()
    pc = P.sem("pc_ewt")
    ewv4 = t["ewt"].rearrange("(e j p) n -> e p j n", p=128, j=NE)
    for e in range(NEXP):
        for rF in range(2):
            P.op("gpsimd", lambda q, e=e, rF=rF: q.dma_start(
                t["ewq"][2 * e + rF:2 * e + rF + 1, :].rearrange(
                    "o (p j c) -> (o p) j c", p=128, j=NE),
                ewv4[e:e + 1, :, :, 512 * rF:512 * (rF + 1)].rearrange(
                    "e p j n -> (e p) j n")), inc=pc, inc_k=16)
    t["pc_ewt"] = pc.tok()



def _emit_ln(nc, pctx_unused, P, t, src_hbm, xnt, dst_hbm, tag, dbgname):
    lctx = ExitStack()
    pctx = lctx
    sb = lambda name, shape, dt=F32: pctx.enter_context(nc.sbuf_tensor(name, shape, dt))
    ps = lambda name, shape, dt=F32: pctx.enter_context(nc.psum_tensor(name, shape, dt))
    xbuf = sb("xbuf" + tag, [128, 3 * E])
    xn_bf = sb("xn_bf" + tag, [128, 2 * E], BF16)
    sq_scr = sb("sq_scr" + tag, [128, E])
    stat = sb("stat" + tag, [128, 24])
    ptr = [ps(f"ptrLN{k}" + tag, [128, 128], BF16) for k in range(4)]

    dv, ac, pe, dv2, ac2 = (P.sem(n) for n in ["dv", "ac", "pe", "dv2", "ac2"])
    lds = [P.sem("dmLNx0" + tag), P.sem("dmLNx1" + tag), P.sem("dmLNx2" + tag)]
    stx = P.sem("dmLNs" + tag)
    apply_tok = [None] * NT
    sq_tok = [None] * NT
    tr_last = [None] * NT
    cp_tok = []
    for i in range(NT):
        b = i % 3
        b2 = i % 2
        buf = xbuf[:, b * E:(b + 1) * E]
        obf = xn_bf[:, b2 * E:(b2 + 1) * E]
        sc = 8 * b
        w = [apply_tok[i - 3], sq_tok[i - 3]] if i >= 3 else []
        tok_x = P.op("sync", lambda q, i=i, buf=buf: q.dma_start(
            buf, src_hbm[128 * i:128 * (i + 1), :]), waits=w, inc=lds[b], inc_k=16)
        if i == 24 and tag == "A":
            t["ln_gate"] = tok_x
        P.op("vector", lambda q, buf=buf, sc=sc: P.sf(q, "vector", q.reduce_sum(
            stat[:, sc:sc + 1], buf, AX.X)) and None, waits=[tok_x])
        sq_tok[i] = P.op("scalar", lambda q, buf=buf, sc=sc: q.activation(
            sq_scr[:, :], buf, AF.Square, accum_out=stat[:, sc + 1:sc + 2]),
            waits=[tok_x], inc=ac)

        def small(q, sc=sc):
            P.sf(q, "vector", q.tensor_scalar_mul(stat[:, sc + 2:sc + 3],
                                                  stat[:, sc:sc + 1], 1.0 / E))
            q.tensor_scalar_mul(stat[:, sc + 3:sc + 4], stat[:, sc + 2:sc + 3], -1.0)
            P.sf(q, "vector", q.tensor_mul(stat[:, sc + 4:sc + 5], stat[:, sc + 2:sc + 3],
                                           stat[:, sc + 2:sc + 3]))
            P.sf(q, "vector", q.tensor_scalar_mul(stat[:, sc + 5:sc + 6],
                                                  stat[:, sc + 1:sc + 2], 1.0 / E))
            return q.tensor_sub(stat[:, sc + 6:sc + 7], stat[:, sc + 5:sc + 6],
                                stat[:, sc + 4:sc + 5])
        tok_var = P.op("vector", small, waits=[sq_tok[i]], inc=dv)

        def rstd(q, sc=sc):
            P.sf(q, "scalar", q.activation(stat[:, sc + 7:sc + 8], stat[:, sc + 6:sc + 7],
                                           AF.Ln, bias=t["epsc"][:, 0:1]))
            return q.activation(stat[:, sc + 7:sc + 8], stat[:, sc + 7:sc + 8], AF.Exp, scale=-0.5)
        tok_rstd = P.op("scalar", rstd, waits=[tok_var], inc=ac)
        wapply = [tok_rstd] + ([tr_last[i - 2]] if i >= 2 else [])
        apply_tok[i] = P.op("vector", lambda q, buf=buf, obf=obf, sc=sc: q.tensor_scalar(
            obf, buf, stat[:, sc + 3:sc + 4], stat[:, sc + 7:sc + 8], ALU.add, ALU.mult),
            waits=wapply, inc=dv)
        for j in range(NE):
            k = (NE * i + j) % 4
            w = [apply_tok[i]] + ([cp_tok[-4]] if len(cp_tok) >= 4 else [])
            tok_tr = P.op("tensor", lambda q, obf=obf, j=j, k=k: q.transpose(
                ptr[k][:, :], obf[:, 128 * j:128 * (j + 1)], t["ident"][:, :]),
                waits=w, inc=pe)
            isv = (j % 2 == 0)

            def cpf(q, i=i, j=j, k=k, isv=isv):
                dst = xnt[j][:, 128 * i:128 * (i + 1)]
                return q.tensor_copy(dst, ptr[k][:, :]) if isv else q.copy(
                    dst, ptr[k][:, :])
            cp_tok.append(P.op("vector" if isv else "scalar", cpf,
                               waits=[tok_tr], inc=(dv2 if isv else ac2)))
        tr_last[i] = pe.tok()
    for j in range(NE):
        if dst_hbm is not None:
            P.op("sync", lambda q, j=j: q.dma_start(
                dst_hbm[128 * j:128 * (j + 1), :], xnt[j][:, :]),
                waits=[cp_tok[-1], cp_tok[-2]], inc=stx, inc_k=16)
    t["ln_cp_last" + tag] = [cp_tok[-1], cp_tok[-2]]
    P.flush("ln" + tag)
    lctx.close()



def _b0(nc, ctx, P, t):
    with ExitStack() as pctx:
        sb = lambda name, shape, dt=F32: pctx.enter_context(nc.sbuf_tensor(name, shape, dt))
        qlat = sb("qlat", [QL, T], BF16)
        kvlat = sb("kvlat", [KVL, T], BF16)
        t["wqb_sb_g"] = sb("wqb_sb_g", [QL, E], BF16)
        t["wkvb_sb_g"] = sb("wkvb_sb_g", [KVL, 512], BF16)
        with ExitStack() as lctx:
            xnt = [lctx.enter_context(nc.sbuf_tensor(f"xnt{j}", [128, T], BF16))
                   for j in range(NE)]
            _emit_ln(nc, None, P, t, t["x_in"], xnt, t["xnt_hbm"], "A", "dbg_xnt")
            _b0_latents(nc, lctx, P, t, xnt, qlat, kvlat)
        qts = [sb(f"qt{j}", [128, T], BF16) for j in range(NE)]
        ktd = sb("ktd", [128, 2 * T], BF16)
        vsb = sb("vsb", [128, NT * 256], BF16)
        wout_sb = sb("wout_sb", [128, NE * E], BF16)
        P.op("gpsimd", lambda q: q.dma_start(
            ap3(wout_sb[:, :], E), t["wout"].rearrange("(j p) c -> p j c", p=128)),
            inc=P.sem("gmWo"), inc_k=16)
        _b0_qkv_tiles(nc, pctx, P, t, qlat, kvlat, qts, ktd, vsb)
        _emit_wprecast(P, t)
        _b0_attn(nc, pctx, P, t, qts, ktd, vsb, wout_sb)
    _b0_ffn(nc, ctx, P, t)


def _b0_latents(nc, lctx, P, t, xnt, qlat, kvlat):
    sb = lambda name, shape, dt=F32: lctx.enter_context(nc.sbuf_tensor(name, shape, dt))
    ps = lambda name, shape, dt=F32: lctx.enter_context(nc.psum_tensor(name, shape, dt))
    wqa_sb = sb("wqa_sb", [128, NE * QL], BF16)
    wkva_sb = sb("wkva_sb", [128, NE * KVL], BF16)
    pql = [ps(f"pql{k}", [128, 512]) for k in range(2)]

    gw = P.sem("gmW")
    pe, ac = P.sem("pe"), P.sem("ac")

    P.op("gpsimd", lambda q: q.dma_start(
        ap3(wqa_sb[:, :], QL), t["wqa"].rearrange("(j p) c -> p j c", p=128)), inc=gw, inc_k=16)
    P.op("gpsimd", lambda q: q.dma_start(
        ap3(wkva_sb[:, :], KVL), t["wkva"].rearrange("(j p) c -> p j c", p=128)), inc=gw, inc_k=16)
    P.op("gpsimd", lambda q: q.dma_start(t["wqb_sb_g"][:, :], t["wqb"]), inc=gw, inc_k=16)
    P.op("gpsimd", lambda q: q.dma_start(t["wkvb_sb_g"][:, :], t["wkvb"]), inc=gw, inc_k=16)
    tok_wqa = tok_wkva = gw.tok()
    t["wqbkv_tok"] = gw.tok()

    ev_tok = [None, None]
    for which, (wtok, wsb, cw, dst, rows) in enumerate(
            [(tok_wqa, wqa_sb, QL, qlat, 64), (tok_wkva, wkva_sb, KVL, kvlat, 32)]):
        for n in range(8):
            k = n % 2
            w = [wtok, ev_tok[k]] + (t["ln_cp_lastA"] if n >= 7 else [])
            for j in range(NE):
                P.op("tensor", lambda q, n=n, j=j, k=k, wsb=wsb, cw=cw, rows=rows: q.matmul(
                    pql[k][0:rows, :], wsb[:, cw * j:cw * (j + 1)],
                    xnt[j][:, 512 * n:512 * (n + 1)], start=(j == 0), stop=(j == NE - 1)),
                    waits=(w if j == 0 else []), inc=(pe if j == NE - 1 else None))
            tok_mm = pe.tok()
            ev_tok[k] = P.op("scalar", lambda q, n=n, k=k, dst=dst, rows=rows: q.copy(
                dst[:, 512 * n:512 * (n + 1)], pql[k][0:rows, :]), waits=[tok_mm], inc=ac)
    P.flush("p2a1")


def _b0_qkv_tiles(nc, pctx, P, t, qlat, kvlat, qts, ktd, vsb):
    sb = lambda name, shape, dt=F32: pctx.enter_context(nc.sbuf_tensor(name, shape, dt))
    lctx = ExitStack()
    sbl = lambda name, shape, dt=F32: lctx.enter_context(nc.sbuf_tensor(name, shape, dt))
    psl = lambda name, shape, dt=F32: lctx.enter_context(nc.psum_tensor(name, shape, dt))
    wqb_sb = t["wqb_sb_g"]
    wkvb_sb = t["wkvb_sb_g"]
    qf = sbl("qf", [128, 2 * E])
    kvf = sbl("kvf", [128, 2 * 512])
    q_rope = sbl("q_rope", [128, 2 * E], BF16)
    k_rope = sbl("k_rope", [128, 2 * 256], BF16)
    cs_t = sbl("cs_t", [128, 2 * 512])
    sn_t = sbl("sn_t", [128, 2 * 512])
    rtmp = sbl("rtmp", [128, 2 * 512])
    sq_scr2 = sbl("sq_scr2", [128, E])
    nrm = sbl("nrm", [128, 8])
    smal = sbl("smal", [1, 160])
    pQ = psl("pQ", [128, 1024])
    pkv = psl("pkv", [128, 512])
    ptr = [psl(f"ptrQK{k}", [128, 128], BF16) for k in range(2)]

    pe, ac, dv, dv2 = (P.sem(n) for n in ["pe", "ac", "dv", "dv2"])
    dcs = [P.sem("dmCS0"), P.sem("dmCS1")]
    dsc, dscl, dnc = P.sem("dmSc"), P.sem("dmScL"), P.sem("dmNc")
    tok_wqb = t["wqbkv_tok"]
    tok_wkvb = t["wqbkv_tok"]

    P.op("vector", lambda q: q.memset(nrm[:, :], 0.0), inc=dv)

    cp_tok = []
    tile = [dict() for _ in range(NT)]
    for i in range(NT):
        b = i % 2
        ti = tile[i]
        prev = tile[i - 2] if i >= 2 else None
        w = [prev["rope_q"], prev["rope_k"]] if prev else []
        P.op("sync", lambda q, i=i, b=b: q.dma_start(
            cs_t[:, 512 * b:512 * (b + 1)], t["cosq"][128 * i:128 * (i + 1), :]),
            waits=w, inc=dcs[b], inc_k=16)
        ti["cs"] = P.op("sync", lambda q, i=i, b=b: q.dma_start(
            sn_t[:, 512 * b:512 * (b + 1)], t["sinq"][128 * i:128 * (i + 1), :]),
            inc=dcs[b], inc_k=16)
        if i == 8:
            t["qkv_gate"] = ti["cs"]
        w = [tok_wqb] + ([tile[i - 1]["qev"]] if i >= 1 else [])
        for h in range(2):
            P.op("tensor", lambda q, i=i, h=h: q.matmul(
                pQ[:, 512 * h:512 * (h + 1)], qlat[:, 128 * i:128 * (i + 1)],
                wqb_sb[:, 512 * h:512 * (h + 1)], start=True, stop=True),
                waits=(w if h == 0 else []), inc=(pe if h == 1 else None))
        tok_qmm = pe.tok()
        w = [tok_qmm] + ([prev["rope_q"], prev["qn"]] if prev else [])
        ti["qev"] = P.op("vector", lambda q, b=b: q.tensor_copy(
            qf[:, E * b:E * (b + 1)], pQ[:, :]), waits=w, inc=dv)
        ti["qn"] = P.op("scalar", lambda q, b=b: q.activation(
            sq_scr2[:, :], qf[:, E * b:E * (b + 1)], AF.Square, accum_out=nrm[:, 2:3]),
            waits=[ti["qev"]], inc=ac)
        P.op("vector", lambda q: q.tensor_max(nrm[:, 0:1], nrm[:, 0:1], nrm[:, 2:3]),
             waits=[ti["qn"]], inc=dv)

        def ropeq(q, b=b):
            qv = qf[:, E * b:E * (b + 1)]
            e = ap3(qv, 64)[:, :, 0:32]
            o = ap3(qv, 64)[:, :, 32:64]
            c = ap3(cs_t[:, 512 * b:512 * (b + 1)], 32)
            s = ap3(sn_t[:, 512 * b:512 * (b + 1)], 32)
            t1 = ap3(rtmp[:, 0:512], 32)
            t2 = ap3(rtmp[:, 512:1024], 32)
            ob = ap3(q_rope[:, E * b:E * (b + 1)], 64)
            q.tensor_mul(t1, e, c)
            q.tensor_mul(t2, o, s)
            q.tensor_sub(ob[:, :, 0:32], t1, t2)
            q.tensor_mul(t1, e, s)
            q.tensor_mul(t2, o, c)
            return q.tensor_add(ob[:, :, 32:64], t1, t2)
        w = [ti["cs"], ti["qev"]] + ([prev["trq_last"]] if prev else [])
        ti["rope_q"] = P.op("vector", ropeq, waits=w, inc=dv)

        w = [tok_wkvb] + ([tile[i - 1]["kvev"]] if i >= 1 else [])
        P.op("tensor", lambda q, i=i: q.matmul(
            pkv[:, :], kvlat[:, 128 * i:128 * (i + 1)], wkvb_sb[:, :], start=True, stop=True),
            waits=w, inc=pe)
        tok_kvmm = pe.tok()
        w = [tok_kvmm] + ([prev["rope_k"], prev["kn"], prev["vev"]] if prev else [])
        ti["kvev"] = P.op("vector", lambda q, b=b: q.tensor_copy(
            kvf[:, 512 * b:512 * (b + 1)], pkv[:, :]), waits=w, inc=dv)
        ti["kn"] = P.op("scalar", lambda q, b=b: q.activation(
            sq_scr2[:, 0:256], kvf[:, 512 * b:512 * b + 256], AF.Square,
            accum_out=nrm[:, 3:4]), waits=[ti["kvev"]], inc=ac)
        P.op("vector", lambda q: q.tensor_max(nrm[:, 1:2], nrm[:, 1:2], nrm[:, 3:4]),
             waits=[ti["kn"]], inc=dv)

        def ropek(q, b=b):
            kv = kvf[:, 512 * b:512 * b + 256]
            e = ap3(kv, 64)[:, :, 0:32]
            o = ap3(kv, 64)[:, :, 32:64]
            c = ap3(cs_t[:, 512 * b:512 * b + 128], 32)
            s = ap3(sn_t[:, 512 * b:512 * b + 128], 32)
            t1 = ap3(rtmp[:, 0:128], 32)
            t2 = ap3(rtmp[:, 128:256], 32)
            ob = ap3(k_rope[:, 256 * b:256 * (b + 1)], 64)
            q.tensor_mul(t1, e, c)
            q.tensor_mul(t2, o, s)
            q.tensor_sub(ob[:, :, 0:32], t1, t2)
            q.tensor_mul(t1, e, s)
            q.tensor_mul(t2, o, c)
            return q.tensor_add(ob[:, :, 32:64], t1, t2)
        w = [ti["cs"], ti["kvev"]] + ([prev["trk_last"]] if prev else [])
        ti["rope_k"] = P.op("vector", ropek, waits=w, inc=dv)
        ti["vev"] = P.op("vector", lambda q, i=i, b=b: q.tensor_copy(
            vsb[:, 256 * i:256 * (i + 1)], kvf[:, 512 * b + 256:512 * b + 512]),
            waits=[ti["kvev"]], inc=dv)

        for j in range(NE):
            k = len(cp_tok) % 2
            w = [ti["rope_q"]] + ([cp_tok[-2]] if len(cp_tok) >= 2 else [])
            tok_tr = P.op("tensor", lambda q, b=b, j=j, k=k: q.transpose(
                ptr[k][:, :], q_rope[:, E * b + 128 * j:E * b + 128 * (j + 1)],
                t["ident"][:, :]), waits=w, inc=pe)
            cp_tok.append(P.op("vector", lambda q, i=i, j=j, k=k: q.tensor_copy(
                qts[j][:, 128 * i:128 * (i + 1)], ptr[k][:, :]), waits=[tok_tr], inc=dv2))
        ti["trq_last"] = pe.tok()
        for jj in range(2):
            k = len(cp_tok) % 2
            w = [ti["rope_k"]] + ([cp_tok[-2]] if len(cp_tok) >= 2 else [])
            tok_tr = P.op("tensor", lambda q, b=b, jj=jj, k=k: q.transpose(
                ptr[k][:, :], k_rope[:, 256 * b + 128 * jj:256 * b + 128 * (jj + 1)],
                t["ident"][:, :]), waits=w, inc=pe)
            cp_tok.append(P.op("vector", lambda q, i=i, jj=jj, k=k: q.tensor_copy(
                ktd[:, T * jj + 128 * i:T * jj + 128 * (i + 1)], ptr[k][:, :]),
                waits=[tok_tr], inc=dv2))
        ti["trk_last"] = pe.tok()

    P.op("sync", lambda q: q.dma_start(
        t["sc_hbm"][0:1, :].rearrange("o n -> n o"), nrm[:, 0:1]),
        waits=[dv.tok()], inc=dsc, inc_k=16)
    tok_k2h = P.op("sync", lambda q: q.dma_start(
        t["sc_hbm"][1:2, :].rearrange("o n -> n o"), nrm[:, 1:2]),
        waits=[dv.tok()], inc=dsc, inc_k=16)
    P.op("sync", lambda q: q.dma_start(
        smal[0:1, 0:128], t["sc_hbm"][0:1, :]), waits=[tok_k2h], inc=dscl, inc_k=16)
    tok_b2 = P.op("sync", lambda q: q.dma_start(
        rtmp[0:1, 0:128], t["sc_hbm"][1:2, :]), waits=[tok_k2h], inc=dscl, inc_k=16)
    tok_b1 = tok_b2

    def cfin(q):
        P.sf(q, "vector", q.reduce_max(smal[0:1, 144:145], smal[0:1, 0:128], AX.X))
        P.sf(q, "vector", q.reduce_max(smal[0:1, 145:146], rtmp[0:1, 0:128], AX.X))
        return q.tensor_mul(smal[0:1, 146:147], smal[0:1, 144:145], smal[0:1, 145:146])
    tok_c1 = P.op("vector", cfin, waits=[tok_b1, tok_b2], inc=dv)

    def cfin2(q):
        P.sf(q, "scalar", q.activation(smal[0:1, 146:147], smal[0:1, 146:147], AF.Ln))
        return q.activation(smal[0:1, 146:147], smal[0:1, 146:147], AF.Exp, scale=0.5)
    tok_c2 = P.op("scalar", cfin2, waits=[tok_c1], inc=ac)

    def cfin3(q):
        P.sf(q, "vector", q.tensor_scalar_mul(smal[0:1, 147:148], smal[0:1, 146:147], 0.125))
        P.sf(q, "vector", q.tensor_scalar_min(smal[0:1, 147:148], smal[0:1, 147:148], 80.0))
        return q.tensor_scalar_mul(smal[0:1, 147:148], smal[0:1, 147:148], -1.0)
    tok_c3 = P.op("vector", cfin3, waits=[tok_c2], inc=dv)
    tok_c4 = P.op("sync", lambda q: q.dma_start(
        t["sc_hbm"][2:3, 0:1], smal[0:1, 147:148]), waits=[tok_c3], inc=dsc, inc_k=16)
    P.op("sync", lambda q: q.dma_start(
        t["negc"][:, :], t["sc_hbm"][2:3, 0:1].partition_broadcast(128)),
        waits=[tok_c4], inc=dnc, inc_k=16)
    P.flush("p2a2")
    lctx.close()


def _b0_attn(nc, pctx, P, t, qts, ktd, vsb, wout_sb):
    sb = lambda name, shape, dt=F32: pctx.enter_context(nc.sbuf_tensor(name, shape, dt))
    ps = lambda name, shape, dt=F32: pctx.enter_context(nc.psum_tensor(name, shape, dt))
    mask_sb = sb("mask_sb", [128, 4 * 1024], BF16)
    osel_sb = sb("osel_sb", [128, 256], BF16)
    sbc_sb = sb("sbc_sb", [16, 8 * 128])
    expb = sb("expb", [128, 4 * 512], BF16)
    outu = sb("outu", [128, 8 * 512], BF16)
    outn = sb("outn", [128, 8 * 512], BF16)
    rec16 = sb("rec16", [16, 512])
    recb = sb("recb", [128, 512])
    xres = sb("xres", [128, 2 * E])
    acct = sb("acct", [128, 2 * E])
    bout_rep = sb("bout_repL", [128, E])

    pst2 = [ps(f"pst2_{k}", [128, 1024]) for k in range(2)]
    pav0 = ps("pav0", [128, 512])
    pav1 = ps("pav1", [128, 512])
    pden = ps("pden", [128, 512])
    pbc = ps("pbc", [128, 512])

    pe, ac, dv, dv2, ac2 = (P.sem(n) for n in ["pe", "ac", "dv", "dv2", "ac2"])
    dm = P.sem("dmM")
    dxr = [P.sem("dmXr0"), P.sem("dmXr1")]
    dac = [P.sem("dmAc0"), P.sem("dmAc1")]

    P.op("sync", lambda q: q.dma_start(mask_sb[:, :], t["mask4"]), inc=dm, inc_k=16)
    P.op("sync", lambda q: q.dma_start(osel_sb[:, :], t["onesel"]), inc=dm, inc_k=16)
    P.op("sync", lambda q: q.dma_start(sbc_sb[:, :], t["selbc"]), inc=dm, inc_k=16)
    P.op("sync", lambda q: q.dma_start(
        bout_rep[:, :], t["bout"][0:1, :].partition_broadcast(128)), inc=dm, inc_k=16)
    dm_tok = dm.tok()

    xload_tok = [None, None]
    acc_store_tok = [None, None]
    norm_tail = None
    wout_tail = None
    rec_tail = None
    stbank_free = [None, None]
    expset_free = [None, None]

    for qt in range(8):
        nkc = 4 * (qt + 1)
        evac_tok = [None] * 8
        first_den = True
        last_exp = None
        for p in range(8):
            kv = p // 4

            def emit_st(kc, p=None, kvblk=None):
                ks = kc % 2
                wst = [stbank_free[ks], dm_tok]
                for h in range(2):
                    P.op("tensor", lambda q, p=p, kvblk=kvblk, kc=kc, h=h, ks=ks, qt=qt: q.matmul(
                        pst2[ks][:, 512 * h:512 * (h + 1)],
                        ktd[64 * h:64 * (h + 1),
                            T * kvblk + 128 * kc:T * kvblk + 128 * (kc + 1)],
                        qts[p][64 * h:64 * (h + 1), 512 * qt:512 * (qt + 1)],
                        start=True, stop=True, tile_position=(64 * h, 0)),
                        waits=(wst if h == 0 else []), inc=(pe if h == 1 else None))
                return pe.tok()

            st_tok = {0: emit_st(0, p, kv)}
            if nkc > 1:
                st_tok[1] = emit_st(1, p, kv)
            av_last = None
            for kc in range(nkc):
                diag_j = kc - 4 * qt
                es = kc % 2
                wexp = [st_tok[kc], expset_free[es]]

                def expf(q, kc=kc, es=es):
                    return q.activation(expb[:, 1024 * es:1024 * (es + 1)],
                                        pst2[kc % 2][:, :],
                                        AF.Exp, scale=0.125, bias=t["negc"][:, 0:1])
                tok_exp = P.op("scalar", expf, waits=wexp, inc=ac)
                stbank_free[kc % 2] = tok_exp
                if diag_j >= 0:
                    def maskf(q, diag_j=diag_j, es=es):
                        return q.tensor_mul(
                            expb[:, 1024 * es:1024 * (es + 1)],
                            expb[:, 1024 * es:1024 * (es + 1)],
                            mask_sb[:, 1024 * diag_j:1024 * (diag_j + 1)])
                    tok_exp = P.op("vector", maskf, waits=[tok_exp, dm_tok], inc=dv)
                last_exp = tok_exp
                if kc + 2 < nkc:
                    st_tok[kc + 2] = emit_st(kc + 2, p, kv)
                wav = [tok_exp]
                if kc == 0:
                    if p > 0:
                        wav.append(evac_tok[p - 1])
                    elif rec_tail is not None:
                        wav.append(rec_tail)

                def avf(q, p=p, kv=kv, kc=kc, nkc=nkc, fd=first_den, es=es):
                    e0 = expb[:, 1024 * es:1024 * es + 512]
                    e1 = expb[:, 1024 * es + 512:1024 * es + 1024]
                    q.matmul(pav0[0:64, :],
                             vsb[:, 256 * kc + 64 * (2 * kv):256 * kc + 64 * (2 * kv) + 64],
                             e0, start=(kc == 0), stop=(kc == nkc - 1), tile_position=(0, 0))
                    q.matmul(pav1[64:128, :],
                             vsb[:, 256 * kc + 64 * (2 * kv + 1):256 * kc + 64 * (2 * kv + 1) + 64],
                             e1, start=(kc == 0), stop=(kc == nkc - 1), tile_position=(0, 64))
                    q.matmul(pden[0:16, :], osel_sb[:, 16 * (2 * p):16 * (2 * p) + 16],
                             e0, start=fd, stop=False)
                    return q.matmul(pden[0:16, :],
                                    osel_sb[:, 16 * (2 * p + 1):16 * (2 * p + 1) + 16],
                                    e1, start=False, stop=(p == 7 and kc == nkc - 1))
                av_last = P.op("tensor", avf, waits=wav + [dm_tok], inc=pe)
                expset_free[es] = av_last
                first_den = False
            w = [av_last] + ([norm_tail] if norm_tail else [])
            def evacf(q, p=p):
                q.copy(outu[0:64, 512 * p:512 * (p + 1)], pav0[0:64, :])
                return q.copy(outu[64:128, 512 * p:512 * (p + 1)], pav1[64:128, :])
            evac_tok[p] = P.op("scalar", evacf, waits=w, inc=ac2)

        def recf(q):
            P.sf(q, "scalar", q.activation(rec16[:, :], pden[0:16, :], AF.Ln))
            return q.activation(rec16[:, :], rec16[:, :], AF.Exp, scale=-1.0)
        rec_tail = P.op("scalar", recf, waits=[av_last], inc=ac)
        norm_toks = []
        for p in range(8):
            w = [rec_tail] + ([norm_toks[-1]] if norm_toks else [])
            tok_bc = P.op("tensor", lambda q, p=p: q.matmul(
                pbc[:, :], sbc_sb[:, 128 * p:128 * (p + 1)], rec16[:, :],
                start=True, stop=True), waits=w, inc=pe)

            def normf(q, p=p):
                q.tensor_copy(recb[:, :], pbc[:, :])
                return q.tensor_mul(outn[:, 512 * p:512 * (p + 1)],
                                    outu[:, 512 * p:512 * (p + 1)], recb[:, :])
            norm_toks.append(P.op("vector", normf, waits=[tok_bc, evac_tok[p]], inc=dv))
        norm_tail = norm_toks[-1]

        wacc_tok = [None] * 4
        for s4 in range(4):
            i = 4 * qt + s4
            b = i % 2
            w = [acc_store_tok[b]] if acc_store_tok[b] else []
            xload_tok[b] = P.op("sync", lambda q, i=i, b=b: q.dma_start(
                xres[:, E * b:E * (b + 1)], t["x_in"][128 * i:128 * (i + 1), :]),
                waits=w, inc=dxr[b], inc_k=16)
            wmm = [norm_tail, last_exp, P.sems["gmWo"].tok()]
            if s4 >= 2:
                wmm.append(wacc_tok[s4 - 2])
            bs = s4 % 2
            for p in range(8):
                for h in range(2):
                    P.op("tensor", lambda q, p=p, h=h, s4=s4, bs=bs: q.matmul(
                        pst2[bs][:, 512 * h:512 * (h + 1)],
                        outn[:, 512 * p + 128 * s4:512 * p + 128 * (s4 + 1)],
                        wout_sb[:, E * p + 512 * h:E * p + 512 * (h + 1)],
                        start=(p == 0), stop=(p == 7)),
                        waits=(wmm if (p == 0 and h == 0) else []),
                        inc=(pe if (p == 7 and h == 1) else None))
            tok_wmm = pe.tok()
            wout_tail = tok_wmm

            def accf(q, b=b, bs=bs):
                q.tensor_add(acct[:, E * b:E * (b + 1)], pst2[bs][:, :], bout_rep[:, :])
                if False:
                    pass
                return q.tensor_add(acct[:, E * b:E * (b + 1)], acct[:, E * b:E * (b + 1)],
                                    xres[:, E * b:E * (b + 1)])
            wacc_tok[s4] = P.op("vector", accf, waits=[tok_wmm, xload_tok[b], dm_tok], inc=dv)
            acc_store_tok[b] = P.op("sync", lambda q, i=i, b=b: q.dma_start(
                t["acc_hbm"][128 * i:128 * (i + 1), :], acct[:, E * b:E * (b + 1)]),
                waits=[wacc_tok[s4]], inc=dac[b], inc_k=16)
            if "dbg_acc" in t["dbg"]:
                acc_store_tok[b] = P.op("sync", lambda q, i=i, b=b: q.dma_start(
                    t["dbg"]["dbg_acc"][128 * i:128 * (i + 1), :], acct[:, E * b:E * (b + 1)]),
                    waits=[wacc_tok[s4]], inc=dac[b], inc_k=16)
        stbank_free[0] = wacc_tok[2]
        stbank_free[1] = wacc_tok[3]
    P.flush("p2b")


def _b0_ffn(nc, ctx, P, t):
    with ExitStack() as pctx:
        sb = lambda name, shape, dt=F32: pctx.enter_context(nc.sbuf_tensor(name, shape, dt))
        ps = lambda name, shape, dt=F32: pctx.enter_context(nc.psum_tensor(name, shape, dt))
        TC = 1024
        NC = T // TC
        g1 = sb("g1", [128, NH * TC], BF16)
        g2 = sb("g2", [128, NH * TC], BF16)
        xct = sb("xct", [128, NE * TC], BF16)
        wt1 = [sb(f"wt1_{k}", [128, NE * 128], BF16) for k in range(2)]
        wt2 = [sb(f"wt2_{k}", [128, NE * 128], BF16) for k in range(2)]
        wth = [sb(f"wth_{k}", [128, NH * 128], BF16) for k in range(2)]
        wt3 = wth
        s1_sb = sb("s1_sb", [128, NH])
        s2_sb = sb("s2_sb", [128, NH])
        sh1_sb = sb("sh1_sb", [128, NH])
        sh2_sb = sb("sh2_sb", [128, NH])
        evtmp = sb("evtmp", [128, 2 * 512])
        t2tmp = sb("t2tmp", [128, 512])
        acc_t = sb("acc_t", [128, 2 * E])
        s3_rep = sb("s3_repL", [128, E])
        stat2 = sb("statB2", [128, 16])
        xnf = sb("xnfB", [128, 2 * E])
        stg = [sb("stgB0", [128, 8 * 512], BF16)]

        pp = [ps(f"pp{k}", [128, 1024]) for k in range(4)]

        pe, ac, dv, dv2, ac2 = (P.sem(n) for n in ["pe", "ac", "dv", "dv2", "ac2"])
        dm = P.sem("dmM")
        dxc = P.sem("dmXc")
        dw1 = [P.sem("gmw1_0"), P.sem("gmw1_1")]
        dw2 = [P.sem("gmw2_0"), P.sem("gmw2_1")]
        dwh = [P.sem("gmwh_0"), P.sem("gmwh_1")]
        wq = ["gpsimd", "gpsimd"]
        daq = [P.sem("dmAq0"), P.sem("dmAq1")]
        dx1 = [P.sem("dmX1_0"), P.sem("dmX1_1")]

        for nm, dst in [("s1m", s1_sb), ("s2m", s2_sb), ("sh1m", sh1_sb), ("sh2m", sh2_sb)]:
            P.op("sync", lambda q, nm=nm, dst=dst: q.dma_start(dst[:, :], t[nm]),
                 inc=dm, inc_k=16)
        P.op("sync", lambda q: q.dma_start(
            s3_rep[:, :], t["s3r"][0:1, :].partition_broadcast(128)), inc=dm, inc_k=16)
        dm_tok = dm.tok()

        ring_free = [None] * 8
        ring = [0]
        lnb_apply = [None, None]
        lnb_tr = [None, None]
        lnb_store = [None]
        dxn = P.sem("dmXn1")

        def slot_ap(s):
            return pp[s // 2][:, 512 * (s % 2):512 * (s % 2 + 1)]

        stageA_mm_tail = None
        acc_guard = [None, None]
        wcons1 = [None, None]
        wcons2 = [None, None]
        for c in range(NC):
            w = [stageA_mm_tail] if stageA_mm_tail else []
            xc_tok = P.op("sync", lambda q, c=c: q.dma_start(
                ap3(xct[:, :], TC),
                t["xnt_hbm"].rearrange("(j p) n -> p j n", p=128)[:, :, TC * c:TC * (c + 1)]),
                waits=w, inc=dxc, inc_k=16)

            evtmp_free = [None] * 2
            t2_free = [None] * 2
            evA_tail = None
            for f in range(NH):
                kb = f % 2
                wl1 = P.op(wq[kb], lambda q, f=f, kb=kb: q.dma_start(
                    ap3(wt1[kb][:, :], 128),
                    t["w1b"][f:f + 1, :].rearrange("f (p j c) -> (f p) j c", p=128, j=NE)),
                    waits=[wcons1[kb]] + ([t["pc_w1b"]] if (c == 0 and f < 2) else []),
                    inc=dw1[kb], inc_k=16)
                wl2 = P.op(wq[kb], lambda q, f=f, kb=kb: q.dma_start(
                    ap3(wt2[kb][:, :], 128),
                    t["w2b"][f:f + 1, :].rearrange("f (p j c) -> (f p) j c", p=128, j=NE)),
                    waits=[wcons2[kb]] + ([t["pc_w2b"]] if (c == 0 and f < 2) else []),
                    inc=dw2[kb], inc_k=16)
                slots = [(ring[0] + d) % 8 for d in range(4)]
                ring[0] += 4
                for half in range(2):
                    s = slots[half]
                    wmm = [xc_tok, wl1, ring_free[s]]
                    for j in range(NE):
                        P.op("tensor", lambda q, f=f, kb=kb, j=j, half=half, s=s: q.matmul(
                            slot_ap(s), wt1[kb][:, 128 * j:128 * (j + 1)],
                            xct[:, TC * j + 512 * half:TC * j + 512 * (half + 1)],
                            start=(j == 0), stop=(j == NE - 1)),
                            waits=(wmm if j == 0 else []), inc=(pe if j == NE - 1 else None))
                wcons1[kb] = pe.tok()
                for half in range(2):
                    s = slots[2 + half]
                    wmm = [wl2, ring_free[s]]
                    for j in range(NE):
                        P.op("tensor", lambda q, f=f, kb=kb, j=j, half=half, s=s: q.matmul(
                            slot_ap(s), wt2[kb][:, 128 * j:128 * (j + 1)],
                            xct[:, TC * j + 512 * half:TC * j + 512 * (half + 1)],
                            start=(j == 0), stop=(j == NE - 1)),
                            waits=(wmm if j == 0 else []), inc=(pe if j == NE - 1 else None))
                wcons2[kb] = pe.tok()
                tok_mm = pe.tok()
                for half in range(2):
                    s_w1 = slots[half]
                    s_w2 = slots[2 + half]
                    et = (2 * f + half) % 2
                    tt = 0
                    tok_silu = P.op("scalar", lambda q, f=f, s_w1=s_w1, et=et: q.activation(
                        evtmp[:, 512 * et:512 * (et + 1)], slot_ap(s_w1), AF.Silu,
                        scale=s1_sb[:, f:f + 1]),
                        waits=[tok_mm, evtmp_free[et], dm_tok], inc=ac)
                    ring_free[s_w1] = tok_silu

                    def evg(q, f=f, half=half, s_w2=s_w2, et=et):
                        q.tensor_scalar(t2tmp[:, :],
                                        slot_ap(s_w2), s2_sb[:, f:f + 1], None, ALU.mult)
                        return q.tensor_mul(
                            g1[:, TC * f + 512 * half:TC * f + 512 * (half + 1)],
                            evtmp[:, 512 * et:512 * (et + 1)],
                            t2tmp[:, :])
                    tok_g = P.op("vector", evg, waits=[tok_mm, tok_silu, dm_tok], inc=dv)
                    ring_free[s_w2] = tok_g
                    evtmp_free[et] = tok_g
                    evA_tail = tok_g
            stageA_mm_tail = pe.tok()

            def hidden_stage(src_g, dst_g, wname, ssb, prev_ev_tail):
                wconsh = [pe.tok(), pe.tok()]
                ev_tail = None
                for f in range(NH):
                    kb = f % 2
                    wlh = P.op(wq[kb], lambda q, f=f, kb=kb, wname=wname: q.dma_start(
                        ap3(wth[kb][:, :], 128),
                        t[wname][f:f + 1, :].rearrange("f (p j c) -> (f p) j c",
                                                       p=128, j=NH)),
                        waits=[wconsh[kb]] + ([t["pc_" + wname]] if f < 2 else []),
                        inc=dwh[kb], inc_k=16)
                    for half in range(2):
                        s = ring[0] % 8
                        ring[0] += 1
                        wmm = [wlh, ring_free[s], prev_ev_tail]
                        for j in range(NH):
                            P.op("tensor",
                                 lambda q, f=f, kb=kb, j=j, half=half, s=s, src_g=src_g: q.matmul(
                                     slot_ap(s), wth[kb][:, 128 * j:128 * (j + 1)],
                                     src_g[:, TC * j + 512 * half:TC * j + 512 * (half + 1)],
                                     start=(j == 0), stop=(j == NH - 1)),
                                 waits=(wmm if j == 0 else []),
                                 inc=(pe if j == NH - 1 else None))
                        tok_mm = pe.tok()
                        tok_ev = P.op("scalar",
                                      lambda q, f=f, half=half, s=s, ssb=ssb, dst_g=dst_g:
                                      q.activation(
                                          dst_g[:, TC * f + 512 * half:TC * f + 512 * (half + 1)],
                                          slot_ap(s), AF.Gelu, scale=ssb[:, f:f + 1]),
                                      waits=[tok_mm, dm_tok], inc=ac)
                        ring_free[s] = tok_ev
                        ev_tail = tok_ev
                    wconsh[kb] = pe.tok()
                return ev_tail

            evB_tail = hidden_stage(g1, g2, "h1b", sh1_sb, evA_tail)
            evC_tail = hidden_stage(g2, g1, "h2b", sh2_sb, evB_tail)

            for hp in range(2):
                wcons3 = [pe.tok(), pe.tok()]
                for jg in range(8):
                    kb = jg % 2
                    wl3 = P.op(wq[kb], lambda q, jg=jg, kb=kb: q.dma_start(
                        ap3(wt3[kb][:, :], E),
                        t["w3b"].rearrange("(a p) n -> p a n", p=128)[:, 4 * jg:4 * (jg + 1), :]),
                        waits=[wcons3[kb], t["pc_w3b"]], inc=dwh[kb], inc_k=16)
                    for j4 in range(4):
                        j = 4 * jg + j4
                        for ts_ in range(4):
                            wmm = []
                            if jg == 0 and j4 == 0:
                                wmm = [wl3, evC_tail, ring_free[2 * ts_],
                                       ring_free[2 * ts_ + 1]]
                            elif j4 == 0 and ts_ == 0:
                                wmm = [wl3]
                            for h in range(2):
                                P.op("tensor",
                                     lambda q, j=j, j4=j4, kb=kb, ts_=ts_, h=h, hp=hp: q.matmul(
                                         pp[ts_][:, 512 * h:512 * (h + 1)],
                                         g1[:, TC * j + 128 * (4 * hp + ts_):
                                            TC * j + 128 * (4 * hp + ts_ + 1)],
                                         wt3[kb][:, E * j4 + 512 * h:E * j4 + 512 * (h + 1)],
                                         start=(j == 0), stop=(j == NH - 1)),
                                     waits=wmm, inc=(pe if (j == NH - 1 and h == 1) else None))
                    wcons3[kb] = pe.tok()
                stageD_mm_tail = pe.tok()
                for ts_ in range(4):
                    i = (TC // 128) * c + 4 * hp + ts_
                    b = ts_ % 2
                    g = 2 * c + hp
                    kst = 0
                    sc = 8 * b
                    w = [acc_guard[b], lnb_apply[b]] if acc_guard[b] else []
                    tok_acc = P.op("sync", lambda q, i=i, b=b: q.dma_start(
                        acc_t[:, E * b:E * (b + 1)], t["acc_hbm"][128 * i:128 * (i + 1), :]),
                        waits=w, inc=daq[b], inc_k=16)

                    def evd(q, ts_=ts_, b=b):
                        q.tensor_mul(evtmp[:, :], pp[ts_][:, :], s3_rep[:, :])
                        return q.tensor_add(acc_t[:, E * b:E * (b + 1)],
                                            acc_t[:, E * b:E * (b + 1)],
                                            evtmp[:, :])
                    tok_ev = P.op("vector", evd, waits=[stageD_mm_tail, tok_acc, dm_tok], inc=dv)
                    tok_st = P.op("sync", lambda q, i=i, b=b: q.dma_start(
                        t["x1_hbm"][128 * i:128 * (i + 1), :], acc_t[:, E * b:E * (b + 1)]),
                        waits=[tok_ev], inc=dx1[b], inc_k=16)
                    acc_guard[b] = tok_st

                    P.op("vector", lambda q, b=b, sc=sc: P.sf(q, "vector", q.reduce_sum(
                        stat2[:, sc:sc + 1], acc_t[:, E * b:E * (b + 1)], AX.X)) and None,
                        waits=[tok_ev, lnb_apply[b]])
                    sq_tok = P.op("scalar", lambda q, b=b, sc=sc: q.activation(
                        xnf[:, E * b:E * (b + 1)], acc_t[:, E * b:E * (b + 1)], AF.Square,
                        accum_out=stat2[:, sc + 1:sc + 2]),
                        waits=[tok_ev, lnb_tr[b]], inc=ac)

                    def smallb(q, sc=sc):
                        P.sf(q, "vector", q.tensor_scalar_mul(stat2[:, sc + 2:sc + 3],
                                                              stat2[:, sc:sc + 1], 1.0 / E))
                        q.tensor_scalar_mul(stat2[:, sc + 3:sc + 4], stat2[:, sc + 2:sc + 3],
                                            -1.0)
                        P.sf(q, "vector", q.tensor_mul(stat2[:, sc + 4:sc + 5],
                                                       stat2[:, sc + 2:sc + 3],
                                                       stat2[:, sc + 2:sc + 3]))
                        P.sf(q, "vector", q.tensor_scalar_mul(stat2[:, sc + 5:sc + 6],
                                                              stat2[:, sc + 1:sc + 2], 1.0 / E))
                        return q.tensor_sub(stat2[:, sc + 6:sc + 7], stat2[:, sc + 5:sc + 6],
                                            stat2[:, sc + 4:sc + 5])
                    tok_var = P.op("vector", smallb, waits=[sq_tok], inc=dv)

                    def rstdb(q, sc=sc):
                        P.sf(q, "scalar", q.activation(stat2[:, sc + 7:sc + 8],
                                                       stat2[:, sc + 6:sc + 7],
                                                       AF.Ln, bias=t["epsc"][:, 0:1]))
                        return q.activation(stat2[:, sc + 7:sc + 8], stat2[:, sc + 7:sc + 8],
                                            AF.Exp, scale=-0.5)
                    tok_rstd = P.op("scalar", rstdb, waits=[tok_var], inc=ac)
                    lnb_apply[b] = P.op("vector", lambda q, b=b, sc=sc: q.tensor_scalar(
                        xnf[:, E * b:E * (b + 1)], acc_t[:, E * b:E * (b + 1)],
                        stat2[:, sc + 3:sc + 4], stat2[:, sc + 7:sc + 8], ALU.add, ALU.mult),
                        waits=[tok_rstd], inc=dv)
                    for k in range(NE):
                        P.op("tensor", lambda q, ts_=ts_, b=b, k=k: q.transpose(
                            pp[ts_][:, 128 * k:128 * (k + 1)],
                            xnf[:, E * b + 128 * k:E * b + 128 * (k + 1)],
                            t["identf"][:, :]),
                            waits=([lnb_apply[b]] if k == 0 else []),
                            inc=(pe if k == NE - 1 else None))
                    lnb_tr[b] = pe.tok()
                    ev_half = [None, None]
                    prev_v = None
                    for k in range(NE):
                        isv = (k % 2 == 0)

                        def evk(q, ts_=ts_, k=k, kst=kst, isv=isv):
                            dst = stg[kst][:, 512 * k + 128 * ts_:512 * k + 128 * (ts_ + 1)]
                            src = pp[ts_][:, 128 * k:128 * (k + 1)]
                            return q.tensor_copy(dst, src) if isv else q.copy(dst, src)
                        wv = [lnb_tr[b]]
                        if ts_ == 0 and k < 2:
                            wv.append(lnb_store[kst])
                        if not isv:
                            wv.append(prev_v)
                        tok = P.op("vector" if isv else "scalar", evk, waits=wv,
                                   inc=(dv if isv else ac))
                        if isv:
                            prev_v = tok
                        else:
                            ev_half[k // 4] = tok
                    ring_free[2 * ts_] = ev_half[0]
                    ring_free[2 * ts_ + 1] = ev_half[1]
                    if ts_ == 3:
                        lnb_store[kst] = P.op("sync", lambda q, g=g, kst=kst: q.dma_start(
                            t["xnt1_hbm"].rearrange("(j p) n -> p j n", p=128)
                            [:, :, 512 * g:512 * (g + 1)],
                            stg[kst][:, :].rearrange("p (j n) -> p j n", n=512)),
                            waits=[ev_half[0], ev_half[1]], inc=dxn, inc_k=16)
        P.flush("p3")



def _b1(nc, ctx, P, t):
    with ExitStack() as pctx:
        sb = lambda name, shape, dt=F32: pctx.enter_context(nc.sbuf_tensor(name, shape, dt))
        scale_gn = sb("scale_gn", [128, NE])
        bias_gn = sb("bias_gn", [128, NE])
        _b1_merged2(nc, P, t, scale_gn, bias_gn)
        _b1_wo(nc, P, t, scale_gn, bias_gn)


def _b1_merged2(nc, P, t, scale_gn, bias_gn):
    lctx = ExitStack()
    sb = lambda name, shape, dt=F32: lctx.enter_context(nc.sbuf_tensor(name, shape, dt))
    ps = lambda name, shape, dt=F32: lctx.enter_context(nc.psum_tensor(name, shape, dt))
    CH, NCH = 256, 16
    TCM, NCM = 512, 8
    CH1 = CH + 1

    pxp = [ps(f"pxpM{k}", [128, 512]) for k in range(4)]
    pr = [ps(f"prB{k}", [128, 256]) for k in range(3)]
    pmisc = ps("pmiscB", [128, 512])

    wrt_sb = sb("wrt_sb", [128, NE * E], BF16)
    wkt_sb = sb("wkt_sb", [128, NE * E], BF16)
    wvt_sb = sb("wvt_sb", [128, NE * E], BF16)
    small8 = sb("small8", [128, 96])
    ob2_sb = sb("ob2_sb", [128, 2])
    sel2_sb = sb("sel2_sb", [2, 128])
    ew_rep = sb("ew_rep", [128, NE * CH])
    xw = [sb(f"xwM{k}", [128, NE * CH1], BF16) for k in range(2)]
    Kbuf = sb("Kbuf", [128, NE * CH])
    vbuf = sb("vbufB", [128, NE * CH], BF16)
    sigr = sb("sigrB", [128, NE * CH], BF16)
    Bbuf = sb("Bbuf", [128, NE * CH1])
    Gbuf = sb("Gbuf", [128, NE * CH1])
    tmp1 = sb("tmp1B", [128, NE * CH])
    tmp2 = sb("tmp2B", [128, NE * CH])
    tmp3 = sb("tmp3B", [128, NE * CH])
    tmp4 = sb("tmp4B", [128, CH])
    tokd = sb("tokdB", [128, 1])
    xmixp = {p: sb(f"xmix_{p}", [128, NE * CH], BF16) for p in "rkv"}
    sums = sb("sumsB", [128, 32])
    redb = sb("redbB", [128, 16])
    gstat = sb("gstatB", [2, 32])

    mw = [sb(f"mwM{k}", [128, NE * TCM], BF16) for k in range(2)]
    ew_t = [sb(f"ewtM{k}", [128, NE * 512], BF16) for k in range(2)]
    scl = [sb(f"sclM{k}", [128, NE * 512], BF16) for k in range(3)]
    wball = [sb(f"wbM{k}", [128, TCM], BF16) for k in range(3)]
    mev = sb("mevM", [128, 4 * 512])
    gw_sb = sb("gwM", [128, NE * NEXP], BF16)
    gbr = sb("gbrM", [128, NEXP])
    eb_sb = sb("ebM", [NEXP, E], BF16)
    gg = sb("ggM", [128, 64])
    wn = sb("wnM", [128, NEXP], BF16)
    wnT = [sb(f"wnTM{k}", [NEXP, TCM], BF16) for k in range(2)]

    pe, ac, dv = (P.sem(n) for n in ["pe", "ac", "dv"])
    gp = P.sem("gpP")
    gw = P.sem("gmW")
    gmG = P.sem("gmG")
    dmb = P.sem("dmM")
    drw = P.sem("gmRw")
    dmw = [P.sem("dmMw0"), P.sem("dmMw1")]
    dwn = P.sem("dmWn")
    dwb = [P.sem("smWb0"), P.sem("smWb1"), P.sem("smWb2")]
    dmx = [P.sem("dmXw0"), P.sem("dmXw1")]
    dme = [P.sem(f"dmEw{k}") for k in range(2)]
    dmoe = P.sem("dmMoe")

    xnv = t["xnt1_hbm"].rearrange("(j p) n -> p j n", p=128)
    TD, TF, TMR, TMK, TMV, GM, GB = 0, 8, 16, 24, 32, 40, 48
    EW, EU, OMR, OMK, OMV = 56, 64, 72, 80, 88

    shared = {"moe_tail": []}

    def gen_scan():
        for nm, dst in [("wrt", wrt_sb), ("wkt", wkt_sb), ("wvt", wvt_sb)]:
            P.op("gpsimd", lambda q, nm=nm, dst=dst: q.dma_start(
                ap3(dst[:, :], E), t[nm].rearrange("(j p) c -> p j c", p=128)),
                inc=gw, inc_k=16)
        gw_tok = gw.tok()
        for i, nm in enumerate(["tdm", "tfm", "tmr", "tmk", "tmv", "gmm", "gbm"]):
            P.op("sync", lambda q, nm=nm, i=i: q.dma_start(
                small8[:, 8 * i:8 * (i + 1)], t[nm]), inc=dmb, inc_k=16)
        P.op("sync", lambda q: q.dma_start(ob2_sb[:, :], t["onesbd2"]), inc=dmb, inc_k=16)
        P.op("sync", lambda q: q.dma_start(sel2_sb[:, :], t["sel2"]), inc=dmb, inc_k=16)
        dmb_tok = dmb.tok()
        OM = {TMR: OMR, TMK: OMK, TMV: OMV}

        def prep(q):
            P.sf(q, "scalar", q.activation(small8[:, EW:EW + 8], small8[:, TD:TD + 8],
                                           AF.Exp))
            P.sf(q, "scalar", q.activation(small8[:, EW:EW + 8], small8[:, EW:EW + 8],
                                           AF.Exp, scale=-1.0))
            return q.activation(small8[:, EU:EU + 8], small8[:, TF:TF + 8], AF.Exp)
        tok_prep = P.op("scalar", prep, waits=[dmb_tok], inc=ac)

        def prep2(q):
            for src_c, dst_c in [(TMR, OMR), (TMK, OMK), (TMV, OMV)]:
                P.sf(q, "vector", q.tensor_scalar(small8[:, dst_c:dst_c + 8],
                                                  small8[:, src_c:src_c + 8],
                                                  -1.0, 1.0, ALU.mult, ALU.add))
            q.memset(Bbuf[:, :], 0.0)
            q.memset(Gbuf[:, :], 0.0)
            q.memset(sums[:, :], 0.0)
            q.memset(ap3(xw[0][:, :], CH1)[:, :, 0:1], 0.0)
            P.sf(q, "vector", q.memset(tmp1[:, 0:CH], 1.0))
            last = None
            for j in range(NE):
                last = q.tensor_scalar(ew_rep[:, CH * j:CH * (j + 1)], tmp1[:, 0:CH],
                                       small8[:, EW + j:EW + j + 1], None, ALU.mult)
            return last
        tok_prep2 = P.op("vector", prep2, waits=[dmb_tok, tok_prep], inc=dv)
        yield

        B3 = lambda off: Bbuf[:, :].rearrange("p (j x) -> p j x", x=CH1)[:, :, off:off + CH]
        G3 = lambda off: Gbuf[:, :].rearrange("p (j x) -> p j x", x=CH1)[:, :, off:off + CH]
        f3 = lambda buf: ap3(buf[:, :], CH)

        ring_free = [None] * 3
        ring = [0]
        mmtail_p = {}
        prev_store = None
        tok_out = None
        mix_last = [None, None]
        def emit_xw(c):
            c0 = CH * c
            w = c % 2
            if c == 0:
                xw_tok = P.op("sync", lambda q: q.dma_start(
                    ap3(xw[0][:, :], CH1)[:, :, 1:CH1], xnv[:, :, 0:CH]),
                    waits=[], inc=dmx[0], inc_k=16)
            else:
                xw_tok = P.op("sync", lambda q, c0=c0, w=w: q.dma_start(
                    ap3(xw[w][:, :], CH1), xnv[:, :, c0 - 1:c0 + CH]),
                    waits=[mix_last[w]], inc=dmx[w], inc_k=16)
            return xw_tok, w

        def emit_mix(proj, xw_tok, w):
            tm_c = {"r": TMR, "k": TMK, "v": TMV}[proj]
            om_c = OM[tm_c]
            xmix = xmixp[proj]

            def mixf(q, w=w, tm_c=tm_c, om_c=om_c, xmix=xmix):
                last = None
                for j in range(NE):
                    cur = xw[w][:, CH1 * j + 1:CH1 * (j + 1)]
                    prv = xw[w][:, CH1 * j:CH1 * j + CH]
                    q.tensor_scalar(tmp4[:, :], cur,
                                    small8[:, tm_c + j:tm_c + j + 1], None, ALU.mult)
                    last = q.scalar_tensor_tensor(
                        xmix[:, CH * j:CH * (j + 1)], prv,
                        small8[:, om_c + j:om_c + j + 1],
                        tmp4[:, :], ALU.mult, ALU.add)
                return last
            tok = P.op("vector", mixf,
                       waits=[tok_prep2, xw_tok, mmtail_p.get(proj)], inc=dv)
            if proj == "v":
                mix_last[w] = tok
            return tok

        nxt = emit_xw(0)
        mixes = {p: emit_mix(p, *nxt) for p in "rkv"}
        for c in range(NCH):
            c0 = CH * c
            proj_ev = {}
            for proj, wsb, tm_c in [("r", wrt_sb, TMR), ("k", wkt_sb, TMK),
                                    ("v", wvt_sb, TMV)]:
                xmix = xmixp[proj]
                tok_mix = mixes[proj]
                for fj in range(NE):
                    s = ring[0] % 3
                    ring[0] += 1
                    wmm = [tok_mix, gw_tok, ring_free[s]]
                    for j in range(NE):
                        P.op("tensor", lambda q, fj=fj, j=j, wsb=wsb, s=s, xmix=xmix: q.matmul(
                            pr[s][:, :], wsb[:, E * j + 128 * fj:E * j + 128 * (fj + 1)],
                            xmix[:, CH * j:CH * (j + 1)], start=(j == 0),
                            stop=(j == NE - 1)),
                            waits=(wmm if j == 0 else []), inc=(pe if j == NE - 1 else None))
                    tok_mm = pe.tok()
                    if proj == "r":
                        ev = P.op("scalar", lambda q, fj=fj, s=s: q.activation(
                            sigr[:, CH * fj:CH * (fj + 1)], pr[s][:, :], AF.Sigmoid),
                            waits=[tok_mm] + ([prev_store] if prev_store else []), inc=ac)
                    elif proj == "k":
                        ev = P.op("scalar", lambda q, fj=fj, s=s: q.activation(
                            Kbuf[:, CH * fj:CH * (fj + 1)], pr[s][:, :], AF.Exp),
                            waits=[tok_mm] + ([prev_store] if prev_store else []), inc=ac)
                    else:
                        ev = P.op("scalar", lambda q, fj=fj, s=s: q.copy(
                            vbuf[:, CH * fj:CH * (fj + 1)], pr[s][:, :]),
                            waits=[tok_mm], inc=ac)
                    ring_free[s] = ev
                mmtail_p[proj] = pe.tok()
                proj_ev[proj] = ev
                yield

            nxt = emit_xw(c + 1) if c + 1 < NCH else None
            nmix = {}

            def bscan(q):
                last = None
                for j in range(NE):
                    last = q.tensor_tensor_scan(
                        Bbuf[:, CH1 * j + 1:CH1 * (j + 1)],
                        ew_rep[:, CH * j:CH * (j + 1)], Kbuf[:, CH * j:CH * (j + 1)],
                        Bbuf[:, CH1 * j:CH1 * j + 1], ALU.mult, ALU.add)
                return last
            tok_b = P.op("vector", bscan, waits=[proj_ev["k"], tok_prep2], inc=dv)

            def mxf(q):
                last = None
                for j in range(NE):
                    last = q.scalar_tensor_tensor(
                        tmp1[:, CH * j:CH * (j + 1)],
                        Bbuf[:, CH1 * j:CH1 * j + CH],
                        small8[:, EW + j:EW + j + 1],
                        Kbuf[:, CH * j:CH * (j + 1)], ALU.mult, ALU.max)
                return last
            tok_mx = P.op("vector", mxf, waits=[tok_b], inc=dv)
            if nxt:
                nmix["r"] = emit_mix("r", *nxt)

            def rmxf(q):
                P.sf(q, "scalar", q.activation(tmp2[:, :], tmp1[:, :], AF.Ln))
                return q.activation(tmp1[:, :], tmp2[:, :], AF.Exp, scale=-1.0)
            tok_rmx = P.op("scalar", rmxf, waits=[tok_mx], inc=ac)
            tok_rho = P.op("vector", lambda q: q.tensor_mul(f3(tmp2), B3(1), f3(tmp1)),
                           waits=[tok_rmx] + ([prev_store] if prev_store else []), inc=dv)

            def cdf(q):
                for j in range(NE):
                    q.tensor_scalar(tmp1[:, CH * j:CH * (j + 1)],
                                    tmp2[:, CH * j:CH * (j + 1)],
                                    small8[:, EW + j:EW + j + 1], None, ALU.mult)
                q.tensor_mul(tmp3[:, :], Kbuf[:, :], vbuf[:, :])
                return q.tensor_mul(tmp3[:, :], tmp2[:, :], tmp3[:, :])
            tok_cd = P.op("vector", cdf, waits=[tok_rho, proj_ev["v"]], inc=dv)

            def gscan(q):
                last = None
                for j in range(NE):
                    last = q.tensor_tensor_scan(
                        Gbuf[:, CH1 * j + 1:CH1 * (j + 1)],
                        tmp1[:, CH * j:CH * (j + 1)], tmp3[:, CH * j:CH * (j + 1)],
                        Gbuf[:, CH1 * j:CH1 * j + 1], ALU.mult, ALU.add)
                return last
            tok_g = P.op("vector", gscan, waits=[tok_cd], inc=dv)

            def kuf(q):
                for j in range(NE):
                    q.tensor_scalar(tmp2[:, CH * j:CH * (j + 1)],
                                    Kbuf[:, CH * j:CH * (j + 1)],
                                    small8[:, EU + j:EU + j + 1], None, ALU.mult)
                return q.tensor_add(f3(tmp3), B3(0), f3(tmp2))
            tok_kua = P.op("vector", kuf, waits=[tok_g], inc=dv)
            if nxt:
                nmix["k"] = emit_mix("k", *nxt)

            def rdenf(q):
                P.sf(q, "scalar", q.activation(tmp1[:, :], tmp3[:, :], AF.Ln))
                return q.activation(tmp3[:, :], tmp1[:, :], AF.Exp, scale=-1.0)
            tok_rden = P.op("scalar", rdenf, waits=[tok_kua], inc=ac)

            def outf(q):
                q.tensor_mul(tmp1[:, :], tmp2[:, :], vbuf[:, :])
                q.tensor_add(f3(tmp2), G3(0), f3(tmp1))
                q.tensor_mul(tmp1[:, :], tmp2[:, :], tmp3[:, :])
                q.tensor_mul(tmp2[:, :], tmp1[:, :], sigr[:, :])
                P.sf(q, "vector", q.tensor_copy(
                    Bbuf[:, :].rearrange("p (j x) -> p j x", x=CH1)[:, :, 0:1],
                    Bbuf[:, :].rearrange("p (j x) -> p j x", x=CH1)[:, :, CH:CH1]))
                P.sf(q, "vector", q.tensor_copy(
                    Gbuf[:, :].rearrange("p (j x) -> p j x", x=CH1)[:, :, 0:1],
                    Gbuf[:, :].rearrange("p (j x) -> p j x", x=CH1)[:, :, CH:CH1]))
                return q.memset(tokd[:, :], 0.0)
            wo = [tok_rden, proj_ev["r"]] + ([prev_store] if prev_store else [])
            tok_out = P.op("vector", outf, waits=wo, inc=dv)
            if nxt:
                nmix["v"] = emit_mix("v", *nxt)
                mixes = nmix

            tok_sq = P.op("scalar", lambda q: q.activation(tmp1[:, :], tmp2[:, :],
                                                           AF.Square),
                          waits=[tok_out], inc=ac)

            def statf(q, c=c):
                po, pn = 16 * (c % 2), 16 * ((c + 1) % 2)
                for j in range(NE):
                    P.sf(q, "vector", q.reduce_sum(redb[:, j:j + 1],
                                                   tmp2[:, CH * j:CH * (j + 1)], AX.X))
                    P.sf(q, "vector", q.reduce_sum(redb[:, NE + j:NE + j + 1],
                                                   tmp1[:, CH * j:CH * (j + 1)], AX.X))
                return q.tensor_add(sums[:, pn:pn + 16], sums[:, po:po + 16], redb[:, :])
            tok_stat = P.op("vector", statf, waits=[tok_sq], inc=dv)
            prev_store = P.op("gpsimd", lambda q, c0=c0: q.dma_start(
                t["rwkv_hbm"].rearrange("(j p) n -> p j n", p=128)[:, :, c0:c0 + CH],
                ap3(tmp2[:, :], CH)), waits=[tok_out], inc=drw, inc_k=16)
            t["rwkv_store_tok"] = prev_store
            if "dbg_wkv" in t["dbg"]:
                prev_store = P.op("sync", lambda q, c0=c0: q.dma_start(
                    t["dbg"]["dbg_wkv"].rearrange("(j p) n -> p j n", p=128)[:, :,
                                                                            c0:c0 + CH],
                    ap3(tmp2[:, :], CH)), waits=[tok_stat], inc=drw, inc_k=16)
            yield

        shared["tok_stat"] = tok_stat
        shared["dmb_tok"] = dmb_tok

    def gen_moe():
        P.op("gpsimd", lambda q: q.dma_start(
            ap3(gw_sb[:, :], NEXP), t["gw"].rearrange("(j p) c -> p j c", p=128)),
            inc=gmG, inc_k=16)
        P.op("gpsimd", lambda q: q.dma_start(eb_sb[:, :], t["eb"]), inc=gmG, inc_k=16)
        tok_tbl = gmG.tok()
        P.op("sync", lambda q: q.dma_start(
            gbr[:, :], t["gb"][0:1, :].partition_broadcast(128)), inc=dmb, inc_k=16)
        tok_gb = dmb.tok()

        widx = 0
        scl_toks = {}
        wl_toks = {}
        wb_toks = {}
        emm_tails = {}
        mw_guard = [None, None]
        wnT_guard = [None, None]
        wnT_done = [None, None]
        gst = [None, None]
        mevs = [mev]
        st_mev = [None]
        prev_ev = {}
        tok_bias = None
        gstate = {"g1": None, "g2": None, "g4": None, "wn_ev": None}

        def emit_gates_load(c):
            par = c % 2
            c0 = TCM * c
            mw_tok = P.op("sync", lambda q, c0=c0, par=par: q.dma_start(
                ap3(mw[par][:, :], TCM), xnv[:, :, c0:c0 + TCM]),
                waits=(mw_guard[par] or []), inc=dmw[par], inc_k=16)
            return mw_tok

        def emit_gates_i4(c, mw_tok, i4):
            par = c % 2
            c0 = TCM * c
            if True:
                wmm = [mw_tok, tok_tbl]
                if gstate_first[0]:
                    gstate_first[0] = False
                else:
                    wmm += [gstate["g1"], gstate["wn_ev"]]
                for j in range(NE):
                    P.op("tensor", lambda q, i4=i4, j=j, par=par: q.matmul(
                        pmisc[:, 0:NEXP],
                        mw[par][:, TCM * j + 128 * i4:TCM * j + 128 * (i4 + 1)],
                        gw_sb[:, NEXP * j:NEXP * (j + 1)],
                        start=(j == 0), stop=(j == NE - 1)),
                        waits=(wmm if j == 0 else []), inc=(pe if j == NE - 1 else None))
                tok_gmm = pe.tok()

                def gatef(q):
                    P.sf(q, "vector", q.tensor_add(gg[:, 0:8], pmisc[:, 0:NEXP],
                                                   gbr[:, :]))
                    P.sf(q, "vector", q.reduce_max(gg[:, 8:9], gg[:, 0:8], AX.X))
                    return q.tensor_scalar_mul(gg[:, 9:10], gg[:, 8:9], -1.0)
                gstate["g1"] = P.op("vector", gatef,
                                    waits=[tok_gmm, tok_gb] +
                                    ([gstate["g2"]] if gstate["g2"] else []), inc=dv)
                gstate["g2"] = P.op("scalar", lambda q: q.activation(
                    gg[:, 16:24], gg[:, 0:8], AF.Exp, bias=gg[:, 9:10],
                    accum_out=gg[:, 10:11]), waits=[gstate["g1"]], inc=ac)

                def gatef2(q):
                    P.sf(q, "vector", q.reciprocal(gg[:, 11:12], gg[:, 10:11]))
                    P.sf(q, "vector", q.tensor_scalar(gg[:, 24:32], gg[:, 16:24],
                                                      gg[:, 11:12], None, ALU.mult))
                    P.sf(q, "vector", q.tensor_scalar(gg[:, 32:40], gg[:, 24:32],
                                                      0.1, None, ALU.is_ge))
                    P.sf(q, "vector", q.tensor_mul(gg[:, 40:48], gg[:, 24:32],
                                                   gg[:, 32:40]))
                    P.sf(q, "vector", q.reduce_sum(gg[:, 12:13], gg[:, 40:48], AX.X))
                    P.sf(q, "vector", q.tensor_scalar(gg[:, 13:14], gg[:, 12:13],
                                                      0.0, None, ALU.is_equal))
                    P.sf(q, "vector", q.tensor_add(gg[:, 14:15], gg[:, 12:13],
                                                   gg[:, 13:14]))
                    P.sf(q, "vector", q.reciprocal(gg[:, 15:16], gg[:, 14:15]))
                    return q.tensor_scalar(wn[:, :], gg[:, 40:48], gg[:, 15:16],
                                           None, ALU.mult)
                gstate["g3"] = P.op("vector", gatef2,
                              waits=[gstate["g2"]] +
                              ([gstate["g4"]] if gstate["g4"] else []), inc=dv)

        def emit_gates_b(c, i4):
            par = c % 2
            c0 = TCM * c
            if True:
                tok_g3 = gstate["g3"]
                gstate["g4"] = P.op("tensor", lambda q: q.matmul(
                    pmisc[0:8, 128:256], wn[:, :], t["ident"][:, :],
                    start=True, stop=True),
                    waits=[tok_g3] + ([gstate["wn_ev"]] if gstate["wn_ev"] else []),
                    inc=pe)
                wev_w = [gstate["g4"]]
                if i4 == 0 and wnT_guard[par] is not None:
                    wev_w.append(wnT_guard[par])
                gstate["wn_ev"] = P.op("scalar", lambda q, i4=i4, par=par: q.copy(
                    wnT[par][:, 128 * i4:128 * (i4 + 1)], pmisc[0:8, 128:256]),
                    waits=wev_w, inc=ac)
            if i4 == 3:
                wnT_done[par] = gstate["wn_ev"]
                gst[par] = P.op("sync", lambda q, c0=c0, par=par: q.dma_start(
                    t["wn_hbm"][:, c0:c0 + TCM], wnT[par][:, :]),
                    waits=[gstate["wn_ev"]], inc=dwn, inc_k=16)

        gstate_first = [True]
        mwg = emit_gates_load(0)
        for _i4 in range(4):
            emit_gates_i4(0, mwg, _i4)
            emit_gates_b(0, _i4)
        yield

        for c in range(NCM):
            c0 = TCM * c
            par = c % 2
            quanta = [(rF, e) for rF in range(2) for e in range(NEXP)]

            def prepq(qi, c0=c0, par=par, quanta=quanta, widx=None):
                rF, e = quanta[qi]
                g = wbase + qi
                s = g % 2
                r = g % 3
                wl_toks[g] = P.op("sync", lambda q, e=e, rF=rF, s=s: q.dma_start(
                    ap3(ew_t[s][:, :], 512),
                    t["ewq"][2 * e + rF:2 * e + rF + 1, :].rearrange(
                        "o (p j c) -> (o p) j c", p=128, j=NE)),
                    waits=[emm_tails.get(g - 2)] + ([t["pc_ewt"]] if g < 2 else []),
                    inc=dme[s], inc_k=16)
                pw = scl_toks.get(g - 3)
                wb_toks[g] = P.op("scalar", lambda q, e=e, r=r, c0=c0: q.dma_start(
                    wball[r][:, :],
                    t["wn_hbm"][e:e + 1, c0:c0 + TCM].partition_broadcast(128)),
                    waits=[gst[par]] + (pw if pw else []), inc=dwb[r], inc_k=16)

                def sclh(q, r=r, par=par, js=None):
                    last = None
                    for j in js:
                        last = q.tensor_mul(scl[r][:, 512 * j:512 * (j + 1)],
                                            mw[par][:, TCM * j:TCM * (j + 1)],
                                            wball[r][:, :])
                    return last
                wsc = [wb_toks[g], emm_tails.get(g - 3)]
                ta = P.op("gpsimd", lambda q, r=r, par=par: sclh(q, r, par, (0, 1, 2, 3)),
                          waits=wsc, inc=gp)
                tb = P.op("gpsimd", lambda q, r=r, par=par: sclh(q, r, par, (4, 5, 6, 7)),
                          waits=[], inc=gp)
                scl_toks[g] = [ta, tb]

            wbase = widx
            prepq(0)
            for qi, (rF, e) in enumerate(quanta):
                if qi + 1 < len(quanta):
                    prepq(qi + 1)
                g = wbase + qi
                s = g % 2
                r = g % 3
                for Fp in range(4):
                    wmm = [wl_toks[g]] + scl_toks[g]
                    if e == 0:
                        wmm.append(prev_ev.get(Fp))
                    for j in range(NE):
                        P.op("tensor", lambda q, Fp=Fp, j=j, e=e, s=s, r=r: q.matmul(
                            pxp[Fp][:, :],
                            ew_t[s][:, 512 * j + 128 * Fp:512 * j + 128 * (Fp + 1)],
                            scl[r][:, 512 * j:512 * (j + 1)],
                            start=(e == 0 and j == 0), stop=False),
                            waits=(wmm if j == 0 else []),
                            inc=(pe if j == NE - 1 else None))
                emm_tails[g] = pe.tok()
                yield
                if c + 1 < NCM and qi in (1, 4, 7, 10):
                    if qi == 1:
                        mwg = emit_gates_load(c + 1)
                    emit_gates_i4(c + 1, mwg, (qi - 1) // 3)
                    yield
                if c + 1 < NCM and qi in (3, 6, 9, 12):
                    emit_gates_b(c + 1, (qi - 3) // 3)
                if e < NEXP - 1:
                    continue
                for Fp in range(4):
                    P.op("tensor", lambda q, Fp=Fp, rF=rF, par=par: q.matmul(
                        pxp[Fp][:, :], eb_sb[:, 128 * (4 * rF + Fp):128 * (4 * rF + Fp + 1)],
                        wnT[par][:, :], start=False, stop=True),
                        waits=([tok_tbl, wnT_done[par]] if Fp == 0 else []),
                        inc=(pe if Fp == 3 else None))
                tok_bias = pe.tok()
                gpar = 0
                mv = mevs[gpar]
                for Fp in range(4):
                    def evm(q, Fp=Fp, mv=mv):
                        return q.copy(mv[:, 512 * Fp:512 * (Fp + 1)], pxp[Fp][:, :])
                    prev_ev[Fp] = P.op("scalar", evm,
                                       waits=[tok_bias, st_mev[gpar]], inc=ac)
                st_mev[gpar] = P.op("sync", lambda q, c0=c0, rF=rF, mv=mv: q.dma_start(
                    t["moe_hbm"].rearrange("(f p) n -> p f n", p=128)
                    [:, 4 * rF:4 * (rF + 1), c0:c0 + TCM],
                    mv[:, :].rearrange("p (f n) -> p f n", n=512)),
                    waits=[prev_ev[2], prev_ev[3]], inc=dmoe, inc_k=16)
                yield
            widx += len(quanta)
            wnT_guard[par] = tok_bias
            mw_guard[par] = scl_toks[widx - 1]
        shared["moe_tail"] = [st_mev[0],
                              gstate["g1"], gstate["wn_ev"]]

    gs = gen_scan()
    gm = gen_moe()
    next(gs)
    next(gm)
    gs_alive = gm_alive = True
    phase = 0
    while gs_alive or gm_alive:
        if gs_alive:
            try:
                next(gs)
            except StopIteration:
                gs_alive = False
        nmoe = 3 if (phase % 2 == 0) else 2
        for _ in range(nmoe):
            if gm_alive:
                try:
                    next(gm)
                except StopIteration:
                    gm_alive = False
        phase += 1

    fin = 16 * (NCH % 2)

    def statmm(q, fin=fin):
        return q.matmul(pmisc[0:2, 0:16], ob2_sb[:, :], sums[:, fin:fin + 16],
                        start=True, stop=True)
    tok_smm = P.op("tensor", statmm,
                   waits=[shared["tok_stat"], shared["dmb_tok"]] + shared["moe_tail"], inc=pe)
    NTOT = float(64 * T)

    def gnf(q):
        P.sf(q, "vector", q.tensor_copy(gstat[:, 0:16], pmisc[0:2, 0:16]))
        P.sf(q, "vector", q.tensor_scalar_mul(gstat[:, 16:24], gstat[:, 0:8],
                                              1.0 / NTOT))
        P.sf(q, "vector", q.tensor_scalar_mul(gstat[:, 24:32], gstat[:, 8:16],
                                              1.0 / NTOT))
        P.sf(q, "vector", q.tensor_mul(gstat[:, 0:8], gstat[:, 16:24],
                                       gstat[:, 16:24]))
        return q.tensor_sub(gstat[:, 8:16], gstat[:, 24:32], gstat[:, 0:8])
    tok_gn = P.op("vector", gnf, waits=[tok_smm], inc=dv)

    def gnf2(q):
        P.sf(q, "scalar", q.activation(gstat[:, 24:32], gstat[:, 8:16], AF.Ln,
                                       bias=t["epsc"][0:2, 0:1]))
        return q.activation(gstat[:, 24:32], gstat[:, 24:32], AF.Exp, scale=-0.5)
    tok_gn2 = P.op("scalar", gnf2, waits=[tok_gn], inc=ac)

    def bcmm(q):
        q.matmul(pmisc[:, 16:24], sel2_sb[:, :], gstat[:, 16:24],
                 start=True, stop=True)
        return q.matmul(pmisc[:, 24:32], sel2_sb[:, :], gstat[:, 24:32],
                        start=True, stop=True)
    tok_bcm = P.op("tensor", bcmm, waits=[tok_gn2], inc=pe)

    def sbf(q):
        P.sf(q, "vector", q.tensor_copy(redb[:, 0:8], pmisc[:, 16:24]))
        P.sf(q, "vector", q.tensor_copy(redb[:, 8:16], pmisc[:, 24:32]))
        P.sf(q, "vector", q.tensor_mul(scale_gn[:, :], redb[:, 8:16],
                                       small8[:, GM:GM + 8]))
        P.sf(q, "vector", q.tensor_mul(redb[:, 0:8], redb[:, 0:8], scale_gn[:, :]))
        return q.tensor_sub(bias_gn[:, :], small8[:, GB:GB + 8], redb[:, 0:8])
    P.op("vector", sbf, waits=[tok_bcm], inc=dv)


    P.flush("p45")
    lctx.close()


def _b1_wo(nc, P, t, scale_gn, bias_gn):
    lctx = ExitStack()
    sb = lambda name, shape, dt=F32: lctx.enter_context(nc.sbuf_tensor(name, shape, dt))
    ps = lambda name, shape, dt=F32: lctx.enter_context(nc.psum_tensor(name, shape, dt))
    wot_sb = sb("wotW", [128, NE * E], BF16)
    rwt = sb("rwtW", [128, NE * 512], BF16)
    nrm_b = sb("nrmW", [128, NE * 512], BF16)
    wo32 = sb("wo32W", [128, 512])
    moe_t = [sb(f"moeW{k}", [128, 512]) for k in range(2)]
    x1t = sb("x1tW", [128, 4 * E])
    aA = sb("aAW", [128, 4 * E])
    pwo = [ps(f"pwoW{k}", [128, 512]) for k in range(4)]

    pe, ac, dv, dv2 = (P.sem(n) for n in ["pe", "ac", "dv", "dv2"])
    gmw = P.sem("gmW")
    drl = P.sem("dmRl")
    dx1 = P.sem("dmXr0")
    dot = P.sem("dmAc0")

    P.op("gpsimd", lambda q: q.dma_start(
        ap3(wot_sb[:, :], E), t["wot"].rearrange("(j p) c -> p j c", p=128)),
        inc=gmw, inc_k=16)
    tok_w = gmw.tok()

    x1_guard = None
    o_guard = None
    rl_guard = None
    nrm_guard = None
    for n in range(8):
        c0 = 512 * n
        w = ([rl_guard] if rl_guard else []) + \
            ([t["rwkv_store_tok"]] if n == 0 else [])
        tok_rl = P.op("sync", lambda q, c0=c0: q.dma_start(
            ap3(rwt[:, :], 512),
            t["rwkv_hbm"].rearrange("(j p) n -> p j n", p=128)[:, :, c0:c0 + 512]),
            waits=w, inc=drl, inc_k=16)

        def nrmf(q):
            last = None
            for j in range(NE):
                last = q.tensor_scalar(nrm_b[:, 512 * j:512 * (j + 1)],
                                       rwt[:, 512 * j:512 * (j + 1)],
                                       scale_gn[:, j:j + 1], bias_gn[:, j:j + 1],
                                       ALU.mult, ALU.add)
            return last
        tok_nrm = P.op("vector", nrmf, waits=[tok_rl] + ([nrm_guard] if nrm_guard else []),
                       inc=dv)
        rl_guard = tok_nrm

        w = [x1_guard] if x1_guard else []
        tok_a1 = P.op("sync", lambda q, n=n: q.dma_start(
            x1t[:, :].rearrange("p (s e) -> p s e", e=E),
            t["x1_hbm"][512 * n:512 * (n + 1), :].rearrange("(s p) e -> p s e", p=128)),
            waits=w, inc=dx1, inc_k=16)

        comb_tok = {}
        ev_tok = {}
        tr_tok = {}
        tok_mo = {}
        for F in range(NE):
            kbank = F % 2
            tok_mo[F] = P.op("sync", lambda q, n=n, F=F: q.dma_start(
                moe_t[F % 2][:, :],
                t["moe_hbm"][128 * F:128 * (F + 1), 512 * n:512 * (n + 1)]),
                waits=[ev_tok.get(F - 2)], inc=dx1, inc_k=16)
            wmm = [tok_nrm, tok_w]
            if F >= 1:
                wmm.append(ev_tok[F - 1])
            if F >= 2:
                wmm.append(ev_tok[F - 2])
            for j in range(NE):
                P.op("tensor", lambda q, F=F, j=j, kbank=kbank: q.matmul(
                    pwo[kbank][:, :], wot_sb[:, E * j + 128 * F:E * j + 128 * (F + 1)],
                    nrm_b[:, 512 * j:512 * (j + 1)], start=(j == 0), stop=(j == NE - 1)),
                    waits=(wmm if j == 0 else []), inc=(pe if j == NE - 1 else None))
            tok_mm = pe.tok()
            w = [tok_mm, tok_mo[F]] + ([tr_tok[F - 1]] if F >= 1 else [])
            ev_tok[F] = P.op("vector", lambda q, F=F, kbank=kbank: q.tensor_add(
                wo32[:, :], pwo[kbank][:, :], moe_t[F % 2][:, :]), waits=w, inc=dv)

            def trf(q, F=F):
                last = None
                for s in range(4):
                    last = q.transpose(pwo[2 + (F % 2)][:, 128 * s:128 * (s + 1)],
                                      wo32[:, 128 * s:128 * (s + 1)], t["identf"][:, :])
                return last
            w = [ev_tok[F]] + ([comb_tok[F - 2]] if F >= 2 else [])
            tr_tok[F] = P.op("tensor", trf, waits=w, inc=pe)

            def combF(q, F=F):
                last = None
                for s in range(4):
                    last = q.tensor_add(aA[:, E * s + 128 * F:E * s + 128 * (F + 1)],
                                        x1t[:, E * s + 128 * F:E * s + 128 * (F + 1)],
                                        pwo[2 + (F % 2)][:, 128 * s:128 * (s + 1)])
                return last
            w = [tr_tok[F], tok_a1] + ([o_guard] if (F == 0 and o_guard) else [])
            comb_tok[F] = P.op("vector", combF, waits=w, inc=dv)
        x1_guard = comb_tok[NE - 1]
        o_guard = P.op("sync", lambda q, n=n: q.dma_start(
            t["out"][512 * n:512 * (n + 1), :].rearrange("(s p) e -> p s e", p=128),
            aA[:, :].rearrange("p (s e) -> p s e", e=E)),
            waits=[comb_tok[NE - 1]], inc=dot, inc_k=16)
        nrm_guard = tok_mm
    P.flush("p6")
    lctx.close()



def kernel(**inputs):
    prep = _host_prep(inputs)
    x = np.ascontiguousarray(np.asarray(inputs["L0_attn_x"], np.float32))
    nc = build()
    in_maps = []
    for c in range(NCORE):
        m = dict(prep)
        m["x"] = x[c]
        in_maps.append(m)
    res = run_bass_kernel_spmd(nc, in_maps, core_ids=list(range(NCORE)))
    out = np.stack([np.asarray(res.results[c]["out"], np.float32) for c in range(NCORE)], axis=0)
    return out


# version 30
# speedup vs baseline: 1.0344x; 1.0159x over previous
import numpy as np
import ml_dtypes
import concourse.bass as bass
import concourse.mybir as mybir
from concourse.bass_utils import run_bass_kernel_spmd
from contextlib import ExitStack

F32 = mybir.dt.float32
BF16 = mybir.dt.bfloat16
AF = mybir.ActivationFunctionType
ALU = mybir.AluOpType
AX = mybir.AxisListType

T = 4096
E = 1024
NT = T // 128
NE = E // 128
HID = 4096
NH = HID // 128
QL, KVL = 64, 32
NHEAD, KVH = 16, 4
NEXP = 8
LN_EPS = 1e-5
NCORE = 8

ENGINES = ("sync", "gpsimd", "tensor", "vector", "scalar")


class SemCtr:
    __slots__ = ("h", "n")

    def __init__(self, h):
        self.h = h
        self.n = 0

    def tok(self):
        return (self.h, self.n)


class Prog:
    def __init__(self, nc, ctx):
        self.nc = nc
        self.ctx = ctx
        self.streams = {e: [] for e in ENGINES}
        self.sems = {}
        self.lastwait = {e: {} for e in ENGINES}

    def sem(self, name):
        if name not in self.sems:
            h = self.ctx.enter_context(self.nc.semaphore(name))
            self.sems[name] = SemCtr(h)
        return self.sems[name]

    def op(self, eng, build, waits=(), inc=None, inc_k=1):
        real = []
        lw = self.lastwait[eng]
        for wt in waits:
            if wt is None:
                continue
            h, v = wt
            if v <= 0 or lw.get(h, -1) >= v:
                continue
            lw[h] = v
            real.append((h, v))
        if inc is not None:
            inc.n += inc_k

        def fn(q, build=build, rw=tuple(real), inc=inc, k=inc_k):
            for (h, v) in rw:
                q.wait_ge(h, v)
            ins = build(q)
            if inc is not None:
                ins.then_inc(inc.h, k)
        self.streams[eng].append(fn)
        return inc.tok() if inc is not None else None

    def fence_sem(self, eng):
        return self.sem("fence_" + eng)

    def sf(self, q, eng, ins):
        s = self.fence_sem(eng)
        s.n += 1
        ins.then_inc(s.h, 1)
        q.wait_ge(s.h, s.n)
        return ins

    def barrier_wait(self, eng, sem):
        def fn(q, sem=sem):
            q.wait_ge(sem.h, sem.n)
        self.streams[eng].append(fn)

    def flush(self, name=None):
        for snm, sem in self.sems.items():
            if sem.n > 0 and (snm.startswith("dm") or snm.startswith("gm")):
                eng = "gpsimd" if snm.startswith("gm") else "sync"
                self.barrier_wait(eng, sem)
        with self.nc.Block(name) as blk:
            for e in ENGINES:
                fns = self.streams[e]
                if fns:
                    getattr(blk, e)(lambda q, fns=tuple(fns): [f(q) for f in fns])
        self.streams = {e: [] for e in ENGINES}
        self.lastwait = {e: {} for e in ENGINES}


def ap3(ap, inner):
    return ap.rearrange("p (h d) -> p h d", d=inner)



def _host_prep(inputs):
    g = {}
    f32 = np.float32
    bf16 = ml_dtypes.bfloat16
    A = lambda a: np.ascontiguousarray(np.asarray(a), f32)

    pair_order = [0, 4, 1, 5, 2, 6, 3, 7, 8, 12, 9, 13, 10, 14, 11, 15]
    perm_q = []
    for s in range(16):
        base = pair_order[s] * 64
        perm_q += [base + 2 * j for j in range(32)]
        perm_q += [base + 2 * j + 1 for j in range(32)]
    perm_q = np.asarray(perm_q)
    wout_rows = np.concatenate([np.arange(64) + 64 * pair_order[s] for s in range(16)])

    g["wqa"] = A(inputs["L0_attn_Wqa"])
    g["wqb"] = A(np.asarray(inputs["L0_attn_Wqb"])[:, perm_q])
    g["wkva"] = A(inputs["L0_attn_Wkva"])
    wkvb = np.asarray(inputs["L0_attn_Wkvb"])
    cols_k, cols_v = [], []
    for kv in range(KVH):
        base = kv * 128
        cols_k += [base + 4 * j for j in range(32)]
        cols_k += [base + 4 * j + 2 for j in range(32)]
        cols_v += [base + 2 * d + 1 for d in range(64)]
    g["wkvb"] = A(wkvb[:, np.asarray(cols_k + cols_v)])
    g["wout"] = A(np.asarray(inputs["L0_attn_Wout"])[wout_rows, :])
    g["bout"] = A(np.asarray(inputs["L0_attn_bout"]).reshape(1, E))

    g["w1t"] = A(np.asarray(inputs["L0_ffn_w1_w"]).T)
    g["w2t"] = A(np.asarray(inputs["L0_ffn_w2_w"]).T)
    g["h1t"] = A(np.asarray(inputs["L0_ffn_h1_w"]).T)
    g["h2t"] = A(np.asarray(inputs["L0_ffn_h2_w"]).T)
    g["w3t"] = A(np.asarray(inputs["L0_ffn_w3_w"]).T)
    for nm, key in [("s1m", "L0_ffn_w1_s"), ("s2m", "L0_ffn_w2_s"),
                    ("sh1m", "L0_ffn_h1_s"), ("sh2m", "L0_ffn_h2_s")]:
        g[nm] = A(np.asarray(inputs[key]).reshape(-1, 128).T)
    g["s3r"] = A(np.asarray(inputs["L0_ffn_w3_s"]).reshape(1, E))

    g["wrt"] = A(np.asarray(inputs["L1_attn_W_r"]).T)
    g["wkt"] = A(np.asarray(inputs["L1_attn_W_k"]).T)
    g["wvt"] = A(np.asarray(inputs["L1_attn_W_v"]).T)
    g["wot"] = A(np.asarray(inputs["L1_attn_W_o"]).T)
    for nm, key in [("tmr", "L1_attn_time_mix_r"), ("tmk", "L1_attn_time_mix_k"),
                    ("tmv", "L1_attn_time_mix_v"), ("tdm", "L1_attn_time_decay"),
                    ("tfm", "L1_attn_time_first"), ("gmm", "L1_attn_ln_gamma"),
                    ("gbm", "L1_attn_ln_beta")]:
        g[nm] = A(np.asarray(inputs[key]).reshape(-1, 128).T)

    g["gw"] = A(inputs["L1_ffn_gate_w"])
    g["gb"] = A(np.asarray(inputs["L1_ffn_gate_b"]).reshape(1, NEXP))
    g["ewt"] = A(np.asarray(inputs["L1_ffn_expert_w"]).reshape(NEXP * E, E))
    g["eb"] = A(inputs["L1_ffn_expert_b"])

    g["eye128"] = np.eye(128, dtype=bf16)
    g["eye128f"] = np.eye(128, dtype=f32)
    inv_freq = (1.0 / (f32(10000.0) ** (np.arange(0, 64, 2, dtype=f32) / f32(64.0)))).astype(f32)
    freqs = np.outer(np.arange(T, dtype=f32), inv_freq).astype(f32)
    g["cosq"] = np.ascontiguousarray(np.tile(np.cos(freqs).astype(f32), (1, NHEAD)))
    g["sinq"] = np.ascontiguousarray(np.tile(np.sin(freqs).astype(f32), (1, NHEAD)))
    m = np.zeros((128, 4 * 1024), dtype=bf16)
    r = np.arange(128)[:, None]
    qf = np.arange(512)[None, :]
    for j in range(4):
        pat = (qf >= j * 128 + r).astype(bf16)
        m[:, 1024 * j:1024 * j + 512] = pat
        m[:, 1024 * j + 512:1024 * (j + 1)] = pat
    g["mask4"] = m
    osel = np.zeros((128, 256), dtype=bf16)
    for h in range(16):
        osel[:, 16 * h + h] = 1
    g["onesel"] = osel
    sbc = np.zeros((16, 8 * 128), dtype=f32)
    for p in range(8):
        sbc[2 * p, 128 * p:128 * p + 64] = 1
        sbc[2 * p + 1, 128 * p + 64:128 * p + 128] = 1
    g["selbc"] = sbc
    ob = np.zeros((128, 2), dtype=f32)
    ob[0:64, 0] = 1
    ob[64:128, 1] = 1
    g["onesbd2"] = ob
    s2 = np.zeros((2, 128), dtype=f32)
    s2[0, 0:64] = 1
    s2[1, 64:128] = 1
    g["sel2"] = s2
    se = np.zeros((8, 8 * 128), dtype=f32)
    for e in range(8):
        se[e, 128 * e:128 * (e + 1)] = 1
    g["selexp"] = se
    g["miscc"] = np.asarray([[1e-5, 0.0, 0.0, 0.0]], dtype=f32)
    return g



def build(dbg=(), blocks=(0, 1)):
    nc = bass.Bass("TRN2", target_bir_lowering=False, detect_race_conditions=False)
    I = lambda name, shape, dt=F32: nc.dram_tensor(name, shape, dt, kind="ExternalInput").ap()
    D = lambda name, shape, dt=F32: nc.dram_tensor(name, shape, dt).ap()

    t = {}
    t["x_in"] = I("x", [T, E])
    for name, shape in [("wqa", [E, QL]), ("wqb", [QL, E]), ("wkva", [E, KVL]),
                        ("wkvb", [KVL, 512]), ("wout", [E, E]), ("bout", [1, E]),
                        ("w1t", [E, HID]), ("w2t", [E, HID]), ("h1t", [HID, HID]),
                        ("h2t", [HID, HID]), ("w3t", [HID, E]),
                        ("s1m", [128, NH]), ("s2m", [128, NH]), ("sh1m", [128, NH]),
                        ("sh2m", [128, NH]), ("s3r", [1, E]),
                        ("wrt", [E, E]), ("wkt", [E, E]), ("wvt", [E, E]), ("wot", [E, E]),
                        ("tmr", [128, NE]), ("tmk", [128, NE]), ("tmv", [128, NE]),
                        ("tdm", [128, NE]), ("tfm", [128, NE]),
                        ("gmm", [128, NE]), ("gbm", [128, NE]),
                        ("gw", [E, NEXP]), ("gb", [1, NEXP]),
                        ("ewt", [NEXP * E, E]), ("eb", [NEXP, E]),
                        ("cosq", [T, 512]), ("sinq", [T, 512]),
                        ("selbc", [16, 8 * 128]), ("onesbd2", [128, 2]),
                        ("sel2", [2, 128]), ("selexp", [8, 8 * 128]),
                        ("miscc", [1, 4])]:
        t[name] = I(name, shape)
    for name, shape in [("eye128", [128, 128]), ("mask4", [128, 4 * 1024]),
                        ("onesel", [128, 256])]:
        t[name] = I(name, shape, BF16)
    t["eye128f"] = I("eye128f", [128, 128])

    t["out"] = nc.dram_tensor("out", [T, E], F32, kind="ExternalOutput").ap()
    t["w1b"] = D("w1b", [NH, 128 * E], BF16)
    t["w2b"] = D("w2b", [NH, 128 * E], BF16)
    t["h1b"] = D("h1b", [NH, 128 * HID], BF16)
    t["h2b"] = D("h2b", [NH, 128 * HID], BF16)
    t["w3b"] = D("w3b", [HID, E], BF16)
    t["xnt_hbm"] = D("xnt_hbm", [E, T], BF16)
    t["acc_hbm"] = D("acc_hbm", [T, E])
    t["x1_hbm"] = D("x1_hbm", [T, E])
    t["xnt1_hbm"] = D("xnt1_hbm", [E, T], BF16)
    t["rwkv_hbm"] = D("rwkv_hbm", [E, T], BF16)
    t["moe_hbm"] = D("moe_hbm", [E, T])
    t["ewq"] = D("ewq", [2 * NEXP, 128 * NE * 512], BF16)
    t["wn_hbm"] = D("wn_hbm", [NEXP, T], BF16)
    t["sc_hbm"] = D("sc_hbm", [8, 128])

    t["dbg"] = {}
    for name in dbg:
        shp, dt = {"dbg_xnt": ([E, T], BF16), "dbg_acc": ([T, E], F32),
                   "dbg_x1": ([T, E], F32), "dbg_xnt1": ([E, T], BF16),
                   "dbg_kk": ([E, T], F32), "dbg_rr": ([E, T], F32),
                   "dbg_vv": ([E, T], F32),
                   "dbg_wkv": ([E, T], F32), "dbg_rwkv": ([E, T], BF16),
                   "dbg_acc1": ([T, E], F32)}[name]
        t["dbg"][name] = nc.dram_tensor(name, shp, dt, kind="ExternalOutput").ap()

    with ExitStack() as ctx:
        P = Prog(nc, ctx)
        sb = lambda name, shape, dt=F32: ctx.enter_context(nc.sbuf_tensor(name, shape, dt))
        t["ident"] = sb("identsb", [128, 128], BF16)
        t["identf"] = sb("identfsb", [128, 128])
        t["negc"] = sb("negcsb", [128, 1])
        t["epsc"] = sb("epscsb", [128, 4])

        d0 = P.sem("dmP0")
        P.op("sync", lambda q: q.dma_start(t["ident"][:, :], t["eye128"]), inc=d0, inc_k=16)
        P.op("sync", lambda q: q.dma_start(t["identf"][:, :], t["eye128f"]), inc=d0, inc_k=16)
        P.op("sync", lambda q: q.dma_start(
            t["epsc"][:, :], t["miscc"][0:1, :].partition_broadcast(128)), inc=d0, inc_k=16)
        P.flush("p0")

        if 0 in blocks:
            _b0(nc, ctx, P, t)
        if 1 in blocks:
            _b1(nc, ctx, P, t)

        P.flush("fin")
    return nc



def _emit_wprecast(P, t):
    first = [t.get("qkv_gate")]
    for (dst, src, nf, j) in [("w1b", "w1t", NH, NE), ("w2b", "w2t", NH, NE),
                              ("h1b", "h1t", NH, NH), ("h2b", "h2t", NH, NH)]:
        pc = P.sem("pc_" + dst)
        srcv = t[src].rearrange("(j p) n -> p j n", p=128)
        for f in range(nf):
            P.op("gpsimd", lambda q, dst=dst, srcv=srcv, f=f, j=j: q.dma_start(
                t[dst][f:f + 1, :].rearrange("f (p j c) -> (f p) j c", p=128, j=j),
                srcv[:, :, 128 * f:128 * (f + 1)]), waits=first, inc=pc, inc_k=16)
            first = []
        t["pc_" + dst] = pc.tok()
    pc = P.sem("pc_w3b")
    w3v = t["w3t"].rearrange("(a p) n -> p a n", p=128)
    w3bv = t["w3b"].rearrange("(a p) n -> p a n", p=128)
    for a in range(8):
        P.op("gpsimd", lambda q, a=a: q.dma_start(
            w3bv[:, 4 * a:4 * (a + 1), :], w3v[:, 4 * a:4 * (a + 1), :]),
            inc=pc, inc_k=16)
    t["pc_w3b"] = pc.tok()
    pc = P.sem("pc_ewt")
    ewv4 = t["ewt"].rearrange("(e j p) n -> e p j n", p=128, j=NE)
    for e in range(NEXP):
        for rF in range(2):
            P.op("gpsimd", lambda q, e=e, rF=rF: q.dma_start(
                t["ewq"][2 * e + rF:2 * e + rF + 1, :].rearrange(
                    "o (p j c) -> (o p) j c", p=128, j=NE),
                ewv4[e:e + 1, :, :, 512 * rF:512 * (rF + 1)].rearrange(
                    "e p j n -> (e p) j n")), inc=pc, inc_k=16)
    t["pc_ewt"] = pc.tok()



def _emit_ln(nc, pctx_unused, P, t, src_hbm, xnt, dst_hbm, tag, dbgname):
    lctx = ExitStack()
    pctx = lctx
    sb = lambda name, shape, dt=F32: pctx.enter_context(nc.sbuf_tensor(name, shape, dt))
    ps = lambda name, shape, dt=F32: pctx.enter_context(nc.psum_tensor(name, shape, dt))
    xbuf = sb("xbuf" + tag, [128, 3 * E])
    xn_bf = sb("xn_bf" + tag, [128, 2 * E], BF16)
    sq_scr = sb("sq_scr" + tag, [128, E])
    stat = sb("stat" + tag, [128, 24])
    ptr = [ps(f"ptrLN{k}" + tag, [128, 128], BF16) for k in range(4)]

    dv, ac, pe, dv2, ac2 = (P.sem(n) for n in ["dv", "ac", "pe", "dv2", "ac2"])
    lds = [P.sem("dmLNx0" + tag), P.sem("dmLNx1" + tag), P.sem("dmLNx2" + tag)]
    stx = P.sem("dmLNs" + tag)
    apply_tok = [None] * NT
    sq_tok = [None] * NT
    tr_last = [None] * NT
    cp_tok = []
    for i in range(NT):
        b = i % 3
        b2 = i % 2
        buf = xbuf[:, b * E:(b + 1) * E]
        obf = xn_bf[:, b2 * E:(b2 + 1) * E]
        sc = 8 * b
        w = [apply_tok[i - 3], sq_tok[i - 3]] if i >= 3 else []
        tok_x = P.op("sync", lambda q, i=i, buf=buf: q.dma_start(
            buf, src_hbm[128 * i:128 * (i + 1), :]), waits=w, inc=lds[b], inc_k=16)
        if i == 24 and tag == "A":
            t["ln_gate"] = tok_x
        P.op("vector", lambda q, buf=buf, sc=sc: P.sf(q, "vector", q.reduce_sum(
            stat[:, sc:sc + 1], buf, AX.X)) and None, waits=[tok_x])
        sq_tok[i] = P.op("scalar", lambda q, buf=buf, sc=sc: q.activation(
            sq_scr[:, :], buf, AF.Square, accum_out=stat[:, sc + 1:sc + 2]),
            waits=[tok_x], inc=ac)

        def small(q, sc=sc):
            P.sf(q, "vector", q.tensor_scalar_mul(stat[:, sc + 2:sc + 3],
                                                  stat[:, sc:sc + 1], 1.0 / E))
            q.tensor_scalar_mul(stat[:, sc + 3:sc + 4], stat[:, sc + 2:sc + 3], -1.0)
            P.sf(q, "vector", q.tensor_mul(stat[:, sc + 4:sc + 5], stat[:, sc + 2:sc + 3],
                                           stat[:, sc + 2:sc + 3]))
            P.sf(q, "vector", q.tensor_scalar_mul(stat[:, sc + 5:sc + 6],
                                                  stat[:, sc + 1:sc + 2], 1.0 / E))
            return q.tensor_sub(stat[:, sc + 6:sc + 7], stat[:, sc + 5:sc + 6],
                                stat[:, sc + 4:sc + 5])
        tok_var = P.op("vector", small, waits=[sq_tok[i]], inc=dv)

        def rstd(q, sc=sc):
            P.sf(q, "scalar", q.activation(stat[:, sc + 7:sc + 8], stat[:, sc + 6:sc + 7],
                                           AF.Ln, bias=t["epsc"][:, 0:1]))
            return q.activation(stat[:, sc + 7:sc + 8], stat[:, sc + 7:sc + 8], AF.Exp, scale=-0.5)
        tok_rstd = P.op("scalar", rstd, waits=[tok_var], inc=ac)
        wapply = [tok_rstd] + ([tr_last[i - 2]] if i >= 2 else [])
        apply_tok[i] = P.op("vector", lambda q, buf=buf, obf=obf, sc=sc: q.tensor_scalar(
            obf, buf, stat[:, sc + 3:sc + 4], stat[:, sc + 7:sc + 8], ALU.add, ALU.mult),
            waits=wapply, inc=dv)
        for j in range(NE):
            k = (NE * i + j) % 4
            w = [apply_tok[i]] + ([cp_tok[-4]] if len(cp_tok) >= 4 else [])
            tok_tr = P.op("tensor", lambda q, obf=obf, j=j, k=k: q.transpose(
                ptr[k][:, :], obf[:, 128 * j:128 * (j + 1)], t["ident"][:, :]),
                waits=w, inc=pe)
            isv = (j % 2 == 0)

            def cpf(q, i=i, j=j, k=k, isv=isv):
                dst = xnt[j][:, 128 * i:128 * (i + 1)]
                return q.tensor_copy(dst, ptr[k][:, :]) if isv else q.copy(
                    dst, ptr[k][:, :])
            cp_tok.append(P.op("vector" if isv else "scalar", cpf,
                               waits=[tok_tr], inc=(dv2 if isv else ac2)))
        tr_last[i] = pe.tok()
    for j in range(NE):
        if dst_hbm is not None:
            P.op("sync", lambda q, j=j: q.dma_start(
                dst_hbm[128 * j:128 * (j + 1), :], xnt[j][:, :]),
                waits=[cp_tok[-1], cp_tok[-2]], inc=stx, inc_k=16)
    t["ln_cp_last" + tag] = [cp_tok[-1], cp_tok[-2]]
    P.flush("ln" + tag)
    lctx.close()



def _b0(nc, ctx, P, t):
    with ExitStack() as pctx:
        sb = lambda name, shape, dt=F32: pctx.enter_context(nc.sbuf_tensor(name, shape, dt))
        qlat = sb("qlat", [QL, T], BF16)
        kvlat = sb("kvlat", [KVL, T], BF16)
        t["wqb_sb_g"] = sb("wqb_sb_g", [QL, E], BF16)
        t["wkvb_sb_g"] = sb("wkvb_sb_g", [KVL, 512], BF16)
        with ExitStack() as lctx:
            xnt = [lctx.enter_context(nc.sbuf_tensor(f"xnt{j}", [128, T], BF16))
                   for j in range(NE)]
            _emit_ln(nc, None, P, t, t["x_in"], xnt, t["xnt_hbm"], "A", "dbg_xnt")
            _b0_latents(nc, lctx, P, t, xnt, qlat, kvlat)
        qts = [sb(f"qt{j}", [128, T], BF16) for j in range(NE)]
        ktd = sb("ktd", [128, 2 * T], BF16)
        vsb = sb("vsb", [128, NT * 256], BF16)
        wout_sb = sb("wout_sb", [128, NE * E], BF16)
        P.op("gpsimd", lambda q: q.dma_start(
            ap3(wout_sb[:, :], E), t["wout"].rearrange("(j p) c -> p j c", p=128)),
            inc=P.sem("gmWo"), inc_k=16)
        _b0_qkv_tiles(nc, pctx, P, t, qlat, kvlat, qts, ktd, vsb)
        _emit_wprecast(P, t)
        _b0_attn(nc, pctx, P, t, qts, ktd, vsb, wout_sb)
    _b0_ffn(nc, ctx, P, t)


def _b0_latents(nc, lctx, P, t, xnt, qlat, kvlat):
    sb = lambda name, shape, dt=F32: lctx.enter_context(nc.sbuf_tensor(name, shape, dt))
    ps = lambda name, shape, dt=F32: lctx.enter_context(nc.psum_tensor(name, shape, dt))
    wqa_sb = sb("wqa_sb", [128, NE * QL], BF16)
    wkva_sb = sb("wkva_sb", [128, NE * KVL], BF16)
    pql = [ps(f"pql{k}", [128, 512]) for k in range(2)]

    gw = P.sem("gmW")
    pe, ac = P.sem("pe"), P.sem("ac")

    P.op("gpsimd", lambda q: q.dma_start(
        ap3(wqa_sb[:, :], QL), t["wqa"].rearrange("(j p) c -> p j c", p=128)), inc=gw, inc_k=16)
    P.op("gpsimd", lambda q: q.dma_start(
        ap3(wkva_sb[:, :], KVL), t["wkva"].rearrange("(j p) c -> p j c", p=128)), inc=gw, inc_k=16)
    P.op("gpsimd", lambda q: q.dma_start(t["wqb_sb_g"][:, :], t["wqb"]), inc=gw, inc_k=16)
    P.op("gpsimd", lambda q: q.dma_start(t["wkvb_sb_g"][:, :], t["wkvb"]), inc=gw, inc_k=16)
    tok_wqa = tok_wkva = gw.tok()
    t["wqbkv_tok"] = gw.tok()

    ev_tok = [None, None]
    for which, (wtok, wsb, cw, dst, rows) in enumerate(
            [(tok_wqa, wqa_sb, QL, qlat, 64), (tok_wkva, wkva_sb, KVL, kvlat, 32)]):
        for n in range(8):
            k = n % 2
            w = [wtok, ev_tok[k]] + (t["ln_cp_lastA"] if n >= 7 else [])
            for j in range(NE):
                P.op("tensor", lambda q, n=n, j=j, k=k, wsb=wsb, cw=cw, rows=rows: q.matmul(
                    pql[k][0:rows, :], wsb[:, cw * j:cw * (j + 1)],
                    xnt[j][:, 512 * n:512 * (n + 1)], start=(j == 0), stop=(j == NE - 1)),
                    waits=(w if j == 0 else []), inc=(pe if j == NE - 1 else None))
            tok_mm = pe.tok()
            ev_tok[k] = P.op("scalar", lambda q, n=n, k=k, dst=dst, rows=rows: q.copy(
                dst[:, 512 * n:512 * (n + 1)], pql[k][0:rows, :]), waits=[tok_mm], inc=ac)
    P.flush("p2a1")


def _b0_qkv_tiles(nc, pctx, P, t, qlat, kvlat, qts, ktd, vsb):
    sb = lambda name, shape, dt=F32: pctx.enter_context(nc.sbuf_tensor(name, shape, dt))
    lctx = ExitStack()
    sbl = lambda name, shape, dt=F32: lctx.enter_context(nc.sbuf_tensor(name, shape, dt))
    psl = lambda name, shape, dt=F32: lctx.enter_context(nc.psum_tensor(name, shape, dt))
    wqb_sb = t["wqb_sb_g"]
    wkvb_sb = t["wkvb_sb_g"]
    qf = sbl("qf", [128, 2 * E])
    kvf = sbl("kvf", [128, 2 * 512])
    q_rope = sbl("q_rope", [128, 2 * E], BF16)
    k_rope = sbl("k_rope", [128, 2 * 256], BF16)
    cs_t = sbl("cs_t", [128, 2 * 512])
    sn_t = sbl("sn_t", [128, 2 * 512])
    rtmp = sbl("rtmp", [128, 2 * 512])
    sq_scr2 = sbl("sq_scr2", [128, E])
    nrm = sbl("nrm", [128, 8])
    smal = sbl("smal", [1, 160])
    pQ = psl("pQ", [128, 1024])
    pkv = psl("pkv", [128, 512])
    ptr = [psl(f"ptrQK{k}", [128, 128], BF16) for k in range(2)]

    pe, ac, dv, dv2 = (P.sem(n) for n in ["pe", "ac", "dv", "dv2"])
    dcs = [P.sem("dmCS0"), P.sem("dmCS1")]
    dsc, dscl, dnc = P.sem("dmSc"), P.sem("dmScL"), P.sem("dmNc")
    tok_wqb = t["wqbkv_tok"]
    tok_wkvb = t["wqbkv_tok"]

    P.op("vector", lambda q: q.memset(nrm[:, :], 0.0), inc=dv)

    cp_tok = []
    tile = [dict() for _ in range(NT)]
    for i in range(NT):
        b = i % 2
        ti = tile[i]
        prev = tile[i - 2] if i >= 2 else None
        w = [prev["rope_q"], prev["rope_k"]] if prev else []
        P.op("sync", lambda q, i=i, b=b: q.dma_start(
            cs_t[:, 512 * b:512 * (b + 1)], t["cosq"][128 * i:128 * (i + 1), :]),
            waits=w, inc=dcs[b], inc_k=16)
        ti["cs"] = P.op("sync", lambda q, i=i, b=b: q.dma_start(
            sn_t[:, 512 * b:512 * (b + 1)], t["sinq"][128 * i:128 * (i + 1), :]),
            inc=dcs[b], inc_k=16)
        if i == 8:
            t["qkv_gate"] = ti["cs"]
        w = [tok_wqb] + ([tile[i - 1]["qev"]] if i >= 1 else [])
        for h in range(2):
            P.op("tensor", lambda q, i=i, h=h: q.matmul(
                pQ[:, 512 * h:512 * (h + 1)], qlat[:, 128 * i:128 * (i + 1)],
                wqb_sb[:, 512 * h:512 * (h + 1)], start=True, stop=True),
                waits=(w if h == 0 else []), inc=(pe if h == 1 else None))
        tok_qmm = pe.tok()
        w = [tok_qmm] + ([prev["rope_q"], prev["qn"]] if prev else [])
        ti["qev"] = P.op("vector", lambda q, b=b: q.tensor_copy(
            qf[:, E * b:E * (b + 1)], pQ[:, :]), waits=w, inc=dv)
        ti["qn"] = P.op("scalar", lambda q, b=b: q.activation(
            sq_scr2[:, :], qf[:, E * b:E * (b + 1)], AF.Square, accum_out=nrm[:, 2:3]),
            waits=[ti["qev"]], inc=ac)
        P.op("vector", lambda q: q.tensor_max(nrm[:, 0:1], nrm[:, 0:1], nrm[:, 2:3]),
             waits=[ti["qn"]], inc=dv)

        def ropeq(q, b=b):
            qv = qf[:, E * b:E * (b + 1)]
            e = ap3(qv, 64)[:, :, 0:32]
            o = ap3(qv, 64)[:, :, 32:64]
            c = ap3(cs_t[:, 512 * b:512 * (b + 1)], 32)
            s = ap3(sn_t[:, 512 * b:512 * (b + 1)], 32)
            t1 = ap3(rtmp[:, 0:512], 32)
            t2 = ap3(rtmp[:, 512:1024], 32)
            ob = ap3(q_rope[:, E * b:E * (b + 1)], 64)
            q.tensor_mul(t1, e, c)
            q.tensor_mul(t2, o, s)
            q.tensor_sub(ob[:, :, 0:32], t1, t2)
            q.tensor_mul(t1, e, s)
            q.tensor_mul(t2, o, c)
            return q.tensor_add(ob[:, :, 32:64], t1, t2)
        w = [ti["cs"], ti["qev"]] + ([prev["trq_last"]] if prev else [])
        ti["rope_q"] = P.op("vector", ropeq, waits=w, inc=dv)

        w = [tok_wkvb] + ([tile[i - 1]["kvev"]] if i >= 1 else [])
        P.op("tensor", lambda q, i=i: q.matmul(
            pkv[:, :], kvlat[:, 128 * i:128 * (i + 1)], wkvb_sb[:, :], start=True, stop=True),
            waits=w, inc=pe)
        tok_kvmm = pe.tok()
        w = [tok_kvmm] + ([prev["rope_k"], prev["kn"], prev["vev"]] if prev else [])
        ti["kvev"] = P.op("vector", lambda q, b=b: q.tensor_copy(
            kvf[:, 512 * b:512 * (b + 1)], pkv[:, :]), waits=w, inc=dv)
        ti["kn"] = P.op("scalar", lambda q, b=b: q.activation(
            sq_scr2[:, 0:256], kvf[:, 512 * b:512 * b + 256], AF.Square,
            accum_out=nrm[:, 3:4]), waits=[ti["kvev"]], inc=ac)
        P.op("vector", lambda q: q.tensor_max(nrm[:, 1:2], nrm[:, 1:2], nrm[:, 3:4]),
             waits=[ti["kn"]], inc=dv)

        def ropek(q, b=b):
            kv = kvf[:, 512 * b:512 * b + 256]
            e = ap3(kv, 64)[:, :, 0:32]
            o = ap3(kv, 64)[:, :, 32:64]
            c = ap3(cs_t[:, 512 * b:512 * b + 128], 32)
            s = ap3(sn_t[:, 512 * b:512 * b + 128], 32)
            t1 = ap3(rtmp[:, 0:128], 32)
            t2 = ap3(rtmp[:, 128:256], 32)
            ob = ap3(k_rope[:, 256 * b:256 * (b + 1)], 64)
            q.tensor_mul(t1, e, c)
            q.tensor_mul(t2, o, s)
            q.tensor_sub(ob[:, :, 0:32], t1, t2)
            q.tensor_mul(t1, e, s)
            q.tensor_mul(t2, o, c)
            return q.tensor_add(ob[:, :, 32:64], t1, t2)
        w = [ti["cs"], ti["kvev"]] + ([prev["trk_last"]] if prev else [])
        ti["rope_k"] = P.op("vector", ropek, waits=w, inc=dv)
        ti["vev"] = P.op("vector", lambda q, i=i, b=b: q.tensor_copy(
            vsb[:, 256 * i:256 * (i + 1)], kvf[:, 512 * b + 256:512 * b + 512]),
            waits=[ti["kvev"]], inc=dv)

        for j in range(NE):
            k = len(cp_tok) % 2
            w = [ti["rope_q"]] + ([cp_tok[-2]] if len(cp_tok) >= 2 else [])
            tok_tr = P.op("tensor", lambda q, b=b, j=j, k=k: q.transpose(
                ptr[k][:, :], q_rope[:, E * b + 128 * j:E * b + 128 * (j + 1)],
                t["ident"][:, :]), waits=w, inc=pe)
            cp_tok.append(P.op("vector", lambda q, i=i, j=j, k=k: q.tensor_copy(
                qts[j][:, 128 * i:128 * (i + 1)], ptr[k][:, :]), waits=[tok_tr], inc=dv2))
        ti["trq_last"] = pe.tok()
        for jj in range(2):
            k = len(cp_tok) % 2
            w = [ti["rope_k"]] + ([cp_tok[-2]] if len(cp_tok) >= 2 else [])
            tok_tr = P.op("tensor", lambda q, b=b, jj=jj, k=k: q.transpose(
                ptr[k][:, :], k_rope[:, 256 * b + 128 * jj:256 * b + 128 * (jj + 1)],
                t["ident"][:, :]), waits=w, inc=pe)
            cp_tok.append(P.op("vector", lambda q, i=i, jj=jj, k=k: q.tensor_copy(
                ktd[:, T * jj + 128 * i:T * jj + 128 * (i + 1)], ptr[k][:, :]),
                waits=[tok_tr], inc=dv2))
        ti["trk_last"] = pe.tok()

    P.op("sync", lambda q: q.dma_start(
        t["sc_hbm"][0:1, :].rearrange("o n -> n o"), nrm[:, 0:1]),
        waits=[dv.tok()], inc=dsc, inc_k=16)
    tok_k2h = P.op("sync", lambda q: q.dma_start(
        t["sc_hbm"][1:2, :].rearrange("o n -> n o"), nrm[:, 1:2]),
        waits=[dv.tok()], inc=dsc, inc_k=16)
    P.op("sync", lambda q: q.dma_start(
        smal[0:1, 0:128], t["sc_hbm"][0:1, :]), waits=[tok_k2h], inc=dscl, inc_k=16)
    tok_b2 = P.op("sync", lambda q: q.dma_start(
        rtmp[0:1, 0:128], t["sc_hbm"][1:2, :]), waits=[tok_k2h], inc=dscl, inc_k=16)
    tok_b1 = tok_b2

    def cfin(q):
        P.sf(q, "vector", q.reduce_max(smal[0:1, 144:145], smal[0:1, 0:128], AX.X))
        P.sf(q, "vector", q.reduce_max(smal[0:1, 145:146], rtmp[0:1, 0:128], AX.X))
        return q.tensor_mul(smal[0:1, 146:147], smal[0:1, 144:145], smal[0:1, 145:146])
    tok_c1 = P.op("vector", cfin, waits=[tok_b1, tok_b2], inc=dv)

    def cfin2(q):
        P.sf(q, "scalar", q.activation(smal[0:1, 146:147], smal[0:1, 146:147], AF.Ln))
        return q.activation(smal[0:1, 146:147], smal[0:1, 146:147], AF.Exp, scale=0.5)
    tok_c2 = P.op("scalar", cfin2, waits=[tok_c1], inc=ac)

    def cfin3(q):
        P.sf(q, "vector", q.tensor_scalar_mul(smal[0:1, 147:148], smal[0:1, 146:147], 0.125))
        P.sf(q, "vector", q.tensor_scalar_min(smal[0:1, 147:148], smal[0:1, 147:148], 80.0))
        return q.tensor_scalar_mul(smal[0:1, 147:148], smal[0:1, 147:148], -1.0)
    tok_c3 = P.op("vector", cfin3, waits=[tok_c2], inc=dv)
    tok_c4 = P.op("sync", lambda q: q.dma_start(
        t["sc_hbm"][2:3, 0:1], smal[0:1, 147:148]), waits=[tok_c3], inc=dsc, inc_k=16)
    P.op("sync", lambda q: q.dma_start(
        t["negc"][:, :], t["sc_hbm"][2:3, 0:1].partition_broadcast(128)),
        waits=[tok_c4], inc=dnc, inc_k=16)
    P.flush("p2a2")
    lctx.close()


def _b0_attn(nc, pctx, P, t, qts, ktd, vsb, wout_sb):
    sb = lambda name, shape, dt=F32: pctx.enter_context(nc.sbuf_tensor(name, shape, dt))
    ps = lambda name, shape, dt=F32: pctx.enter_context(nc.psum_tensor(name, shape, dt))
    mask_sb = sb("mask_sb", [128, 4 * 1024], BF16)
    osel_sb = sb("osel_sb", [128, 256], BF16)
    sbc_sb = sb("sbc_sb", [16, 8 * 128])
    expb = sb("expb", [128, 4 * 512], BF16)
    outu = sb("outu", [128, 8 * 512], BF16)
    outn = sb("outn", [128, 8 * 512], BF16)
    rec16 = sb("rec16", [16, 512])
    recb = sb("recb", [128, 512])
    xres = sb("xres", [128, 2 * E])
    acct = sb("acct", [128, 2 * E])
    bout_rep = sb("bout_repL", [128, E])

    pst2 = [ps(f"pst2_{k}", [128, 1024]) for k in range(2)]
    pav0 = ps("pav0", [128, 512])
    pav1 = ps("pav1", [128, 512])
    pden = ps("pden", [128, 512])
    pbc = ps("pbc", [128, 512])

    pe, ac, dv, dv2, ac2 = (P.sem(n) for n in ["pe", "ac", "dv", "dv2", "ac2"])
    dm = P.sem("dmM")
    dxr = [P.sem("dmXr0"), P.sem("dmXr1")]
    dac = [P.sem("dmAc0"), P.sem("dmAc1")]

    P.op("sync", lambda q: q.dma_start(mask_sb[:, :], t["mask4"]), inc=dm, inc_k=16)
    P.op("sync", lambda q: q.dma_start(osel_sb[:, :], t["onesel"]), inc=dm, inc_k=16)
    P.op("sync", lambda q: q.dma_start(sbc_sb[:, :], t["selbc"]), inc=dm, inc_k=16)
    P.op("sync", lambda q: q.dma_start(
        bout_rep[:, :], t["bout"][0:1, :].partition_broadcast(128)), inc=dm, inc_k=16)
    dm_tok = dm.tok()

    xload_tok = [None, None]
    acc_store_tok = [None, None]
    norm_tail = None
    wout_tail = None
    rec_tail = None
    stbank_free = [None, None]
    expset_free = [None, None]

    for qt in range(8):
        nkc = 4 * (qt + 1)
        evac_tok = [None] * 8
        first_den = True
        last_exp = None
        for p in range(8):
            kv = p // 4

            def emit_st(kc, p=None, kvblk=None):
                ks = kc % 2
                wst = [stbank_free[ks], dm_tok]
                for h in range(2):
                    P.op("tensor", lambda q, p=p, kvblk=kvblk, kc=kc, h=h, ks=ks, qt=qt: q.matmul(
                        pst2[ks][:, 512 * h:512 * (h + 1)],
                        ktd[64 * h:64 * (h + 1),
                            T * kvblk + 128 * kc:T * kvblk + 128 * (kc + 1)],
                        qts[p][64 * h:64 * (h + 1), 512 * qt:512 * (qt + 1)],
                        start=True, stop=True, tile_position=(64 * h, 0)),
                        waits=(wst if h == 0 else []), inc=(pe if h == 1 else None))
                return pe.tok()

            st_tok = {0: emit_st(0, p, kv)}
            if nkc > 1:
                st_tok[1] = emit_st(1, p, kv)
            av_last = None
            for kc in range(nkc):
                diag_j = kc - 4 * qt
                es = kc % 2
                wexp = [st_tok[kc], expset_free[es]]

                def expf(q, kc=kc, es=es):
                    return q.activation(expb[:, 1024 * es:1024 * (es + 1)],
                                        pst2[kc % 2][:, :],
                                        AF.Exp, scale=0.125, bias=t["negc"][:, 0:1])
                tok_exp = P.op("scalar", expf, waits=wexp, inc=ac)
                stbank_free[kc % 2] = tok_exp
                if diag_j >= 0:
                    def maskf(q, diag_j=diag_j, es=es):
                        return q.tensor_mul(
                            expb[:, 1024 * es:1024 * (es + 1)],
                            expb[:, 1024 * es:1024 * (es + 1)],
                            mask_sb[:, 1024 * diag_j:1024 * (diag_j + 1)])
                    tok_exp = P.op("vector", maskf, waits=[tok_exp, dm_tok], inc=dv)
                last_exp = tok_exp
                if kc + 2 < nkc:
                    st_tok[kc + 2] = emit_st(kc + 2, p, kv)
                wav = [tok_exp]
                if kc == 0:
                    if p > 0:
                        wav.append(evac_tok[p - 1])
                    elif rec_tail is not None:
                        wav.append(rec_tail)

                def avf(q, p=p, kv=kv, kc=kc, nkc=nkc, fd=first_den, es=es):
                    e0 = expb[:, 1024 * es:1024 * es + 512]
                    e1 = expb[:, 1024 * es + 512:1024 * es + 1024]
                    q.matmul(pav0[0:64, :],
                             vsb[:, 256 * kc + 64 * (2 * kv):256 * kc + 64 * (2 * kv) + 64],
                             e0, start=(kc == 0), stop=(kc == nkc - 1), tile_position=(0, 0))
                    q.matmul(pav1[64:128, :],
                             vsb[:, 256 * kc + 64 * (2 * kv + 1):256 * kc + 64 * (2 * kv + 1) + 64],
                             e1, start=(kc == 0), stop=(kc == nkc - 1), tile_position=(0, 64))
                    q.matmul(pden[0:16, :], osel_sb[:, 16 * (2 * p):16 * (2 * p) + 16],
                             e0, start=fd, stop=False)
                    return q.matmul(pden[0:16, :],
                                    osel_sb[:, 16 * (2 * p + 1):16 * (2 * p + 1) + 16],
                                    e1, start=False, stop=(p == 7 and kc == nkc - 1))
                av_last = P.op("tensor", avf, waits=wav + [dm_tok], inc=pe)
                expset_free[es] = av_last
                first_den = False
            w = [av_last] + ([norm_tail] if norm_tail else [])
            def evacf(q, p=p):
                q.copy(outu[0:64, 512 * p:512 * (p + 1)], pav0[0:64, :])
                return q.copy(outu[64:128, 512 * p:512 * (p + 1)], pav1[64:128, :])
            evac_tok[p] = P.op("scalar", evacf, waits=w, inc=ac2)

        def recf(q):
            P.sf(q, "scalar", q.activation(rec16[:, :], pden[0:16, :], AF.Ln))
            return q.activation(rec16[:, :], rec16[:, :], AF.Exp, scale=-1.0)
        rec_tail = P.op("scalar", recf, waits=[av_last], inc=ac)
        norm_toks = []
        for p in range(8):
            w = [rec_tail] + ([norm_toks[-1]] if norm_toks else [])
            tok_bc = P.op("tensor", lambda q, p=p: q.matmul(
                pbc[:, :], sbc_sb[:, 128 * p:128 * (p + 1)], rec16[:, :],
                start=True, stop=True), waits=w, inc=pe)

            def normf(q, p=p):
                q.tensor_copy(recb[:, :], pbc[:, :])
                return q.tensor_mul(outn[:, 512 * p:512 * (p + 1)],
                                    outu[:, 512 * p:512 * (p + 1)], recb[:, :])
            norm_toks.append(P.op("vector", normf, waits=[tok_bc, evac_tok[p]], inc=dv))
        norm_tail = norm_toks[-1]

        wacc_tok = [None] * 4
        for s4 in range(4):
            i = 4 * qt + s4
            b = i % 2
            w = [acc_store_tok[b]] if acc_store_tok[b] else []
            xload_tok[b] = P.op("sync", lambda q, i=i, b=b: q.dma_start(
                xres[:, E * b:E * (b + 1)], t["x_in"][128 * i:128 * (i + 1), :]),
                waits=w, inc=dxr[b], inc_k=16)
            wmm = [norm_tail, last_exp, P.sems["gmWo"].tok()]
            if s4 >= 2:
                wmm.append(wacc_tok[s4 - 2])
            bs = s4 % 2
            for p in range(8):
                for h in range(2):
                    P.op("tensor", lambda q, p=p, h=h, s4=s4, bs=bs: q.matmul(
                        pst2[bs][:, 512 * h:512 * (h + 1)],
                        outn[:, 512 * p + 128 * s4:512 * p + 128 * (s4 + 1)],
                        wout_sb[:, E * p + 512 * h:E * p + 512 * (h + 1)],
                        start=(p == 0), stop=(p == 7)),
                        waits=(wmm if (p == 0 and h == 0) else []),
                        inc=(pe if (p == 7 and h == 1) else None))
            tok_wmm = pe.tok()
            wout_tail = tok_wmm

            def accf(q, b=b, bs=bs):
                q.tensor_add(acct[:, E * b:E * (b + 1)], pst2[bs][:, :], bout_rep[:, :])
                if False:
                    pass
                return q.tensor_add(acct[:, E * b:E * (b + 1)], acct[:, E * b:E * (b + 1)],
                                    xres[:, E * b:E * (b + 1)])
            wacc_tok[s4] = P.op("vector", accf, waits=[tok_wmm, xload_tok[b], dm_tok], inc=dv)
            acc_store_tok[b] = P.op("sync", lambda q, i=i, b=b: q.dma_start(
                t["acc_hbm"][128 * i:128 * (i + 1), :], acct[:, E * b:E * (b + 1)]),
                waits=[wacc_tok[s4]], inc=dac[b], inc_k=16)
            if "dbg_acc" in t["dbg"]:
                acc_store_tok[b] = P.op("sync", lambda q, i=i, b=b: q.dma_start(
                    t["dbg"]["dbg_acc"][128 * i:128 * (i + 1), :], acct[:, E * b:E * (b + 1)]),
                    waits=[wacc_tok[s4]], inc=dac[b], inc_k=16)
        stbank_free[0] = wacc_tok[2]
        stbank_free[1] = wacc_tok[3]
    P.flush("p2b")


def _b0_ffn(nc, ctx, P, t):
    with ExitStack() as pctx:
        sb = lambda name, shape, dt=F32: pctx.enter_context(nc.sbuf_tensor(name, shape, dt))
        ps = lambda name, shape, dt=F32: pctx.enter_context(nc.psum_tensor(name, shape, dt))
        TC = 1024
        NC = T // TC
        g1 = sb("g1", [128, NH * TC], BF16)
        g2 = sb("g2", [128, NH * TC], BF16)
        xct = sb("xct", [128, NE * TC], BF16)
        wt1 = [sb(f"wt1_{k}", [128, NE * 128], BF16) for k in range(2)]
        wt2 = [sb(f"wt2_{k}", [128, NE * 128], BF16) for k in range(2)]
        wth = [sb(f"wth_{k}", [128, NH * 128], BF16) for k in range(2)]
        wt3 = wth
        s1_sb = sb("s1_sb", [128, NH])
        s2_sb = sb("s2_sb", [128, NH])
        sh1_sb = sb("sh1_sb", [128, NH])
        sh2_sb = sb("sh2_sb", [128, NH])
        evtmp = sb("evtmp", [128, 2 * 512])
        t2tmp = sb("t2tmp", [128, 512])
        acc_t = sb("acc_t", [128, 2 * E])
        s3_rep = sb("s3_repL", [128, E])
        stat2 = sb("statB2", [128, 16])
        xnf = sb("xnfB", [128, 2 * E])
        stg = [sb("stgB0", [128, 8 * 512], BF16)]

        pp = [ps(f"pp{k}", [128, 1024]) for k in range(4)]

        pe, ac, dv, dv2, ac2 = (P.sem(n) for n in ["pe", "ac", "dv", "dv2", "ac2"])
        dm = P.sem("dmM")
        dxc = P.sem("dmXc")
        dw1 = [P.sem("gmw1_0"), P.sem("gmw1_1")]
        dw2 = [P.sem("gmw2_0"), P.sem("gmw2_1")]
        dwh = [P.sem("gmwh_0"), P.sem("gmwh_1")]
        wq = ["gpsimd", "gpsimd"]
        daq = [P.sem("dmAq0"), P.sem("dmAq1")]
        dx1 = [P.sem("dmX1_0"), P.sem("dmX1_1")]

        for nm, dst in [("s1m", s1_sb), ("s2m", s2_sb), ("sh1m", sh1_sb), ("sh2m", sh2_sb)]:
            P.op("sync", lambda q, nm=nm, dst=dst: q.dma_start(dst[:, :], t[nm]),
                 inc=dm, inc_k=16)
        P.op("sync", lambda q: q.dma_start(
            s3_rep[:, :], t["s3r"][0:1, :].partition_broadcast(128)), inc=dm, inc_k=16)
        dm_tok = dm.tok()

        ring_free = [None] * 8
        ring = [0]
        lnb_apply = [None, None]
        lnb_tr = [None, None]
        lnb_store = [None]
        dxn = P.sem("dmXn1")

        def slot_ap(s):
            return pp[s // 2][:, 512 * (s % 2):512 * (s % 2 + 1)]

        stageA_mm_tail = None
        acc_guard = [None, None]
        wcons1 = [None, None]
        wcons2 = [None, None]
        for c in range(NC):
            w = [stageA_mm_tail] if stageA_mm_tail else []
            xc_tok = P.op("sync", lambda q, c=c: q.dma_start(
                ap3(xct[:, :], TC),
                t["xnt_hbm"].rearrange("(j p) n -> p j n", p=128)[:, :, TC * c:TC * (c + 1)]),
                waits=w, inc=dxc, inc_k=16)

            evtmp_free = [None] * 2
            t2_free = [None] * 2
            evA_tail = None
            for f in range(NH):
                kb = f % 2
                wl1 = P.op(wq[kb], lambda q, f=f, kb=kb: q.dma_start(
                    ap3(wt1[kb][:, :], 128),
                    t["w1b"][f:f + 1, :].rearrange("f (p j c) -> (f p) j c", p=128, j=NE)),
                    waits=[wcons1[kb]] + ([t["pc_w1b"]] if (c == 0 and f < 2) else []),
                    inc=dw1[kb], inc_k=16)
                wl2 = P.op(wq[kb], lambda q, f=f, kb=kb: q.dma_start(
                    ap3(wt2[kb][:, :], 128),
                    t["w2b"][f:f + 1, :].rearrange("f (p j c) -> (f p) j c", p=128, j=NE)),
                    waits=[wcons2[kb]] + ([t["pc_w2b"]] if (c == 0 and f < 2) else []),
                    inc=dw2[kb], inc_k=16)
                slots = [(ring[0] + d) % 8 for d in range(4)]
                ring[0] += 4
                for half in range(2):
                    s = slots[half]
                    wmm = [xc_tok, wl1, ring_free[s]]
                    for j in range(NE):
                        P.op("tensor", lambda q, f=f, kb=kb, j=j, half=half, s=s: q.matmul(
                            slot_ap(s), wt1[kb][:, 128 * j:128 * (j + 1)],
                            xct[:, TC * j + 512 * half:TC * j + 512 * (half + 1)],
                            start=(j == 0), stop=(j == NE - 1)),
                            waits=(wmm if j == 0 else []), inc=(pe if j == NE - 1 else None))
                wcons1[kb] = pe.tok()
                for half in range(2):
                    s = slots[2 + half]
                    wmm = [wl2, ring_free[s]]
                    for j in range(NE):
                        P.op("tensor", lambda q, f=f, kb=kb, j=j, half=half, s=s: q.matmul(
                            slot_ap(s), wt2[kb][:, 128 * j:128 * (j + 1)],
                            xct[:, TC * j + 512 * half:TC * j + 512 * (half + 1)],
                            start=(j == 0), stop=(j == NE - 1)),
                            waits=(wmm if j == 0 else []), inc=(pe if j == NE - 1 else None))
                wcons2[kb] = pe.tok()
                tok_mm = pe.tok()
                for half in range(2):
                    s_w1 = slots[half]
                    s_w2 = slots[2 + half]
                    et = (2 * f + half) % 2
                    tt = 0
                    tok_silu = P.op("scalar", lambda q, f=f, s_w1=s_w1, et=et: q.activation(
                        evtmp[:, 512 * et:512 * (et + 1)], slot_ap(s_w1), AF.Silu,
                        scale=s1_sb[:, f:f + 1]),
                        waits=[tok_mm, evtmp_free[et], dm_tok], inc=ac)
                    ring_free[s_w1] = tok_silu

                    def evg(q, f=f, half=half, s_w2=s_w2, et=et):
                        q.tensor_scalar(t2tmp[:, :],
                                        slot_ap(s_w2), s2_sb[:, f:f + 1], None, ALU.mult)
                        return q.tensor_mul(
                            g1[:, TC * f + 512 * half:TC * f + 512 * (half + 1)],
                            evtmp[:, 512 * et:512 * (et + 1)],
                            t2tmp[:, :])
                    tok_g = P.op("vector", evg, waits=[tok_mm, tok_silu, dm_tok], inc=dv)
                    ring_free[s_w2] = tok_g
                    evtmp_free[et] = tok_g
                    evA_tail = tok_g
            stageA_mm_tail = pe.tok()

            def hidden_stage(src_g, dst_g, wname, ssb, prev_ev_tail):
                wconsh = [pe.tok(), pe.tok()]
                ev_tail = None
                for f in range(NH):
                    kb = f % 2
                    wlh = P.op(wq[kb], lambda q, f=f, kb=kb, wname=wname: q.dma_start(
                        ap3(wth[kb][:, :], 128),
                        t[wname][f:f + 1, :].rearrange("f (p j c) -> (f p) j c",
                                                       p=128, j=NH)),
                        waits=[wconsh[kb]] + ([t["pc_" + wname]] if f < 2 else []),
                        inc=dwh[kb], inc_k=16)
                    for half in range(2):
                        s = ring[0] % 8
                        ring[0] += 1
                        wmm = [wlh, ring_free[s], prev_ev_tail]
                        for j in range(NH):
                            P.op("tensor",
                                 lambda q, f=f, kb=kb, j=j, half=half, s=s, src_g=src_g: q.matmul(
                                     slot_ap(s), wth[kb][:, 128 * j:128 * (j + 1)],
                                     src_g[:, TC * j + 512 * half:TC * j + 512 * (half + 1)],
                                     start=(j == 0), stop=(j == NH - 1)),
                                 waits=(wmm if j == 0 else []),
                                 inc=(pe if j == NH - 1 else None))
                        tok_mm = pe.tok()
                        tok_ev = P.op("scalar",
                                      lambda q, f=f, half=half, s=s, ssb=ssb, dst_g=dst_g:
                                      q.activation(
                                          dst_g[:, TC * f + 512 * half:TC * f + 512 * (half + 1)],
                                          slot_ap(s), AF.Gelu, scale=ssb[:, f:f + 1]),
                                      waits=[tok_mm, dm_tok], inc=ac)
                        ring_free[s] = tok_ev
                        ev_tail = tok_ev
                    wconsh[kb] = pe.tok()
                return ev_tail

            evB_tail = hidden_stage(g1, g2, "h1b", sh1_sb, evA_tail)
            evC_tail = hidden_stage(g2, g1, "h2b", sh2_sb, evB_tail)

            for hp in range(2):
                wcons3 = [pe.tok(), pe.tok()]
                for jg in range(8):
                    kb = jg % 2
                    wl3 = P.op(wq[kb], lambda q, jg=jg, kb=kb: q.dma_start(
                        ap3(wt3[kb][:, :], E),
                        t["w3b"].rearrange("(a p) n -> p a n", p=128)[:, 4 * jg:4 * (jg + 1), :]),
                        waits=[wcons3[kb], t["pc_w3b"]], inc=dwh[kb], inc_k=16)
                    for j4 in range(4):
                        j = 4 * jg + j4
                        for ts_ in range(4):
                            wmm = []
                            if jg == 0 and j4 == 0:
                                wmm = [wl3, evC_tail, ring_free[2 * ts_],
                                       ring_free[2 * ts_ + 1]]
                            elif j4 == 0 and ts_ == 0:
                                wmm = [wl3]
                            for h in range(2):
                                P.op("tensor",
                                     lambda q, j=j, j4=j4, kb=kb, ts_=ts_, h=h, hp=hp: q.matmul(
                                         pp[ts_][:, 512 * h:512 * (h + 1)],
                                         g1[:, TC * j + 128 * (4 * hp + ts_):
                                            TC * j + 128 * (4 * hp + ts_ + 1)],
                                         wt3[kb][:, E * j4 + 512 * h:E * j4 + 512 * (h + 1)],
                                         start=(j == 0), stop=(j == NH - 1)),
                                     waits=wmm, inc=(pe if (j == NH - 1 and h == 1) else None))
                    wcons3[kb] = pe.tok()
                stageD_mm_tail = pe.tok()
                for ts_ in range(4):
                    i = (TC // 128) * c + 4 * hp + ts_
                    b = ts_ % 2
                    g = 2 * c + hp
                    kst = 0
                    sc = 8 * b
                    w = [acc_guard[b], lnb_apply[b]] if acc_guard[b] else []
                    tok_acc = P.op("sync", lambda q, i=i, b=b: q.dma_start(
                        acc_t[:, E * b:E * (b + 1)], t["acc_hbm"][128 * i:128 * (i + 1), :]),
                        waits=w, inc=daq[b], inc_k=16)

                    def evd(q, ts_=ts_, b=b):
                        q.tensor_mul(evtmp[:, :], pp[ts_][:, :], s3_rep[:, :])
                        return q.tensor_add(acc_t[:, E * b:E * (b + 1)],
                                            acc_t[:, E * b:E * (b + 1)],
                                            evtmp[:, :])
                    tok_ev = P.op("vector", evd, waits=[stageD_mm_tail, tok_acc, dm_tok], inc=dv)
                    tok_st = P.op("sync", lambda q, i=i, b=b: q.dma_start(
                        t["x1_hbm"][128 * i:128 * (i + 1), :], acc_t[:, E * b:E * (b + 1)]),
                        waits=[tok_ev], inc=dx1[b], inc_k=16)
                    acc_guard[b] = tok_st

                    P.op("vector", lambda q, b=b, sc=sc: P.sf(q, "vector", q.reduce_sum(
                        stat2[:, sc:sc + 1], acc_t[:, E * b:E * (b + 1)], AX.X)) and None,
                        waits=[tok_ev, lnb_apply[b]])
                    sq_tok = P.op("scalar", lambda q, b=b, sc=sc: q.activation(
                        xnf[:, E * b:E * (b + 1)], acc_t[:, E * b:E * (b + 1)], AF.Square,
                        accum_out=stat2[:, sc + 1:sc + 2]),
                        waits=[tok_ev, lnb_tr[b]], inc=ac)

                    def smallb(q, sc=sc):
                        P.sf(q, "vector", q.tensor_scalar_mul(stat2[:, sc + 2:sc + 3],
                                                              stat2[:, sc:sc + 1], 1.0 / E))
                        q.tensor_scalar_mul(stat2[:, sc + 3:sc + 4], stat2[:, sc + 2:sc + 3],
                                            -1.0)
                        P.sf(q, "vector", q.tensor_mul(stat2[:, sc + 4:sc + 5],
                                                       stat2[:, sc + 2:sc + 3],
                                                       stat2[:, sc + 2:sc + 3]))
                        P.sf(q, "vector", q.tensor_scalar_mul(stat2[:, sc + 5:sc + 6],
                                                              stat2[:, sc + 1:sc + 2], 1.0 / E))
                        return q.tensor_sub(stat2[:, sc + 6:sc + 7], stat2[:, sc + 5:sc + 6],
                                            stat2[:, sc + 4:sc + 5])
                    tok_var = P.op("vector", smallb, waits=[sq_tok], inc=dv)

                    def rstdb(q, sc=sc):
                        P.sf(q, "scalar", q.activation(stat2[:, sc + 7:sc + 8],
                                                       stat2[:, sc + 6:sc + 7],
                                                       AF.Ln, bias=t["epsc"][:, 0:1]))
                        return q.activation(stat2[:, sc + 7:sc + 8], stat2[:, sc + 7:sc + 8],
                                            AF.Exp, scale=-0.5)
                    tok_rstd = P.op("scalar", rstdb, waits=[tok_var], inc=ac)
                    lnb_apply[b] = P.op("vector", lambda q, b=b, sc=sc: q.tensor_scalar(
                        xnf[:, E * b:E * (b + 1)], acc_t[:, E * b:E * (b + 1)],
                        stat2[:, sc + 3:sc + 4], stat2[:, sc + 7:sc + 8], ALU.add, ALU.mult),
                        waits=[tok_rstd], inc=dv)
                    for k in range(NE):
                        P.op("tensor", lambda q, ts_=ts_, b=b, k=k: q.transpose(
                            pp[ts_][:, 128 * k:128 * (k + 1)],
                            xnf[:, E * b + 128 * k:E * b + 128 * (k + 1)],
                            t["identf"][:, :]),
                            waits=([lnb_apply[b]] if k == 0 else []),
                            inc=(pe if k == NE - 1 else None))
                    lnb_tr[b] = pe.tok()
                    ev_half = [None, None]
                    prev_v = None
                    for k in range(NE):
                        isv = (k % 2 == 0)

                        def evk(q, ts_=ts_, k=k, kst=kst, isv=isv):
                            dst = stg[kst][:, 512 * k + 128 * ts_:512 * k + 128 * (ts_ + 1)]
                            src = pp[ts_][:, 128 * k:128 * (k + 1)]
                            return q.tensor_copy(dst, src) if isv else q.copy(dst, src)
                        wv = [lnb_tr[b]]
                        if ts_ == 0 and k < 2:
                            wv.append(lnb_store[kst])
                        if not isv:
                            wv.append(prev_v)
                        tok = P.op("vector" if isv else "scalar", evk, waits=wv,
                                   inc=(dv if isv else ac))
                        if isv:
                            prev_v = tok
                        else:
                            ev_half[k // 4] = tok
                    ring_free[2 * ts_] = ev_half[0]
                    ring_free[2 * ts_ + 1] = ev_half[1]
                    if ts_ == 3:
                        lnb_store[kst] = P.op("sync", lambda q, g=g, kst=kst: q.dma_start(
                            t["xnt1_hbm"].rearrange("(j p) n -> p j n", p=128)
                            [:, :, 512 * g:512 * (g + 1)],
                            stg[kst][:, :].rearrange("p (j n) -> p j n", n=512)),
                            waits=[ev_half[0], ev_half[1]], inc=dxn, inc_k=16)
        P.flush("p3")



def _b1(nc, ctx, P, t):
    with ExitStack() as pctx:
        sb = lambda name, shape, dt=F32: pctx.enter_context(nc.sbuf_tensor(name, shape, dt))
        scale_gn = sb("scale_gn", [128, NE])
        bias_gn = sb("bias_gn", [128, NE])
        _b1_merged2(nc, P, t, scale_gn, bias_gn)
        _b1_wo(nc, P, t, scale_gn, bias_gn)


def _b1_merged2(nc, P, t, scale_gn, bias_gn):
    lctx = ExitStack()
    sb = lambda name, shape, dt=F32: lctx.enter_context(nc.sbuf_tensor(name, shape, dt))
    ps = lambda name, shape, dt=F32: lctx.enter_context(nc.psum_tensor(name, shape, dt))
    CH, NCH = 256, 16
    TCM, NCM = 512, 8
    CH1 = CH + 1

    pxp = [ps(f"pxpM{k}", [128, 512]) for k in range(4)]
    pr = [ps(f"prB{k}", [128, 256]) for k in range(3)]
    pmisc = ps("pmiscB", [128, 512])

    wrt_sb = sb("wrt_sb", [128, NE * E], BF16)
    wkt_sb = sb("wkt_sb", [128, NE * E], BF16)
    wvt_sb = sb("wvt_sb", [128, NE * E], BF16)
    small8 = sb("small8", [128, 96])
    ob2_sb = sb("ob2_sb", [128, 2])
    sel2_sb = sb("sel2_sb", [2, 128])
    ew_rep = sb("ew_rep", [128, NE * CH])
    xw = [sb(f"xwM{k}", [128, NE * CH1], BF16) for k in range(2)]
    Kbuf = sb("Kbuf", [128, NE * CH])
    vbuf = sb("vbufB", [128, NE * CH], BF16)
    sigr = sb("sigrB", [128, NE * CH], BF16)
    Bbuf = sb("Bbuf", [128, NE * CH1])
    Gbuf = sb("Gbuf", [128, NE * CH1])
    tmp1 = sb("tmp1B", [128, NE * CH])
    tmp2 = sb("tmp2B", [128, NE * CH])
    tmp3 = sb("tmp3B", [128, NE * CH])
    tmp4 = sb("tmp4B", [128, CH])
    tokd = sb("tokdB", [128, 1])
    xmixp = {p: sb(f"xmix_{p}", [128, NE * CH], BF16) for p in "rkv"}
    sums = sb("sumsB", [128, 32])
    redb = sb("redbB", [128, 16])
    gstat = sb("gstatB", [2, 32])

    mw = [sb(f"mwM{k}", [128, NE * TCM], BF16) for k in range(2)]
    ew_t = [sb(f"ewtM{k}", [128, NE * 512], BF16) for k in range(2)]
    scl = [sb(f"sclM{k}", [128, NE * 512], BF16) for k in range(3)]
    wball = [sb(f"wbM{k}", [128, TCM], BF16) for k in range(3)]
    mev = sb("mevM", [128, 4 * 512])
    gw_sb = sb("gwM", [128, NE * NEXP], BF16)
    gbr = sb("gbrM", [128, NEXP])
    eb_sb = sb("ebM", [NEXP, E], BF16)
    gg = sb("ggM", [128, 64])
    wn = sb("wnM", [128, NEXP], BF16)
    wnT = [sb(f"wnTM{k}", [NEXP, TCM], BF16) for k in range(2)]

    pe, ac, dv = (P.sem(n) for n in ["pe", "ac", "dv"])
    gp = P.sem("gpP")
    gw = P.sem("gmW")
    gmG = P.sem("gmG")
    dmb = P.sem("dmM")
    drw = P.sem("gmRw")
    dmw = [P.sem("dmMw0"), P.sem("dmMw1")]
    dwn = P.sem("dmWn")
    dwb = [P.sem("smWb0"), P.sem("smWb1"), P.sem("smWb2")]
    dmx = [P.sem("dmXw0"), P.sem("dmXw1")]
    dme = [P.sem(f"dmEw{k}") for k in range(2)]
    dmoe = P.sem("dmMoe")

    xnv = t["xnt1_hbm"].rearrange("(j p) n -> p j n", p=128)
    TD, TF, TMR, TMK, TMV, GM, GB = 0, 8, 16, 24, 32, 40, 48
    EW, EU, OMR, OMK, OMV = 56, 64, 72, 80, 88

    shared = {"moe_tail": []}

    def gen_scan():
        for nm, dst in [("wrt", wrt_sb), ("wkt", wkt_sb), ("wvt", wvt_sb)]:
            P.op("gpsimd", lambda q, nm=nm, dst=dst: q.dma_start(
                ap3(dst[:, :], E), t[nm].rearrange("(j p) c -> p j c", p=128)),
                inc=gw, inc_k=16)
        gw_tok = gw.tok()
        for i, nm in enumerate(["tdm", "tfm", "tmr", "tmk", "tmv", "gmm", "gbm"]):
            P.op("sync", lambda q, nm=nm, i=i: q.dma_start(
                small8[:, 8 * i:8 * (i + 1)], t[nm]), inc=dmb, inc_k=16)
        P.op("sync", lambda q: q.dma_start(ob2_sb[:, :], t["onesbd2"]), inc=dmb, inc_k=16)
        P.op("sync", lambda q: q.dma_start(sel2_sb[:, :], t["sel2"]), inc=dmb, inc_k=16)
        dmb_tok = dmb.tok()
        OM = {TMR: OMR, TMK: OMK, TMV: OMV}

        def prep(q):
            P.sf(q, "scalar", q.activation(small8[:, EW:EW + 8], small8[:, TD:TD + 8],
                                           AF.Exp))
            P.sf(q, "scalar", q.activation(small8[:, EW:EW + 8], small8[:, EW:EW + 8],
                                           AF.Exp, scale=-1.0))
            return q.activation(small8[:, EU:EU + 8], small8[:, TF:TF + 8], AF.Exp)
        tok_prep = P.op("scalar", prep, waits=[dmb_tok], inc=ac)

        def prep2(q):
            for src_c, dst_c in [(TMR, OMR), (TMK, OMK), (TMV, OMV)]:
                P.sf(q, "vector", q.tensor_scalar(small8[:, dst_c:dst_c + 8],
                                                  small8[:, src_c:src_c + 8],
                                                  -1.0, 1.0, ALU.mult, ALU.add))
            q.memset(Bbuf[:, :], 0.0)
            q.memset(Gbuf[:, :], 0.0)
            q.memset(sums[:, :], 0.0)
            q.memset(ap3(xw[0][:, :], CH1)[:, :, 0:1], 0.0)
            P.sf(q, "vector", q.memset(tmp1[:, 0:CH], 1.0))
            last = None
            for j in range(NE):
                last = q.tensor_scalar(ew_rep[:, CH * j:CH * (j + 1)], tmp1[:, 0:CH],
                                       small8[:, EW + j:EW + j + 1], None, ALU.mult)
            return last
        tok_prep2 = P.op("vector", prep2, waits=[dmb_tok, tok_prep], inc=dv)
        yield

        B3 = lambda off: Bbuf[:, :].rearrange("p (j x) -> p j x", x=CH1)[:, :, off:off + CH]
        G3 = lambda off: Gbuf[:, :].rearrange("p (j x) -> p j x", x=CH1)[:, :, off:off + CH]
        f3 = lambda buf: ap3(buf[:, :], CH)

        ring_free = [None] * 3
        ring = [0]
        mmtail_p = {}
        prev_store = None
        tok_out = None
        mix_last = [None, None]
        def emit_xw(c):
            c0 = CH * c
            w = c % 2
            if c == 0:
                xw_tok = P.op("sync", lambda q: q.dma_start(
                    ap3(xw[0][:, :], CH1)[:, :, 1:CH1], xnv[:, :, 0:CH]),
                    waits=[], inc=dmx[0], inc_k=16)
            else:
                xw_tok = P.op("sync", lambda q, c0=c0, w=w: q.dma_start(
                    ap3(xw[w][:, :], CH1), xnv[:, :, c0 - 1:c0 + CH]),
                    waits=[mix_last[w]], inc=dmx[w], inc_k=16)
            return xw_tok, w

        def emit_mix(proj, xw_tok, w):
            tm_c = {"r": TMR, "k": TMK, "v": TMV}[proj]
            om_c = OM[tm_c]
            xmix = xmixp[proj]

            def mixf(q, w=w, tm_c=tm_c, om_c=om_c, xmix=xmix):
                last = None
                for j in range(NE):
                    cur = xw[w][:, CH1 * j + 1:CH1 * (j + 1)]
                    prv = xw[w][:, CH1 * j:CH1 * j + CH]
                    q.tensor_scalar(tmp4[:, :], cur,
                                    small8[:, tm_c + j:tm_c + j + 1], None, ALU.mult)
                    last = q.scalar_tensor_tensor(
                        xmix[:, CH * j:CH * (j + 1)], prv,
                        small8[:, om_c + j:om_c + j + 1],
                        tmp4[:, :], ALU.mult, ALU.add)
                return last
            tok = P.op("vector", mixf,
                       waits=[tok_prep2, xw_tok, mmtail_p.get(proj)], inc=dv)
            if proj == "v":
                mix_last[w] = tok
            return tok

        nxt = emit_xw(0)
        mixes = {p: emit_mix(p, *nxt) for p in "rkv"}
        for c in range(NCH):
            c0 = CH * c
            proj_ev = {}
            for proj, wsb, tm_c in [("r", wrt_sb, TMR), ("k", wkt_sb, TMK),
                                    ("v", wvt_sb, TMV)]:
                xmix = xmixp[proj]
                tok_mix = mixes[proj]
                for fj in range(NE):
                    s = ring[0] % 3
                    ring[0] += 1
                    wmm = [tok_mix, gw_tok, ring_free[s]]
                    for j in range(NE):
                        P.op("tensor", lambda q, fj=fj, j=j, wsb=wsb, s=s, xmix=xmix: q.matmul(
                            pr[s][:, :], wsb[:, E * j + 128 * fj:E * j + 128 * (fj + 1)],
                            xmix[:, CH * j:CH * (j + 1)], start=(j == 0),
                            stop=(j == NE - 1)),
                            waits=(wmm if j == 0 else []), inc=(pe if j == NE - 1 else None))
                    tok_mm = pe.tok()
                    if proj == "r":
                        ev = P.op("scalar", lambda q, fj=fj, s=s: q.activation(
                            sigr[:, CH * fj:CH * (fj + 1)], pr[s][:, :], AF.Sigmoid),
                            waits=[tok_mm] + ([prev_store] if prev_store else []), inc=ac)
                    elif proj == "k":
                        ev = P.op("scalar", lambda q, fj=fj, s=s: q.activation(
                            Kbuf[:, CH * fj:CH * (fj + 1)], pr[s][:, :], AF.Exp),
                            waits=[tok_mm] + ([prev_store] if prev_store else []), inc=ac)
                    else:
                        ev = P.op("scalar", lambda q, fj=fj, s=s: q.copy(
                            vbuf[:, CH * fj:CH * (fj + 1)], pr[s][:, :]),
                            waits=[tok_mm], inc=ac)
                    ring_free[s] = ev
                mmtail_p[proj] = pe.tok()
                proj_ev[proj] = ev
                yield

            nxt = emit_xw(c + 1) if c + 1 < NCH else None
            nmix = {}

            def bscan(q):
                last = None
                for j in range(NE):
                    last = q.tensor_tensor_scan(
                        Bbuf[:, CH1 * j + 1:CH1 * (j + 1)],
                        ew_rep[:, CH * j:CH * (j + 1)], Kbuf[:, CH * j:CH * (j + 1)],
                        Bbuf[:, CH1 * j:CH1 * j + 1], ALU.mult, ALU.add)
                return last
            tok_b = P.op("vector", bscan, waits=[proj_ev["k"], tok_prep2], inc=dv)

            def mxf(q):
                last = None
                for j in range(NE):
                    last = q.scalar_tensor_tensor(
                        tmp1[:, CH * j:CH * (j + 1)],
                        Bbuf[:, CH1 * j:CH1 * j + CH],
                        small8[:, EW + j:EW + j + 1],
                        Kbuf[:, CH * j:CH * (j + 1)], ALU.mult, ALU.max)
                return last
            tok_mx = P.op("vector", mxf, waits=[tok_b], inc=dv)
            if nxt:
                nmix["r"] = emit_mix("r", *nxt)

            def rmxf(q):
                P.sf(q, "scalar", q.activation(tmp2[:, :], tmp1[:, :], AF.Ln))
                return q.activation(tmp1[:, :], tmp2[:, :], AF.Exp, scale=-1.0)
            tok_rmx = P.op("scalar", rmxf, waits=[tok_mx], inc=ac)
            tok_rho = P.op("vector", lambda q: q.tensor_mul(f3(tmp2), B3(1), f3(tmp1)),
                           waits=[tok_rmx] + ([prev_store] if prev_store else []), inc=dv)

            def cdf(q):
                for j in range(NE):
                    q.tensor_scalar(tmp1[:, CH * j:CH * (j + 1)],
                                    tmp2[:, CH * j:CH * (j + 1)],
                                    small8[:, EW + j:EW + j + 1], None, ALU.mult)
                q.tensor_mul(tmp3[:, :], Kbuf[:, :], vbuf[:, :])
                return q.tensor_mul(tmp3[:, :], tmp2[:, :], tmp3[:, :])
            tok_cd = P.op("vector", cdf, waits=[tok_rho, proj_ev["v"]], inc=dv)

            def gscan(q):
                last = None
                for j in range(NE):
                    last = q.tensor_tensor_scan(
                        Gbuf[:, CH1 * j + 1:CH1 * (j + 1)],
                        tmp1[:, CH * j:CH * (j + 1)], tmp3[:, CH * j:CH * (j + 1)],
                        Gbuf[:, CH1 * j:CH1 * j + 1], ALU.mult, ALU.add)
                return last
            tok_g = P.op("vector", gscan, waits=[tok_cd], inc=dv)

            def kuf(q):
                for j in range(NE):
                    q.tensor_scalar(tmp2[:, CH * j:CH * (j + 1)],
                                    Kbuf[:, CH * j:CH * (j + 1)],
                                    small8[:, EU + j:EU + j + 1], None, ALU.mult)
                return q.tensor_add(f3(tmp3), B3(0), f3(tmp2))
            tok_kua = P.op("vector", kuf, waits=[tok_g], inc=dv)
            if nxt:
                nmix["k"] = emit_mix("k", *nxt)

            def rdenf(q):
                P.sf(q, "scalar", q.activation(tmp1[:, :], tmp3[:, :], AF.Ln))
                return q.activation(tmp3[:, :], tmp1[:, :], AF.Exp, scale=-1.0)
            tok_rden = P.op("scalar", rdenf, waits=[tok_kua], inc=ac)

            def outf(q):
                q.tensor_mul(tmp1[:, :], tmp2[:, :], vbuf[:, :])
                q.tensor_add(f3(tmp2), G3(0), f3(tmp1))
                q.tensor_mul(tmp1[:, :], tmp2[:, :], tmp3[:, :])
                q.tensor_mul(tmp2[:, :], tmp1[:, :], sigr[:, :])
                P.sf(q, "vector", q.tensor_copy(
                    Bbuf[:, :].rearrange("p (j x) -> p j x", x=CH1)[:, :, 0:1],
                    Bbuf[:, :].rearrange("p (j x) -> p j x", x=CH1)[:, :, CH:CH1]))
                P.sf(q, "vector", q.tensor_copy(
                    Gbuf[:, :].rearrange("p (j x) -> p j x", x=CH1)[:, :, 0:1],
                    Gbuf[:, :].rearrange("p (j x) -> p j x", x=CH1)[:, :, CH:CH1]))
                return q.memset(tokd[:, :], 0.0)
            wo = [tok_rden, proj_ev["r"]] + ([prev_store] if prev_store else [])
            tok_out = P.op("vector", outf, waits=wo, inc=dv)
            if nxt:
                nmix["v"] = emit_mix("v", *nxt)
                mixes = nmix

            tok_sq = P.op("scalar", lambda q: q.activation(tmp1[:, :], tmp2[:, :],
                                                           AF.Square),
                          waits=[tok_out], inc=ac)

            def statf(q, c=c):
                po, pn = 16 * (c % 2), 16 * ((c + 1) % 2)
                for j in range(NE):
                    P.sf(q, "vector", q.reduce_sum(redb[:, j:j + 1],
                                                   tmp2[:, CH * j:CH * (j + 1)], AX.X))
                    P.sf(q, "vector", q.reduce_sum(redb[:, NE + j:NE + j + 1],
                                                   tmp1[:, CH * j:CH * (j + 1)], AX.X))
                return q.tensor_add(sums[:, pn:pn + 16], sums[:, po:po + 16], redb[:, :])
            tok_stat = P.op("vector", statf, waits=[tok_sq], inc=dv)
            prev_store = P.op("gpsimd", lambda q, c0=c0: q.dma_start(
                t["rwkv_hbm"].rearrange("(j p) n -> p j n", p=128)[:, :, c0:c0 + CH],
                ap3(tmp2[:, :], CH)), waits=[tok_out], inc=drw, inc_k=16)
            t["rwkv_store_tok"] = prev_store
            if "dbg_wkv" in t["dbg"]:
                prev_store = P.op("sync", lambda q, c0=c0: q.dma_start(
                    t["dbg"]["dbg_wkv"].rearrange("(j p) n -> p j n", p=128)[:, :,
                                                                            c0:c0 + CH],
                    ap3(tmp2[:, :], CH)), waits=[tok_stat], inc=drw, inc_k=16)
            yield

        shared["tok_stat"] = tok_stat
        shared["dmb_tok"] = dmb_tok

    def gen_moe():
        P.op("gpsimd", lambda q: q.dma_start(
            ap3(gw_sb[:, :], NEXP), t["gw"].rearrange("(j p) c -> p j c", p=128)),
            inc=gmG, inc_k=16)
        P.op("gpsimd", lambda q: q.dma_start(eb_sb[:, :], t["eb"]), inc=gmG, inc_k=16)
        tok_tbl = gmG.tok()
        P.op("sync", lambda q: q.dma_start(
            gbr[:, :], t["gb"][0:1, :].partition_broadcast(128)), inc=dmb, inc_k=16)
        tok_gb = dmb.tok()

        widx = 0
        scl_toks = {}
        wl_toks = {}
        wb_toks = {}
        emm_tails = {}
        mw_guard = [None, None]
        wnT_guard = [None, None]
        wnT_done = [None, None]
        gst = [None, None]
        mevs = [mev]
        st_mev = [None]
        prev_ev = {}
        tok_bias = None
        gstate = {"g1": None, "g2": None, "g4": None, "wn_ev": None}

        def emit_gates_load(c):
            par = c % 2
            c0 = TCM * c
            mw_tok = P.op("sync", lambda q, c0=c0, par=par: q.dma_start(
                ap3(mw[par][:, :], TCM), xnv[:, :, c0:c0 + TCM]),
                waits=(mw_guard[par] or []), inc=dmw[par], inc_k=16)
            return mw_tok

        def emit_gates_i4(c, mw_tok, i4):
            par = c % 2
            c0 = TCM * c
            if True:
                wmm = [mw_tok, tok_tbl]
                if gstate_first[0]:
                    gstate_first[0] = False
                else:
                    wmm += [gstate["g1"], gstate["wn_ev"]]
                for j in range(NE):
                    P.op("tensor", lambda q, i4=i4, j=j, par=par: q.matmul(
                        pmisc[:, 0:NEXP],
                        mw[par][:, TCM * j + 128 * i4:TCM * j + 128 * (i4 + 1)],
                        gw_sb[:, NEXP * j:NEXP * (j + 1)],
                        start=(j == 0), stop=(j == NE - 1)),
                        waits=(wmm if j == 0 else []), inc=(pe if j == NE - 1 else None))
                tok_gmm = pe.tok()

                def gatef(q):
                    P.sf(q, "vector", q.tensor_add(gg[:, 0:8], pmisc[:, 0:NEXP],
                                                   gbr[:, :]))
                    P.sf(q, "vector", q.reduce_max(gg[:, 8:9], gg[:, 0:8], AX.X))
                    return q.tensor_scalar_mul(gg[:, 9:10], gg[:, 8:9], -1.0)
                gstate["g1"] = P.op("vector", gatef,
                                    waits=[tok_gmm, tok_gb] +
                                    ([gstate["g2"]] if gstate["g2"] else []), inc=dv)
                gstate["g2"] = P.op("scalar", lambda q: q.activation(
                    gg[:, 16:24], gg[:, 0:8], AF.Exp, bias=gg[:, 9:10],
                    accum_out=gg[:, 10:11]), waits=[gstate["g1"]], inc=ac)

                def gatef2(q):
                    P.sf(q, "vector", q.reciprocal(gg[:, 11:12], gg[:, 10:11]))
                    P.sf(q, "vector", q.tensor_scalar(gg[:, 24:32], gg[:, 16:24],
                                                      gg[:, 11:12], None, ALU.mult))
                    P.sf(q, "vector", q.tensor_scalar(gg[:, 32:40], gg[:, 24:32],
                                                      0.1, None, ALU.is_ge))
                    P.sf(q, "vector", q.tensor_mul(gg[:, 40:48], gg[:, 24:32],
                                                   gg[:, 32:40]))
                    P.sf(q, "vector", q.reduce_sum(gg[:, 12:13], gg[:, 40:48], AX.X))
                    P.sf(q, "vector", q.tensor_scalar(gg[:, 13:14], gg[:, 12:13],
                                                      0.0, None, ALU.is_equal))
                    P.sf(q, "vector", q.tensor_add(gg[:, 14:15], gg[:, 12:13],
                                                   gg[:, 13:14]))
                    P.sf(q, "vector", q.reciprocal(gg[:, 15:16], gg[:, 14:15]))
                    return q.tensor_scalar(wn[:, :], gg[:, 40:48], gg[:, 15:16],
                                           None, ALU.mult)
                gstate["g3"] = P.op("vector", gatef2,
                              waits=[gstate["g2"]] +
                              ([gstate["g4"]] if gstate["g4"] else []), inc=dv)

        def emit_gates_b(c, i4):
            par = c % 2
            c0 = TCM * c
            if True:
                tok_g3 = gstate["g3"]
                gstate["g4"] = P.op("tensor", lambda q: q.matmul(
                    pmisc[0:8, 128:256], wn[:, :], t["ident"][:, :],
                    start=True, stop=True),
                    waits=[tok_g3] + ([gstate["wn_ev"]] if gstate["wn_ev"] else []),
                    inc=pe)
                wev_w = [gstate["g4"]]
                if i4 == 0 and wnT_guard[par] is not None:
                    wev_w.append(wnT_guard[par])
                gstate["wn_ev"] = P.op("scalar", lambda q, i4=i4, par=par: q.copy(
                    wnT[par][:, 128 * i4:128 * (i4 + 1)], pmisc[0:8, 128:256]),
                    waits=wev_w, inc=ac)
            if i4 == 3:
                wnT_done[par] = gstate["wn_ev"]
                gst[par] = P.op("sync", lambda q, c0=c0, par=par: q.dma_start(
                    t["wn_hbm"][:, c0:c0 + TCM], wnT[par][:, :]),
                    waits=[gstate["wn_ev"]], inc=dwn, inc_k=16)

        gstate_first = [True]
        mwg = emit_gates_load(0)
        for _i4 in range(4):
            emit_gates_i4(0, mwg, _i4)
            emit_gates_b(0, _i4)
        yield

        for c in range(NCM):
            c0 = TCM * c
            par = c % 2
            quanta = [(rF, e) for rF in range(2) for e in range(NEXP)]

            def prepq(qi, c0=c0, par=par, quanta=quanta, widx=None):
                rF, e = quanta[qi]
                g = wbase + qi
                s = g % 2
                r = g % 3
                wl_toks[g] = P.op("sync", lambda q, e=e, rF=rF, s=s: q.dma_start(
                    ap3(ew_t[s][:, :], 512),
                    t["ewq"][2 * e + rF:2 * e + rF + 1, :].rearrange(
                        "o (p j c) -> (o p) j c", p=128, j=NE)),
                    waits=[emm_tails.get(g - 2)] + ([t["pc_ewt"]] if g < 2 else []),
                    inc=dme[s], inc_k=16)
                pw = scl_toks.get(g - 3)
                wb_toks[g] = P.op("scalar", lambda q, e=e, r=r, c0=c0: q.dma_start(
                    wball[r][:, :],
                    t["wn_hbm"][e:e + 1, c0:c0 + TCM].partition_broadcast(128)),
                    waits=[gst[par]] + (pw if pw else []), inc=dwb[r], inc_k=16)

                def sclh(q, r=r, par=par, js=None):
                    last = None
                    for j in js:
                        last = q.tensor_mul(scl[r][:, 512 * j:512 * (j + 1)],
                                            mw[par][:, TCM * j:TCM * (j + 1)],
                                            wball[r][:, :])
                    return last
                wsc = [wb_toks[g], emm_tails.get(g - 3)]
                ta = P.op("gpsimd", lambda q, r=r, par=par: sclh(q, r, par, (0, 1, 2, 3)),
                          waits=wsc, inc=gp)
                tb = P.op("gpsimd", lambda q, r=r, par=par: sclh(q, r, par, (4, 5, 6, 7)),
                          waits=[], inc=gp)
                scl_toks[g] = [ta, tb]

            wbase = widx
            prepq(0)
            prepq(1)
            for qi, (rF, e) in enumerate(quanta):
                g = wbase + qi
                s = g % 2
                r = g % 3
                for Fp in range(4):
                    wmm = [wl_toks[g]] + scl_toks[g]
                    if e == 0:
                        wmm.append(prev_ev.get(Fp))
                    for j in range(NE):
                        P.op("tensor", lambda q, Fp=Fp, j=j, e=e, s=s, r=r: q.matmul(
                            pxp[Fp][:, :],
                            ew_t[s][:, 512 * j + 128 * Fp:512 * j + 128 * (Fp + 1)],
                            scl[r][:, 512 * j:512 * (j + 1)],
                            start=(e == 0 and j == 0), stop=False),
                            waits=(wmm if j == 0 else []),
                            inc=(pe if j == NE - 1 else None))
                emm_tails[g] = pe.tok()
                if qi + 2 < len(quanta):
                    prepq(qi + 2)
                yield
                if c + 1 < NCM and qi in (1, 4, 7, 10):
                    if qi == 1:
                        mwg = emit_gates_load(c + 1)
                    emit_gates_i4(c + 1, mwg, (qi - 1) // 3)
                    yield
                if c + 1 < NCM and qi in (3, 6, 9, 12):
                    emit_gates_b(c + 1, (qi - 3) // 3)
                if e < NEXP - 1:
                    continue
                for Fp in range(4):
                    P.op("tensor", lambda q, Fp=Fp, rF=rF, par=par: q.matmul(
                        pxp[Fp][:, :], eb_sb[:, 128 * (4 * rF + Fp):128 * (4 * rF + Fp + 1)],
                        wnT[par][:, :], start=False, stop=True),
                        waits=([tok_tbl, wnT_done[par]] if Fp == 0 else []),
                        inc=(pe if Fp == 3 else None))
                tok_bias = pe.tok()
                gpar = 0
                mv = mevs[gpar]
                for Fp in range(4):
                    def evm(q, Fp=Fp, mv=mv):
                        return q.copy(mv[:, 512 * Fp:512 * (Fp + 1)], pxp[Fp][:, :])
                    prev_ev[Fp] = P.op("scalar", evm,
                                       waits=[tok_bias, st_mev[gpar]], inc=ac)
                st_mev[gpar] = P.op("sync", lambda q, c0=c0, rF=rF, mv=mv: q.dma_start(
                    t["moe_hbm"].rearrange("(f p) n -> p f n", p=128)
                    [:, 4 * rF:4 * (rF + 1), c0:c0 + TCM],
                    mv[:, :].rearrange("p (f n) -> p f n", n=512)),
                    waits=[prev_ev[2], prev_ev[3]], inc=dmoe, inc_k=16)
                yield
            widx += len(quanta)
            wnT_guard[par] = tok_bias
            mw_guard[par] = scl_toks[widx - 1]
        shared["moe_tail"] = [st_mev[0],
                              gstate["g1"], gstate["wn_ev"]]

    gs = gen_scan()
    gm = gen_moe()
    next(gs)
    next(gm)
    gs_alive = gm_alive = True
    phase = 0
    while gs_alive or gm_alive:
        if gs_alive:
            try:
                next(gs)
            except StopIteration:
                gs_alive = False
        nmoe = 3 if (phase % 2 == 0) else 2
        for _ in range(nmoe):
            if gm_alive:
                try:
                    next(gm)
                except StopIteration:
                    gm_alive = False
        phase += 1

    fin = 16 * (NCH % 2)

    def statmm(q, fin=fin):
        return q.matmul(pmisc[0:2, 0:16], ob2_sb[:, :], sums[:, fin:fin + 16],
                        start=True, stop=True)
    tok_smm = P.op("tensor", statmm,
                   waits=[shared["tok_stat"], shared["dmb_tok"]] + shared["moe_tail"], inc=pe)
    NTOT = float(64 * T)

    def gnf(q):
        P.sf(q, "vector", q.tensor_copy(gstat[:, 0:16], pmisc[0:2, 0:16]))
        P.sf(q, "vector", q.tensor_scalar_mul(gstat[:, 16:24], gstat[:, 0:8],
                                              1.0 / NTOT))
        P.sf(q, "vector", q.tensor_scalar_mul(gstat[:, 24:32], gstat[:, 8:16],
                                              1.0 / NTOT))
        P.sf(q, "vector", q.tensor_mul(gstat[:, 0:8], gstat[:, 16:24],
                                       gstat[:, 16:24]))
        return q.tensor_sub(gstat[:, 8:16], gstat[:, 24:32], gstat[:, 0:8])
    tok_gn = P.op("vector", gnf, waits=[tok_smm], inc=dv)

    def gnf2(q):
        P.sf(q, "scalar", q.activation(gstat[:, 24:32], gstat[:, 8:16], AF.Ln,
                                       bias=t["epsc"][0:2, 0:1]))
        return q.activation(gstat[:, 24:32], gstat[:, 24:32], AF.Exp, scale=-0.5)
    tok_gn2 = P.op("scalar", gnf2, waits=[tok_gn], inc=ac)

    def bcmm(q):
        q.matmul(pmisc[:, 16:24], sel2_sb[:, :], gstat[:, 16:24],
                 start=True, stop=True)
        return q.matmul(pmisc[:, 24:32], sel2_sb[:, :], gstat[:, 24:32],
                        start=True, stop=True)
    tok_bcm = P.op("tensor", bcmm, waits=[tok_gn2], inc=pe)

    def sbf(q):
        P.sf(q, "vector", q.tensor_copy(redb[:, 0:8], pmisc[:, 16:24]))
        P.sf(q, "vector", q.tensor_copy(redb[:, 8:16], pmisc[:, 24:32]))
        P.sf(q, "vector", q.tensor_mul(scale_gn[:, :], redb[:, 8:16],
                                       small8[:, GM:GM + 8]))
        P.sf(q, "vector", q.tensor_mul(redb[:, 0:8], redb[:, 0:8], scale_gn[:, :]))
        return q.tensor_sub(bias_gn[:, :], small8[:, GB:GB + 8], redb[:, 0:8])
    P.op("vector", sbf, waits=[tok_bcm], inc=dv)


    P.flush("p45")
    lctx.close()


def _b1_wo(nc, P, t, scale_gn, bias_gn):
    lctx = ExitStack()
    sb = lambda name, shape, dt=F32: lctx.enter_context(nc.sbuf_tensor(name, shape, dt))
    ps = lambda name, shape, dt=F32: lctx.enter_context(nc.psum_tensor(name, shape, dt))
    wot_sb = sb("wotW", [128, NE * E], BF16)
    rwt = sb("rwtW", [128, NE * 512], BF16)
    nrm_b = sb("nrmW", [128, NE * 512], BF16)
    wo32 = sb("wo32W", [128, 512])
    moe_t = [sb(f"moeW{k}", [128, 512]) for k in range(2)]
    x1t = sb("x1tW", [128, 4 * E])
    aA = sb("aAW", [128, 4 * E])
    pwo = [ps(f"pwoW{k}", [128, 512]) for k in range(4)]

    pe, ac, dv, dv2 = (P.sem(n) for n in ["pe", "ac", "dv", "dv2"])
    gmw = P.sem("gmW")
    drl = P.sem("dmRl")
    dx1 = P.sem("dmXr0")
    dot = P.sem("dmAc0")

    P.op("gpsimd", lambda q: q.dma_start(
        ap3(wot_sb[:, :], E), t["wot"].rearrange("(j p) c -> p j c", p=128)),
        inc=gmw, inc_k=16)
    tok_w = gmw.tok()

    x1_guard = None
    o_guard = None
    rl_guard = None
    nrm_guard = None
    for n in range(8):
        c0 = 512 * n
        w = ([rl_guard] if rl_guard else []) + \
            ([t["rwkv_store_tok"]] if n == 0 else [])
        tok_rl = P.op("sync", lambda q, c0=c0: q.dma_start(
            ap3(rwt[:, :], 512),
            t["rwkv_hbm"].rearrange("(j p) n -> p j n", p=128)[:, :, c0:c0 + 512]),
            waits=w, inc=drl, inc_k=16)

        def nrmf(q):
            last = None
            for j in range(NE):
                last = q.tensor_scalar(nrm_b[:, 512 * j:512 * (j + 1)],
                                       rwt[:, 512 * j:512 * (j + 1)],
                                       scale_gn[:, j:j + 1], bias_gn[:, j:j + 1],
                                       ALU.mult, ALU.add)
            return last
        tok_nrm = P.op("vector", nrmf, waits=[tok_rl] + ([nrm_guard] if nrm_guard else []),
                       inc=dv)
        rl_guard = tok_nrm

        w = [x1_guard] if x1_guard else []
        tok_a1 = P.op("sync", lambda q, n=n: q.dma_start(
            x1t[:, :].rearrange("p (s e) -> p s e", e=E),
            t["x1_hbm"][512 * n:512 * (n + 1), :].rearrange("(s p) e -> p s e", p=128)),
            waits=w, inc=dx1, inc_k=16)

        comb_tok = {}
        ev_tok = {}
        tr_tok = {}
        tok_mo = {}
        for F in range(NE):
            kbank = F % 2
            tok_mo[F] = P.op("sync", lambda q, n=n, F=F: q.dma_start(
                moe_t[F % 2][:, :],
                t["moe_hbm"][128 * F:128 * (F + 1), 512 * n:512 * (n + 1)]),
                waits=[ev_tok.get(F - 2)], inc=dx1, inc_k=16)
            wmm = [tok_nrm, tok_w]
            if F >= 1:
                wmm.append(ev_tok[F - 1])
            if F >= 2:
                wmm.append(ev_tok[F - 2])
            for j in range(NE):
                P.op("tensor", lambda q, F=F, j=j, kbank=kbank: q.matmul(
                    pwo[kbank][:, :], wot_sb[:, E * j + 128 * F:E * j + 128 * (F + 1)],
                    nrm_b[:, 512 * j:512 * (j + 1)], start=(j == 0), stop=(j == NE - 1)),
                    waits=(wmm if j == 0 else []), inc=(pe if j == NE - 1 else None))
            tok_mm = pe.tok()
            w = [tok_mm, tok_mo[F]] + ([tr_tok[F - 1]] if F >= 1 else [])
            ev_tok[F] = P.op("vector", lambda q, F=F, kbank=kbank: q.tensor_add(
                wo32[:, :], pwo[kbank][:, :], moe_t[F % 2][:, :]), waits=w, inc=dv)

            def trf(q, F=F):
                last = None
                for s in range(4):
                    last = q.transpose(pwo[2 + (F % 2)][:, 128 * s:128 * (s + 1)],
                                      wo32[:, 128 * s:128 * (s + 1)], t["identf"][:, :])
                return last
            w = [ev_tok[F]] + ([comb_tok[F - 2]] if F >= 2 else [])
            tr_tok[F] = P.op("tensor", trf, waits=w, inc=pe)

            def combF(q, F=F):
                last = None
                for s in range(4):
                    last = q.tensor_add(aA[:, E * s + 128 * F:E * s + 128 * (F + 1)],
                                        x1t[:, E * s + 128 * F:E * s + 128 * (F + 1)],
                                        pwo[2 + (F % 2)][:, 128 * s:128 * (s + 1)])
                return last
            w = [tr_tok[F], tok_a1] + ([o_guard] if (F == 0 and o_guard) else [])
            comb_tok[F] = P.op("vector", combF, waits=w, inc=dv)
        x1_guard = comb_tok[NE - 1]
        o_guard = P.op("sync", lambda q, n=n: q.dma_start(
            t["out"][512 * n:512 * (n + 1), :].rearrange("(s p) e -> p s e", p=128),
            aA[:, :].rearrange("p (s e) -> p s e", e=E)),
            waits=[comb_tok[NE - 1]], inc=dot, inc_k=16)
        nrm_guard = tok_mm
    P.flush("p6")
    lctx.close()



def kernel(**inputs):
    prep = _host_prep(inputs)
    x = np.ascontiguousarray(np.asarray(inputs["L0_attn_x"], np.float32))
    nc = build()
    in_maps = []
    for c in range(NCORE):
        m = dict(prep)
        m["x"] = x[c]
        in_maps.append(m)
    res = run_bass_kernel_spmd(nc, in_maps, core_ids=list(range(NCORE)))
    out = np.stack([np.asarray(res.results[c]["out"], np.float32) for c in range(NCORE)], axis=0)
    return out
